# version 05
# baseline (speedup 1.0000x reference)
import contextlib
import numpy as np
import concourse.bass as bass
import concourse.mybir as mybir

F32 = mybir.dt.float32
BF16 = mybir.dt.bfloat16
ALU = mybir.AluOpType
AF = mybir.ActivationFunctionType

DM = 2048
DH = 2048
TOWN = 1024
L = 2048
S = 16
NCH = 4
LC = 512
BLK = 256
KT = 16
DTl = 16
HT = 64
NE = 8
HPE = 8
EPS = 1e-5
PAIRS = [[0, 1], [2, 3], [4, 5], [6, 7]]


def build(stage="full", taps=()):
    nc = bass.Bass("TRN2", target_bir_lowering=False, debug=False, detect_race_conditions=False)
    ctx = contextlib.ExitStack()

    dr = {}
    def din(name, shape):
        dr[name] = nc.dram_tensor(name, list(shape), F32, kind="ExternalInput").ap()
    din("x_own", (DM, TOWN))
    din("w_inx", (128, DTl * DM))
    din("w_ing", (128, DTl * DM))
    din("w_out", (128, KT * DH))
    din("w_xp", (128, KT * 160))
    din("w_dtp", (128, DH))
    din("w_g", (128, HT * DM))
    din("w_u", (128, HT * DM))
    din("w_d", (128, NE * KT * HPE * 128))
    din("A_sb", (128, DTl * S))
    din("cw_sb", (128, DTl * 4))
    din("cb_sb", (128, DTl))
    din("D_sb", (128, DTl))
    din("dtb_sb", (128, DTl))
    din("alpha", (128, 2))
    out = nc.dram_tensor("out", [DM, TOWN], F32, kind="ExternalOutput").ap()
    tap_out = {}
    for tname, tshape in taps:
        tap_out[tname] = nc.dram_tensor(tname, list(tshape), F32, kind="ExternalOutput").ap()

    def scr(name, shape, dt=F32):
        return nc.dram_tensor(name, list(shape), dt, kind="Internal").ap()
    g_bf = scr("g_bf", (128, HT * DM), BF16)
    u_bf = scr("u_bf", (128, HT * DM), BF16)
    d_bf = scr("d_bf", (128, NE * KT * HPE * 128), BF16)
    inx_bf = scr("inx_bf", (128, DTl * DM), BF16)
    ing_bf = scr("ing_bf", (128, DTl * DM), BF16)
    outp_bf = scr("outp_bf", (128, KT * DH), BF16)
    xres_h = scr("xres_h", (DM, TOWN))
    xt_hbm = scr("xt_hbm", (DTl, 128, L))
    ag_in = [scr(f"ag_in{h}", (1, DM, LC), BF16) for h in range(2)]
    ag_out = [scr(f"ag_out{h}", (2, DM, LC), BF16) for h in range(2)]
    ts_in = [scr(f"ts_in{c}", (160, LC)) for c in range(NCH)]
    ts_out = [scr(f"ts_out{c}", (160, LC)) for c in range(NCH)]
    op_in = [scr(f"op_in{c}", (DM, LC)) for c in range(NCH)]
    op_out = [scr(f"op_out{c}", (DM, LC)) for c in range(NCH)]
    state_snap = scr("state_snap", (2, 128, 2 * DTl * S))

    sb = {}
    def sbuf(name, shape, dt=F32):
        sb[name] = ctx.enter_context(nc.sbuf_tensor(name, list(shape), dt))
        return sb[name]

    ones = sbuf("ones", (128, 2))
    epsc = sbuf("epsc", (128, 1))
    onesr = sbuf("onesr", (1, 128))
    rows = sbuf("rows", (1, 4 * LC))
    A_s = sbuf("A_s", (128, DTl * S))
    cw_s = sbuf("cw_s", (128, DTl * 4))
    cb_s = sbuf("cb_s", (128, DTl))
    D_s = sbuf("D_s", (128, DTl))
    dtb_s = sbuf("dtb_s", (128, DTl))
    al_s = sbuf("al_s", (128, 2))
    xpT = sbuf("xpT", (128, KT * 160), BF16)
    dtpT = sbuf("dtpT", (128, DH), BF16)
    state = sbuf("state", (128, 2 * DTl * S))
    carry = sbuf("carry", (128, DTl * 4))
    wstrip = [sbuf(f"wstrip{i}", (128, KT * 128), BF16) for i in range(2)]
    wg_st = [sbuf(f"wg_st{i}", (128, 2 * KT * 128), BF16) for i in range(2)]
    wu_st = [sbuf(f"wu_st{i}", (128, 2 * KT * 128), BF16) for i in range(2)]
    wd_st = [sbuf(f"wd_st{i}", (128, 2 * HPE * 128), BF16) for i in range(2)]
    arena = sbuf("arena", (128, 12288))
    acc_f = arena[:, 0:8192]
    gated = arena[:, 8192:10240].bitcast(BF16)
    xh2 = arena[:, 0:4096].bitcast(BF16)
    xt_bf = arena[:, 4096:8192].bitcast(BF16)
    yf_t = arena[:, 8192:12288].bitcast(BF16)
    cbuf = [arena[:, j * 2048:(j + 1) * 2048].bitcast(BF16) for j in range(4)]
    xh_f = sbuf("xh_f", (128, KT * LC), BF16)
    sq_t = sbuf("sq_t", (128, LC))
    ln_tmp = sbuf("ln_tmp", (128, LC))
    mkt = sbuf("mkt", (128, 8))
    res_c = arena[:, 0:KT * BLK]
    res_sl = sbuf("res_sl", (128, 2 * LC))
    xcin = sbuf("xcin", (128, 3 + LC + 1))
    cacc = sq_t[:, 0:LC]
    xt_t = sbuf("xt_t", (128, LC))
    ts_bf = sbuf("ts_bf", (128, LC), BF16)
    bc_sb = sbuf("bc_sb", (32, LC))
    bcflat = sbuf("bcflat", (1, 4 * LC))
    evac_t = sbuf("evac_t", (128, LC))
    dt_q = sbuf("dt_q", (128, 4 * LC))
    et_t = ln_tmp
    xt_sl = sbuf("xt_sl", (128, 2 * LC))
    u_q = sbuf("u_q", (128, 4 * LC))
    yac_q = sbuf("yac_q", (128, 4 * LC))
    SCW = LC + 8
    dA_t = sbuf("dA_t", (128, 3 * SCW))
    dBx_t = sbuf("dBx_t", (128, 2 * SCW))
    so_t = sbuf("so_t", (128, 2 * SCW))
    pr_t = sbuf("pr_t", (128, LC))
    opart = sbuf("opart", (128, 2 * LC))

    p_mm = ctx.enter_context(nc.psum_tensor("p_mm", [128, 1024], F32))
    p_bcB = ctx.enter_context(nc.psum_tensor("p_bcB", [128, 1024], F32))
    p_bcC = ctx.enter_context(nc.psum_tensor("p_bcC", [128, 1024], F32))
    p_f = ctx.enter_context(nc.psum_tensor("p_f", [128, 1024], F32))

    block = ctx.enter_context(nc.Block())

    class Sem:
        def __init__(self, name):
            self.h = ctx.enter_context(nc.semaphore(name))
            self.v = 0

    class DSem(Sem):
        def inc(self, inst):
            inst.then_inc(self.h, 16)
            self.v += 16
        def wait(self, e):
            e.wait_ge(self.h, self.v)

    sT = Sem("sT"); sV = Sem("sV"); sS = Sem("sS")
    d_small = DSem("d_small")
    d_smallg = DSem("d_smallg")
    d_cast = [DSem(f"d_cast{i}") for i in range(4)]
    d_cstore = [DSem(f"d_cstore{i}") for i in range(4)]
    d_strip = [DSem(f"d_strip{i}") for i in range(2)]
    d_wg = [DSem(f"d_wg{i}") for i in range(2)]
    d_wu = [DSem(f"d_wu{i}") for i in range(2)]
    d_wd = [DSem(f"d_wd{i}") for i in range(2)]
    d_xh2 = DSem("d_xh2")
    d_xt_st = DSem("d_xt_st")
    d_xt_ld = [DSem(f"d_xt_ld{i}") for i in range(2)]
    d_ts_st = DSem("d_ts_st")
    d_tsbf = DSem("d_tsbf")
    d_bcf = DSem("d_bcf")
    d_op_st = DSem("d_op_st")
    d_op_ld = [DSem(f"d_op_ld{i}") for i in range(2)]
    d_res = [DSem(f"d_res{i}") for i in range(2)]
    d_xmid_st = DSem("d_xmid_st")
    d_stres = DSem("d_stres")
    d_ldx = DSem("d_ldx")
    d_ag_st = DSem("d_ag_st")
    sO = DSem("sO")
    c_ag = [Sem(f"c_ag{i}") for i in range(2)]
    c_ts = [Sem(f"c_ts{i}") for i in range(NCH)]
    c_op = [Sem(f"c_op{i}") for i in range(NCH)]
    ALL_DSEMS = ([d_small, d_smallg, d_xh2, d_xt_st, d_ts_st, d_tsbf, d_bcf, d_op_st,
                  d_xmid_st, d_stres, d_ldx, d_ag_st, sO]
                 + d_cast + d_cstore + d_strip + d_wg + d_wu + d_wd
                 + d_xt_ld + d_op_ld + d_res)
    ALL_CSEMS = c_ag + c_ts + c_op

    def dma4(e, sem, out_ap, in_ap):
        for qq in range(4):
            sem.inc(e.dma_start(out=out_ap[qq * 32:(qq + 1) * 32], in_=in_ap[qq * 32:(qq + 1) * 32]))

    def barrier():
        def t(e): e.drain().then_inc(sT.h, 1)
        block.tensor(t); sT.v += 1
        def v(e): e.drain().then_inc(sV.h, 1)
        block.vector(v); sV.v += 1
        def s(e): e.drain().then_inc(sS.h, 1)
        block.scalar(s); sS.v += 1
        vals = [(sT, sT.v), (sV, sV.v), (sS, sS.v)] + \
               [(d, d.v) for d in ALL_DSEMS] + [(cs, cs.v) for cs in ALL_CSEMS]
        def waiter(e):
            for sm, vl in vals:
                e.wait_ge(sm.h, vl)
        for emit in (block.tensor, block.vector, block.scalar, block.gpsimd, block.sync):
            emit(waiter)

    def tap(name, src_dram):
        if name not in tap_out:
            return
        def t(e):
            sO.inc(e.dma_start(out=tap_out[name], in_=src_dram))
        block.gpsimd(t)

    def finish():
        ov = sO.v
        def fin(e, ov=ov):
            e.wait_ge(sO.h, ov)
        block.gpsimd(fin)
        return nc, ctx, tap_out

    def kt_rows(dram2d, i):
        return dram2d.rearrange("(kt p) t -> kt p t", p=128)[i]

    def _init_sync(e):
        d_small.inc(e.dma_start(out=A_s[:], in_=dr["A_sb"]))
        d_small.inc(e.dma_start(out=cw_s[:], in_=dr["cw_sb"]))
        d_small.inc(e.dma_start(out=cb_s[:], in_=dr["cb_sb"]))
        d_small.inc(e.dma_start(out=D_s[:], in_=dr["D_sb"]))
        d_small.inc(e.dma_start(out=dtb_s[:], in_=dr["dtb_sb"]))
        d_small.inc(e.dma_start(out=al_s[:], in_=dr["alpha"]))
        d_small.inc(e.dma_start(out=xres_h[:, :], in_=dr["x_own"]))
    block.sync(_init_sync)

    def _init_gps(e):
        d_smallg.inc(e.dma_start(out=xpT[:], in_=dr["w_xp"]))
        d_smallg.inc(e.dma_start(out=dtpT[:], in_=dr["w_dtp"]))
    block.gpsimd(_init_gps)

    def _init_vec(e):
        e.memset(ones[:], 1.0)
        e.memset(epsc[:], EPS)
        e.memset(mkt[:], 1.0)
        e.memset(onesr[:], 1.0)
        e.memset(state[:], 0.0)
        e.memset(dA_t[:], 0.0)
        e.memset(carry[:], 0.0).then_inc(sV.h, 1)
    block.vector(_init_vec); sV.v += 1

    CW = 4096
    CAST_JOBS = []
    for cls_in, cls_out in (("w_g", g_bf), ("w_u", u_bf), ("w_d", d_bf),
                            ("w_inx", inx_bf), ("w_ing", ing_bf), ("w_out", outp_bf)):
        total = dr[cls_in].shape[-1]
        assert total % CW == 0
        for w0 in range(0, total, CW):
            CAST_JOBS.append((dr[cls_in][:, w0:w0 + CW], cls_out[:, w0:w0 + CW]))

    for j, (srca, dsta) in enumerate(CAST_JOBS):
        slot = j % 4
        def gld(e, srca=srca, slot=slot):
            d_cstore[slot].wait(e)
            dma4(e, d_cast[slot], cbuf[slot][:, 0:CW], srca)
        block.gpsimd(gld)
        def wst(e, dsta=dsta, slot=slot):
            d_cast[slot].wait(e)
            dma4(e, d_cstore[slot], dsta, cbuf[slot][:, 0:CW])
        block.sync(wst)

    barrier()

    def layernorm(src_fn, dst_fn, n, ready):
        T0 = sT.v
        S0ln = sS.v
        slots = [sq_t[:, 0:n], xt_t[:, 0:n], evac_t[:, 0:n]]
        for kt in range(KT + 1):
            if kt < KT:
                def sq(e, kt=kt, ready=ready, T0=T0):
                    if kt == 0:
                        for w in ready:
                            w(e)
                    if kt >= 3:
                        e.wait_ge(sT.h, T0 + kt - 2)
                    e.activation(slots[kt % 3], src_fn(kt), AF.Square,
                                 bias=0.0, scale=1.0).then_inc(sS.h, 1)
                block.scalar(sq); sS.v += 1
            else:
                def sq(e):
                    e.activation(mkt[:, 0:8], mkt[:, 0:8], AF.Square, bias=0.0, scale=1.0).then_inc(sS.h, 1)
                block.scalar(sq); sS.v += 1
            if kt >= 1:
                ktm = kt - 1
                sv = S0ln + kt + 1
                def mm(e, ktm=ktm, sv=sv, ready=ready):
                    e.wait_ge(sS.h, sv)
                    if ktm == 0:
                        for w in ready:
                            w(e)
                    e.matmul(p_mm[0:1, :n], ones[:, 0:1], src_fn(ktm), start=(ktm == 0), stop=(ktm == KT - 1))
                    e.matmul(p_mm[0:1, 512:512 + n], ones[:, 0:1], slots[ktm % 3],
                             start=(ktm == 0), stop=(ktm == KT - 1)).then_inc(sT.h, 1)
                block.tensor(mm); sT.v += 1
        tv = sT.v
        def rowmath(e, tv=tv, n=n):
            e.wait_ge(sT.h, tv)
            m = rows[:, 0:n]; v = rows[:, LC:LC + n]
            e.tensor_scalar_mul(m, p_mm[0:1, :n], 1.0 / DM)
            e.tensor_tensor(v, m, m, ALU.mult)
            e.scalar_tensor_tensor(v, p_mm[0:1, 512:512 + n], 1.0 / DM, v, ALU.mult, ALU.subtract)
            e.tensor_scalar_add(v, v, EPS).then_inc(sV.h, 1)
        block.vector(rowmath); sV.v += 1
        vv = sV.v
        def rowsqrt(e, vv=vv, n=n):
            e.wait_ge(sV.h, vv)
            e.activation(rows[:, 2 * LC:2 * LC + n], rows[:, LC:LC + n], AF.Sqrt, bias=0.0, scale=1.0).then_inc(sS.h, 1)
        block.scalar(rowsqrt); sS.v += 1
        def mkrr(e):
            e.activation(mkt[:, 0:8], mkt[:, 0:8], AF.Square, bias=0.0, scale=1.0).then_inc(sS.h, 1)
        block.scalar(mkrr); sS.v += 1
        sv2 = sS.v
        def rowrec(e, sv2=sv2, n=n):
            e.wait_ge(sS.h, sv2)
            v = rows[:, LC:LC + n]
            r = rows[:, 2 * LC:2 * LC + n]
            nm = rows[:, 3 * LC:3 * LC + n]
            e.reciprocal(r, r)
            e.tensor_tensor(nm, r, r, ALU.mult)
            e.tensor_tensor(nm, nm, v, ALU.mult)
            e.tensor_scalar(nm, nm, -0.5, 1.5, ALU.mult, ALU.add)
            e.tensor_tensor(r, r, nm, ALU.mult)
            e.scalar_tensor_tensor(nm, rows[:, 0:n], -1.0, r, ALU.mult, ALU.mult).then_inc(sV.h, 1)
        block.vector(rowrec); sV.v += 1
        vv2 = sV.v
        scur = sS.v
        def bcast(e, vv2=vv2, n=n, scur=scur):
            e.wait_ge(sV.h, vv2)
            e.wait_ge(sS.h, scur)
            e.matmul(p_bcB[:, :n], onesr[:], rows[0:1, 2 * LC:2 * LC + n], start=True, stop=True)
            e.matmul(p_bcC[:, :n], onesr[:], rows[0:1, 3 * LC:3 * LC + n], start=True, stop=True).then_inc(sT.h, 1)
        block.tensor(bcast); sT.v += 1
        tv2 = sT.v
        def apply_(e, tv2=tv2, n=n):
            e.wait_ge(sT.h, tv2)
            for kt in range(KT):
                e.tensor_tensor(ln_tmp[:, :n], src_fn(kt), p_bcB[:, :n], ALU.mult)
                i_ = e.tensor_tensor(dst_fn(kt), ln_tmp[:, :n], p_bcC[:, :n], ALU.add)
                if kt == KT - 1:
                    i_.then_inc(sV.h, 1)
        block.vector(apply_); sV.v += 1

    def ffn(n, res_cols, is_ffn2):
        V_F0 = sV.v
        S_F0 = sS.v
        sil_cnt = {}
        gm_cnt = {}
        ac_end_prev = V_F0
        def z(e):
            d_stres.wait(e)
            e.memset(acc_f[:, : KT * n], 0.0).then_inc(sV.h, 1)
        block.vector(z); sV.v += 1
        for e8 in range(NE):
            for hh in range(HPE):
                ht = e8 * HPE + hh
                slot = (ht // 2) % 2
                sub = (ht % 2) * KT * 128
                pair = [p_f, p_mm][ht % 2]
                sgsl = [ln_tmp, evac_t][ht % 2]
                if ht % 2 == 0:
                    tcur = sT.v
                    def ld(e, ht=ht, slot=slot, tcur=tcur):
                        e.wait_ge(sT.h, tcur)
                        dma4(e, d_wg[slot], wg_st[slot][:], g_bf[:, ht * DM:(ht + 2) * DM])
                        dma4(e, d_wu[slot], wu_st[slot][:], u_bf[:, ht * DM:(ht + 2) * DM])
                    block.sync(ld)
                if hh == 0 and e8 > 0:
                    vv_need = ac_end_prev
                else:
                    vv_need = gm_cnt[ht - 2] if ht >= 2 else V_F0
                sv_need = sil_cnt[ht - 2] if ht >= 2 else S_F0
                def gu(e, slot=slot, sub=sub, pair=pair, vv_need=vv_need, sv_need=sv_need, n=n):
                    d_wg[slot].wait(e); d_wu[slot].wait(e)
                    e.wait_ge(sV.h, vv_need)
                    e.wait_ge(sS.h, sv_need)
                    for k in range(KT):
                        e.matmul(pair[:, 0:n], wg_st[slot][:, sub + k * 128:sub + (k + 1) * 128],
                                 xh_f[:, k * LC:k * LC + n], start=(k == 0), stop=(k == KT - 1))
                    for k in range(KT):
                        i_ = e.matmul(pair[:, 512:512 + n], wu_st[slot][:, sub + k * 128:sub + (k + 1) * 128],
                                      xh_f[:, k * LC:k * LC + n], start=(k == 0), stop=(k == KT - 1))
                        if k == KT - 1:
                            i_.then_inc(sT.h, 1)
                block.tensor(gu); sT.v += 1
                tv = sT.v
                vprot = gm_cnt[ht - 2] if ht >= 2 else V_F0
                def sil(e, tv=tv, n=n, vprot=vprot, pair=pair, sgsl=sgsl):
                    e.wait_ge(sT.h, tv)
                    e.wait_ge(sV.h, vprot)
                    e.activation(sgsl[:, :n], pair[:, 0:n], AF.Silu, bias=0.0, scale=1.0).then_inc(sS.h, 1)
                block.scalar(sil); sS.v += 1
                sil_cnt[ht] = sS.v
                sv = sil_cnt[ht] + 1
                def gm(e, hh=hh, sv=sv, n=n, pair=pair, sgsl=sgsl):
                    e.wait_ge(sS.h, sv)
                    e.tensor_tensor(gated[:, hh * n:(hh + 1) * n], sgsl[:, :n], pair[:, 512:512 + n], ALU.mult).then_inc(sV.h, 1)
                block.vector(gm); sV.v += 1
                gm_cnt[ht] = sV.v
            def mkrf(e):
                e.activation(mkt[:, 0:8], mkt[:, 0:8], AF.Square, bias=0.0, scale=1.0).then_inc(sS.h, 1)
            block.scalar(mkrf); sS.v += 1
            for dmt in range(KT):
                slot = (dmt // 2) % 2
                sub = (dmt % 2) * HPE * 128
                if dmt % 2 == 0:
                    tcur = sT.v
                    def ldd(e, e8=e8, dmt=dmt, slot=slot, tcur=tcur):
                        e.wait_ge(sT.h, tcur)
                        base = (e8 * KT + dmt) * HPE * 128
                        dma4(e, d_wd[slot], wd_st[slot][:], d_bf[:, base:base + 2 * HPE * 128])
                    block.sync(ldd)
                vv_need = sV.v
                sv_need = sS.v
                def dn(e, slot=slot, sub=sub, vv_need=vv_need, sv_need=sv_need, n=n):
                    d_wd[slot].wait(e)
                    e.wait_ge(sV.h, vv_need)
                    e.wait_ge(sS.h, sv_need)
                    for h in range(HPE):
                        i_ = e.matmul(p_f[:, 0:n], wd_st[slot][:, sub + h * 128:sub + (h + 1) * 128],
                                      gated[:, h * n:(h + 1) * n], start=(h == 0), stop=(h == HPE - 1))
                        if h == HPE - 1:
                            i_.then_inc(sT.h, 1)
                block.tensor(dn); sT.v += 1
                tv = sT.v
                def ac(e, dmt=dmt, tv=tv, n=n):
                    e.wait_ge(sT.h, tv)
                    e.tensor_tensor(acc_f[:, dmt * n:(dmt + 1) * n], p_f[:, 0:n],
                                    acc_f[:, dmt * n:(dmt + 1) * n], ALU.add).then_inc(sV.h, 1)
                block.vector(ac); sV.v += 1
            ac_end_prev = sV.v
        for dmt in range(KT):
            slot = dmt % 2
            vcur = sV.v
            def ldr(e, dmt=dmt, slot=slot, res_cols=res_cols, n=n, vcur=vcur):
                e.wait_ge(sV.h, vcur)
                d_stres.wait(e)
                d_xmid_st.wait(e)
                d_res[slot].inc(e.dma_start(
                    out=res_sl[:, slot * LC:slot * LC + n],
                    in_=kt_rows(xres_h, dmt)[:, res_cols:res_cols + n]))
            block.gpsimd(ldr)
            def upd(e, dmt=dmt, slot=slot, n=n):
                d_res[slot].wait(e)
                e.scalar_tensor_tensor(res_sl[:, slot * LC:slot * LC + n],
                                       acc_f[:, dmt * n:(dmt + 1) * n], 0.5,
                                       res_sl[:, slot * LC:slot * LC + n], ALU.mult, ALU.add).then_inc(sV.h, 1)
            block.vector(upd); sV.v += 1
            vv = sV.v
            if is_ffn2:
                def st(e, dmt=dmt, slot=slot, vv=vv, res_cols=res_cols, n=n):
                    e.wait_ge(sV.h, vv)
                    sO.inc(e.dma_start(out=kt_rows(out, dmt)[:, res_cols:res_cols + n],
                                       in_=res_sl[:, slot * LC:slot * LC + n]))
                block.gpsimd(st)
            else:
                def st(e, dmt=dmt, slot=slot, vv=vv, res_cols=res_cols, n=n):
                    e.wait_ge(sV.h, vv)
                    d_xmid_st.inc(e.dma_start(out=kt_rows(xres_h, dmt)[:, res_cols:res_cols + n],
                                              in_=res_sl[:, slot * LC:slot * LC + n]))
                block.gpsimd(st)

    for h in range(2):
        base = h * LC
        def ldx(e, base=base):
            d_xmid_st.wait(e)
            d_ldx.inc(e.dma_start(
                out=acc_f[:, :KT * LC].rearrange("p (kt t) -> p kt t", t=LC),
                in_=xres_h.rearrange("(kt p) t -> p kt t", p=128)[:, :, base:base + LC]))
        block.gpsimd(ldx)
        layernorm(lambda kt: acc_f[:, kt * LC:(kt + 1) * LC],
                  lambda kt: xh_f[:, kt * LC:(kt + 1) * LC], LC,
                  [lambda e: d_ldx.wait(e)])
        ffn(LC, base, False)
        barrier()
    tap("tap_xmid", xres_h)
    if stage == "ffn1":
        return finish()

    for h in range(2):
        base = h * LC
        def ldx2(e, base=base):
            d_xmid_st.wait(e)
            d_ldx.inc(e.dma_start(
                out=acc_f[:, :KT * LC].rearrange("p (kt t) -> p kt t", t=LC),
                in_=xres_h.rearrange("(kt p) t -> p kt t", p=128)[:, :, base:base + LC]))
        block.gpsimd(ldx2)
        layernorm(lambda kt: acc_f[:, kt * LC:(kt + 1) * LC],
                  lambda kt: xh_f[:, kt * LC:(kt + 1) * LC], LC,
                  [lambda e: d_ldx.wait(e)])
        vv = sV.v
        def stag(e, h=h, vv=vv):
            e.wait_ge(sV.h, vv)
            d_ag_st.inc(e.dma_start(out=ag_in[h][0].rearrange("(kt p) t -> p kt t", p=128),
                                    in_=xh_f[:].rearrange("p (kt t) -> p kt t", t=LC)))
        block.gpsimd(stag)
        def agc(e, h=h):
            d_ag_st.wait(e)
            e.collective_compute("AllGather", ALU.bypass, replica_groups=PAIRS,
                                 ins=[ag_in[h]], outs=[ag_out[h]]).then_inc(c_ag[h].h, 1)
        block.gpsimd(agc); c_ag[h].v += 1
        barrier()

    for c in range(NCH):
        hf = c // 2
        coff = (c % 2) * BLK
        tcur0 = sT.v
        def ldxh2(e, hf=hf, coff=coff, tcur0=tcur0):
            e.wait_ge(c_ag[hf].h, c_ag[hf].v)
            e.wait_ge(sT.h, tcur0)
            for slot in range(2):
                d_xh2.inc(e.dma_start(
                    out=xh2.rearrange("p (kt t) -> p kt t", t=LC)[:, :, slot * BLK:(slot + 1) * BLK],
                    in_=ag_out[hf][slot].rearrange("(kt p) t -> p kt t", p=128)[:, :, coff:coff + BLK]))
        block.gpsimd(ldxh2)

        S_A0 = sS.v
        for d in range(DTl):
            slot = d % 2
            tcur = sT.v
            def lds(e, d=d, slot=slot, tcur=tcur):
                e.wait_ge(sT.h, tcur)
                dma4(e, d_strip[slot], wstrip[slot][:], inx_bf[:, d * DM:(d + 1) * DM])
            block.sync(lds)
            scur = sS.v
            def mmx(e, slot=slot, scur=scur):
                d_strip[slot].wait(e)
                d_xh2.wait(e)
                e.wait_ge(sS.h, scur)
                for k in range(KT):
                    i_ = e.matmul(p_mm[:, slot * 512:slot * 512 + LC], wstrip[slot][:, k * 128:(k + 1) * 128],
                                  xh2[:, k * LC:(k + 1) * LC], start=(k == 0), stop=(k == KT - 1))
                    if k == KT - 1:
                        i_.then_inc(sT.h, 1)
            block.tensor(mmx); sT.v += 1
            tv = sT.v
            vcur = sV.v
            def evx(e, slot=slot, tv=tv, vcur=vcur):
                e.wait_ge(sT.h, tv)
                e.wait_ge(sV.h, vcur)
                e.copy(xcin[:, 3:3 + LC], p_mm[:, slot * 512:slot * 512 + LC]).then_inc(sS.h, 1)
            block.scalar(evx); sS.v += 1
            def mkre(e):
                e.activation(mkt[:, 0:8], mkt[:, 0:8], AF.Square, bias=0.0, scale=1.0).then_inc(sS.h, 1)
            block.scalar(mkre); sS.v += 1
            sv = sS.v
            def conv(e, d=d, sv=sv):
                e.wait_ge(sS.h, sv)
                e.tensor_copy(xcin[:, 0:3], carry[:, d * 4:d * 4 + 3])
                e.tensor_copy(carry[:, d * 4:d * 4 + 3], xcin[:, LC:LC + 3])
                e.tensor_scalar_mul(cacc, xcin[:, 0:LC], cw_s[:, d * 4:d * 4 + 1])
                for k in range(1, 4):
                    i_ = e.scalar_tensor_tensor(cacc, xcin[:, k:k + LC], cw_s[:, d * 4 + k:d * 4 + k + 1],
                                                cacc, ALU.mult, ALU.add)
                    if k == 3:
                        i_.then_inc(sV.h, 1)
            block.vector(conv); sV.v += 1
            vv = sV.v
            def silu_x(e, d=d, vv=vv):
                e.wait_ge(sV.h, vv)
                d_xt_st.wait(e)
                e.activation(xt_t[:], cacc, AF.Silu, bias=cb_s[:, d:d + 1], scale=1.0)
                e.copy(xt_bf[:, d * LC:(d + 1) * LC], xt_t[:]).then_inc(sS.h, 1)
            block.scalar(silu_x); sS.v += 1
            sv2 = S_A0 + (3 * d + 4 if d < DTl - 1 else 3 * DTl + 1)
            def stxt(e, d=d, c=c, sv2=sv2):
                e.wait_ge(sS.h, sv2)
                d_xt_st.inc(e.dma_start(out=xt_hbm[d][:, c * LC:(c + 1) * LC], in_=xt_t[:]))
            block.gpsimd(stxt)
        def mkra(e):
            e.activation(mkt[:, 0:8], mkt[:, 0:8], AF.Square, bias=0.0, scale=1.0).then_inc(sS.h, 1)
        block.scalar(mkra); sS.v += 1
        assert sS.v == S_A0 + 3 * DTl + 1
        s_xtbf = sS.v

        vcur = sV.v
        def xproj(e, s_xtbf=s_xtbf, vcur=vcur):
            e.wait_ge(sS.h, s_xtbf)
            e.wait_ge(sV.h, vcur)
            for k in range(DTl):
                e.matmul(p_mm[:, 0:LC], xpT[:, k * 160:k * 160 + 128],
                         xt_bf[:, k * LC:(k + 1) * LC], start=(k == 0), stop=(k == DTl - 1))
            for k in range(DTl):
                i_ = e.matmul(p_bcB[0:32, 0:LC], xpT[:, k * 160 + 128:(k + 1) * 160],
                              xt_bf[:, k * LC:(k + 1) * LC], start=(k == 0), stop=(k == DTl - 1))
                if k == DTl - 1:
                    i_.then_inc(sT.h, 1)
        block.tensor(xproj); sT.v += 1
        tv = sT.v
        def evts(e, tv=tv):
            e.wait_ge(sT.h, tv)
            e.copy(evac_t[:], p_mm[:, 0:LC])
            e.copy(bc_sb[:], p_bcB[0:32, 0:LC]).then_inc(sS.h, 1)
        block.scalar(evts); sS.v += 1
        def mkrt(e):
            e.activation(mkt[:, 0:8], mkt[:, 0:8], AF.Square, bias=0.0, scale=1.0).then_inc(sS.h, 1)
        block.scalar(mkrt); sS.v += 1
        sv = sS.v
        def sts(e, c=c, sv=sv):
            e.wait_ge(sS.h, sv)
            d_ts_st.inc(e.dma_start(out=ts_in[c][0:128, :], in_=evac_t[:]))
            d_ts_st.inc(e.dma_start(out=ts_in[c][128:160, :], in_=bc_sb[:]))
        block.gpsimd(sts)
        def arc(e, c=c):
            d_ts_st.wait(e)
            e.collective_compute("AllReduce", ALU.add, replica_groups=PAIRS,
                                 ins=[ts_in[c]], outs=[ts_out[c]]).then_inc(c_ts[c].h, 1)
        block.gpsimd(arc); c_ts[c].v += 1
        def ldts(e, c=c):
            e.wait_ge(c_ts[c].h, c_ts[c].v)
            d_tsbf.inc(e.dma_start(out=ts_bf[:], in_=ts_out[c][0:128, :]))
        block.gpsimd(ldts)

        if stage == "dbg3" and c == 1:
            def snap1(e):
                sO.inc(e.dma_start(out=state_snap[1], in_=state[:]))
            block.gpsimd(snap1)
        QD = 4
        v_prep_ok = sV.v
        for q in range(DTl // QD):
            S_prep0 = sS.v
            for dq in range(QD):
                d = q * QD + dq
                dsl = d % 2
                scur = sS.v
                def dtmm(e, d=d, scur=scur):
                    d_tsbf.wait(e)
                    e.wait_ge(sS.h, scur)
                    e.matmul(p_mm[:, 512:512 + LC], dtpT[:, d * 128:(d + 1) * 128], ts_bf[:],
                             start=True, stop=True).then_inc(sT.h, 1)
                block.tensor(dtmm); sT.v += 1
                tv = sT.v
                def softp(e, dq=dq, d=d, tv=tv, vfix=v_prep_ok):
                    e.wait_ge(sT.h, tv)
                    e.wait_ge(sV.h, vfix)
                    e.activation(et_t[:, 0:LC], p_mm[:, 512:512 + LC], AF.Exp, bias=dtb_s[:, d:d + 1], scale=1.0)
                    e.activation(dt_q[:, dq * LC:(dq + 1) * LC], et_t[:, 0:LC], AF.Ln, bias=1.0, scale=1.0).then_inc(sS.h, 1)
                block.scalar(softp); sS.v += 1
                assert sS.v == S_prep0 + dq + 1
                vcur = sV.v
                def ldxt(e, d=d, c=c, dsl=dsl, vcur=vcur):
                    e.wait_ge(sV.h, vcur)
                    d_xt_st.wait(e)
                    dma4(e, d_xt_ld[dsl], xt_sl[:, dsl * LC:(dsl + 1) * LC],
                         xt_hbm[d][:, c * LC:(c + 1) * LC])
                block.gpsimd(ldxt)
                s_dt = S_prep0 + dq + 2
                def uprep(e, dq=dq, d=d, dsl=dsl, s_dt=s_dt):
                    e.wait_ge(sS.h, s_dt)
                    d_xt_ld[dsl].wait(e)
                    e.tensor_tensor(u_q[:, dq * LC:(dq + 1) * LC], dt_q[:, dq * LC:(dq + 1) * LC],
                                    xt_sl[:, dsl * LC:(dsl + 1) * LC], ALU.mult)
                    e.tensor_scalar_mul(yac_q[:, dq * LC:(dq + 1) * LC],
                                        xt_sl[:, dsl * LC:(dsl + 1) * LC], D_s[:, d:d + 1]).then_inc(sV.h, 1)
                block.vector(uprep); sV.v += 1
            def mkr2(e):
                e.activation(mkt[:, 0:8], mkt[:, 0:8], AF.Square, bias=0.0, scale=1.0).then_inc(sS.h, 1)
            block.scalar(mkr2); sS.v += 1
            assert sS.v == S_prep0 + QD + 1
            V0 = sV.v
            S0 = sS.v
            T0 = sT.v
            NREF = 8
            for r in range(NREF):
                def refill(e, c=c, r=r, T0=T0, tcur_r=sT.v):
                    e.wait_ge(c_ts[c].h, c_ts[c].v)
                    e.wait_ge(sT.h, T0 + 4 * r if r >= 1 else tcur_r)
                    d_bcf.inc(e.dma_start(
                        out=bcflat[:, 0:2 * LC],
                        in_=ts_out[c][128 + 2 * r:130 + 2 * r, :].rearrange("a b -> (a b)").unsqueeze(0)))
                    d_bcf.inc(e.dma_start(
                        out=bcflat[:, 2 * LC:4 * LC],
                        in_=ts_out[c][144 + 2 * r:146 + 2 * r, :].rearrange("a b -> (a b)").unsqueeze(0)))
                block.gpsimd(refill)
                for i2 in range(2):
                    s_ = 2 * r + i2
                    def bca(e, s_=s_, i2=i2, V0=V0):
                        d_bcf.wait(e)
                        if s_ >= 2:
                            e.wait_ge(sV.h, V0 + QD * (s_ - 1))
                        else:
                            e.wait_ge(sV.h, V0)
                        e.matmul(p_bcB[:, (s_ % 2) * 512:(s_ % 2) * 512 + LC], onesr[:],
                                 bcflat[0:1, i2 * LC:(i2 + 1) * LC], start=True, stop=True)
                        e.matmul(p_bcC[:, (s_ % 2) * 512:(s_ % 2) * 512 + LC], onesr[:],
                                 bcflat[0:1, 2 * LC + i2 * LC:2 * LC + (i2 + 1) * LC], start=True, stop=True).then_inc(sT.h, 2)
                    block.tensor(bca); sT.v += 2
            def dA_all(e, q=q, V0=V0):
                for s_ in range(S):
                    for dq in range(QD):
                        d = q * QD + dq
                        j = s_ * QD + dq
                        e.wait_ge(sV.h, V0 + j - 2 if j >= 3 else V0)
                        e.activation(dA_t[:, (j % 3) * SCW + 1:(j % 3) * SCW + 1 + LC],
                                     dt_q[:, dq * LC:(dq + 1) * LC], AF.Exp,
                                     bias=0.0, scale=A_s[:, d * S + s_:d * S + s_ + 1]).then_inc(sS.h, 1)
                e.activation(mkt[:, 0:8], mkt[:, 0:8], AF.Square, bias=0.0, scale=1.0).then_inc(sS.h, 1)
            block.scalar(dA_all); sS.v += S * QD + 1
            def scan_all(e, q=q, c=c, V0=V0, S0=S0, T0=T0):
                pend = None
                for s_ in range(S):
                    e.wait_ge(sT.h, T0 + 2 * min(s_ + 2, S))
                    for dq in range(QD):
                        d = q * QD + dq
                        j = s_ * QD + dq
                        sl = (j % 2) * SCW
                        bsl = (s_ % 2) * 512
                        rd_off = (c % 2) * DTl * S
                        wr_off = ((c + 1) % 2) * DTl * S
                        col = d * S + s_
                        e.wait_ge(sS.h, S0 + min(j + 2, S * QD + 1))
                        e.tensor_copy(dBx_t[:, sl:sl + 1], state[:, rd_off + col:rd_off + col + 1])
                        e.tensor_tensor(dBx_t[:, sl + 1:sl + 1 + LC], u_q[:, dq * LC:(dq + 1) * LC],
                                        p_bcB[:, bsl:bsl + LC], ALU.mult)
                        e.tensor_tensor_scan(so_t[:, sl:sl + 1 + LC], dA_t[:, (j % 3) * SCW:(j % 3) * SCW + 1 + LC],
                                             dBx_t[:, sl:sl + 1 + LC], 0.0, ALU.mult, ALU.add)
                        if pend is not None:
                            e.tensor_copy(state[:, wr_off + pend[0]:wr_off + pend[0] + 1],
                                          so_t[:, pend[1] + LC:pend[1] + LC + 1])
                        e.tensor_tensor(pr_t[:], so_t[:, sl + 1:sl + 1 + LC], p_bcC[:, bsl:bsl + LC], ALU.mult)
                        e.tensor_tensor(yac_q[:, dq * LC:(dq + 1) * LC], pr_t[:],
                                        yac_q[:, dq * LC:(dq + 1) * LC], ALU.add).then_inc(sV.h, 1)
                        pend = (col, sl)
                if s_ == S - 1:
                    e.tensor_copy(state[:, wr_off + pend[0]:wr_off + pend[0] + 1],
                                  so_t[:, pend[1] + LC:pend[1] + LC + 1])
            block.vector(scan_all); sV.v += S * QD
            v_scan = sV.v
            G_q0 = sS.v
            yfm_cnt_q = []
            for dq in range(QD):
                d = q * QD + dq
                dsl = d % 2
                sgsl = [ln_tmp, evac_t][dq % 2]
                tcur = sT.v
                def ldg(e, d=d, dsl=dsl, tcur=tcur):
                    e.wait_ge(sT.h, tcur)
                    dma4(e, d_strip[dsl], wstrip[dsl][:], ing_bf[:, d * DM:(d + 1) * DM])
                block.sync(ldg)
                scur2 = sS.v
                def gmm(e, dsl=dsl, scur2=scur2, v_scan=v_scan):
                    d_strip[dsl].wait(e)
                    d_xh2.wait(e)
                    e.wait_ge(sS.h, scur2)
                    if dsl % 2 == 1:
                        e.wait_ge(sV.h, v_scan)
                    for k in range(KT):
                        i_ = e.matmul(p_mm[:, (dsl % 2) * 512:(dsl % 2) * 512 + LC],
                                      wstrip[dsl][:, k * 128:(k + 1) * 128],
                                      xh2[:, k * LC:(k + 1) * LC], start=(k == 0), stop=(k == KT - 1))
                        if k == KT - 1:
                            i_.then_inc(sT.h, 1)
                block.tensor(gmm); sT.v += 1
                tv = sT.v
                vfix = yfm_cnt_q[dq - 2] if dq >= 2 else v_prep_ok
                def sgx(e, dsl=dsl, tv=tv, vfix=vfix, sgsl=sgsl):
                    e.wait_ge(sT.h, tv)
                    e.wait_ge(sV.h, vfix)
                    e.activation(sgsl[:, :LC], p_mm[:, (dsl % 2) * 512:(dsl % 2) * 512 + LC],
                                 AF.Silu, bias=0.0, scale=1.0).then_inc(sS.h, 1)
                block.scalar(sgx); sS.v += 1
                assert sS.v == G_q0 + dq + 1
                sv = G_q0 + dq + 2
                def yfm(e, dq=dq, d=d, sv=sv, v_scan=v_scan, sgsl=sgsl):
                    e.wait_ge(sS.h, sv)
                    e.wait_ge(sV.h, v_scan)
                    e.tensor_tensor(yf_t[:, d * LC:(d + 1) * LC], yac_q[:, dq * LC:(dq + 1) * LC],
                                    sgsl[:, :LC], ALU.mult).then_inc(sV.h, 1)
                block.vector(yfm); sV.v += 1
                yfm_cnt_q.append(sV.v)
            def mkrg(e):
                e.activation(mkt[:, 0:8], mkt[:, 0:8], AF.Square, bias=0.0, scale=1.0).then_inc(sS.h, 1)
            block.scalar(mkrg); sS.v += 1
            assert sS.v == G_q0 + QD + 1
            v_prep_ok = sV.v
        v_yf = sV.v
        if stage == "dbg1" and c == 1 and "tap_yf" in tap_out:
            def tapyf(e, v_yf=v_yf):
                e.wait_ge(sV.h, v_yf)
                sO.inc(e.dma_start(out=tap_out["tap_yf"], in_=yf_t[:]))
                sO.inc(e.dma_start(out=tap_out["tap_ts"], in_=ts_out[1]))
            block.gpsimd(tapyf)

        S_C0 = sS.v
        op_base = d_op_st.v
        for i in range(KT):
            slot = i % 2
            tcur = sT.v
            def ldo(e, i=i, slot=slot, tcur=tcur):
                e.wait_ge(sT.h, tcur)
                dma4(e, d_strip[slot], wstrip[slot][:], outp_bf[:, i * DH:(i + 1) * DH])
            block.sync(ldo)
            scur = sS.v
            def omm(e, slot=slot, v_yf=v_yf, scur=scur):
                d_strip[slot].wait(e)
                e.wait_ge(sV.h, v_yf)
                e.wait_ge(sS.h, scur)
                for k in range(DTl):
                    i_ = e.matmul(p_mm[:, slot * 512:slot * 512 + LC], wstrip[slot][:, k * 128:(k + 1) * 128],
                                  yf_t[:, k * LC:(k + 1) * LC], start=(k == 0), stop=(k == DTl - 1))
                    if k == DTl - 1:
                        i_.then_inc(sT.h, 1)
            block.tensor(omm); sT.v += 1
            tv = sT.v
            opw = op_base + 16 * (i - 1) if i >= 2 else op_base
            def evo(e, slot=slot, tv=tv, opw=opw):
                e.wait_ge(sT.h, tv)
                e.wait_ge(d_op_st.h, opw)
                e.copy(opart[:, slot * LC:(slot + 1) * LC], p_mm[:, slot * 512:slot * 512 + LC]).then_inc(sS.h, 1)
            block.scalar(evo); sS.v += 1
            assert sS.v == S_C0 + i + 1
            sv = S_C0 + i + 2
            def sto(e, i=i, c=c, slot=slot, sv=sv):
                e.wait_ge(sS.h, sv)
                d_op_st.inc(e.dma_start(out=op_in[c].rearrange("(kt p) t -> kt p t", p=128)[i],
                                        in_=opart[:, slot * LC:(slot + 1) * LC]))
            block.gpsimd(sto)
        def mkro(e):
            e.activation(mkt[:, 0:8], mkt[:, 0:8], AF.Square, bias=0.0, scale=1.0).then_inc(sS.h, 1)
        block.scalar(mkro); sS.v += 1
        assert sS.v == S_C0 + KT + 1
        def aro(e, c=c):
            d_op_st.wait(e)
            e.collective_compute("AllReduce", ALU.add, replica_groups=PAIRS,
                                 ins=[op_in[c]], outs=[op_out[c]]).then_inc(c_op[c].h, 1)
        block.gpsimd(aro); c_op[c].v += 1

        owncol = c * BLK
        for i in range(KT):
            slot = i % 2
            vcur = sV.v
            def ldop(e, i=i, c=c, slot=slot, vcur=vcur, owncol=owncol):
                e.wait_ge(c_op[c].h, c_op[c].v)
                e.wait_ge(sV.h, vcur)
                d_stres.wait(e)
                d_op_ld[slot].inc(e.dma_start(
                    out=opart[:, slot * LC:(slot + 1) * LC],
                    in_=op_out[c].rearrange("(kt p) t -> kt p t", p=128)[i]))
                d_op_ld[slot].inc(e.dma_start(
                    out=res_sl[:, slot * LC:slot * LC + BLK],
                    in_=kt_rows(xres_h, i)[:, owncol:owncol + BLK]))
            block.gpsimd(ldop)
            def resx(e, i=i, slot=slot):
                d_op_ld[slot].wait(e)
                e.scalar_tensor_tensor(res_sl[:, slot * LC:slot * LC + BLK],
                                       opart[:, slot * LC:slot * LC + BLK], al_s[:, 0:1],
                                       res_sl[:, slot * LC:slot * LC + BLK], ALU.mult, ALU.add)
                e.scalar_tensor_tensor(res_c[:, i * BLK:(i + 1) * BLK],
                                       opart[:, slot * LC + BLK:slot * LC + 2 * BLK], al_s[:, 1:2],
                                       res_sl[:, slot * LC:slot * LC + BLK], ALU.mult, ALU.add).then_inc(sV.h, 1)
            block.vector(resx); sV.v += 1
            vv = sV.v
            def stres(e, i=i, vv=vv, owncol=owncol):
                e.wait_ge(sV.h, vv)
                d_stres.inc(e.dma_start(out=kt_rows(xres_h, i)[:, owncol:owncol + BLK],
                                        in_=res_c[:, i * BLK:(i + 1) * BLK]))
            block.gpsimd(stres)
        v_res = sV.v
        layernorm(lambda kt: res_c[:, kt * BLK:(kt + 1) * BLK],
                  lambda kt: xh_f[:, kt * LC + (c % 2) * BLK:kt * LC + (c % 2) * BLK + BLK],
                  BLK, [lambda e: e.wait_ge(sV.h, v_res)])

        if c % 2 == 1 and stage == "full":
            g = c // 2
            ffn(LC, g * LC, True)
        if stage == "dbg1" and c == 1 and "tap_op" in tap_out:
            def tapop(e):
                e.wait_ge(c_op[1].h, c_op[1].v)
                sO.inc(e.dma_start(out=tap_out["tap_op"], in_=op_out[1]))
            block.gpsimd(tapop)
        barrier()
        if stage == "dbg3" and c == 0:
            def snap0(e):
                sO.inc(e.dma_start(out=state_snap[0], in_=state[:]))
            block.gpsimd(snap0)
            barrier()
        if stage == "dbg3" and c == 1:
            tap("tap_snap", state_snap.rearrange("a p n -> (a p) n"))
            tap("tap_xafter", xres_h)
            return finish()
        if stage == "dbg1" and c == 1:
            tap("tap_xafter", xres_h)
            return finish()
        if stage == "mamba1" and c == 0:
            tap("tap_xafter", xres_h)
            return finish()
    tap("tap_xafter", xres_h)
    tap("tap_xt", xt_hbm.rearrange("d p l -> (d p) l"))
    tap("tap_ts", ts_out[1])
    tap("tap_op", op_out[1])
    tap("tap_ag", ag_out[0].rearrange("s d t -> (s d) t"))
    if stage == "mamba":
        return finish()

    return finish()


def prep_inputs(inputs):
    x = np.ascontiguousarray(np.asarray(inputs["L0_attn_hidden_states"], np.float32))
    w_in = np.asarray(inputs["L0_attn_in_proj_w"], np.float32)
    conv_w = np.asarray(inputs["L0_attn_conv_w"], np.float32)
    conv_b = np.asarray(inputs["L0_attn_conv_b"], np.float32)
    x_proj = np.asarray(inputs["L0_attn_x_proj_w"], np.float32)
    dt_w = np.asarray(inputs["L0_attn_dt_proj_w"], np.float32)
    dt_b = np.asarray(inputs["L0_attn_dt_proj_b"], np.float32)
    A = np.asarray(inputs["L0_attn_A"], np.float32)
    Dv = np.asarray(inputs["L0_attn_D"], np.float32)
    w_out = np.asarray(inputs["L0_attn_out_proj_w"], np.float32)
    wg = np.asarray(inputs["L0_ffn_w_gate"], np.float32)
    wu = np.asarray(inputs["L0_ffn_w_up"], np.float32)
    wd = np.asarray(inputs["L0_ffn_w_down"], np.float32)

    def pmaj(W, nt):
        K = W.shape[1]
        return np.ascontiguousarray(
            W.reshape(nt, 128, K // 128, 128).transpose(3, 0, 2, 1).reshape(128, nt * K))
    wg_s = pmaj(wg, HT)
    wu_s = pmaj(wu, HT)
    wd_s = np.ascontiguousarray(
        wd.reshape(KT, 128, NE, HPE, 128).transpose(4, 2, 0, 3, 1).reshape(128, NE * KT * HPE * 128))

    in_maps = []
    for core in range(8):
        p = core // 2
        q = core & 1
        blocks = [q, q + 2, q + 4, q + 6]
        xb = x[p].reshape(8, BLK, DM)
        x_ownT = np.ascontiguousarray(
            np.concatenate([xb[b] for b in blocks], axis=0).T)
        xrows = slice(q * DH, (q + 1) * DH)
        grows = slice(4096 + q * DH, 4096 + (q + 1) * DH)
        w_inx = pmaj(w_in[xrows], DTl)
        w_ing = pmaj(w_in[grows], DTl)
        w_outs = pmaj(w_out[:, q * DH:(q + 1) * DH], KT)
        w_xp = np.ascontiguousarray(
            x_proj[:, q * DH:(q + 1) * DH].T.reshape(KT, 128, 160).transpose(1, 0, 2).reshape(128, KT * 160))
        w_dtp = np.ascontiguousarray(dt_w[q * DH:(q + 1) * DH, :].T)
        Ah = A[q * DH:(q + 1) * DH]
        A_sb = np.ascontiguousarray(Ah.reshape(DTl, 128, S).transpose(1, 0, 2).reshape(128, DTl * S))
        cwh = conv_w[q * DH:(q + 1) * DH]
        cw_sb = np.ascontiguousarray(cwh.reshape(DTl, 128, 4).transpose(1, 0, 2).reshape(128, DTl * 4))
        cb_sb = np.ascontiguousarray(conv_b[q * DH:(q + 1) * DH].reshape(DTl, 128).T)
        D_sb = np.ascontiguousarray(Dv[q * DH:(q + 1) * DH].reshape(DTl, 128).T)
        dtb_sb = np.ascontiguousarray(dt_b[q * DH:(q + 1) * DH].reshape(DTl, 128).T)
        alpha = np.zeros((128, 2), np.float32)
        alpha[:, 0] = 1.0 if q == 0 else 0.0
        alpha[:, 1] = 0.0 if q == 0 else 1.0
        in_maps.append({
            "x_own": x_ownT, "w_inx": w_inx, "w_ing": w_ing, "w_out": w_outs,
            "w_xp": w_xp, "w_dtp": w_dtp, "w_g": wg_s, "w_u": wu_s, "w_d": wd_s,
            "A_sb": A_sb, "cw_sb": cw_sb, "cb_sb": cb_sb, "D_sb": D_sb,
            "dtb_sb": dtb_sb, "alpha": alpha,
        })
    return in_maps


def own_tokens_to_full(results, key="out"):
    full = np.zeros((4, 2048, DM), np.float32)
    for core in range(8):
        p = core // 2
        q = core & 1
        blocks = [q, q + 2, q + 4, q + 6]
        o = np.asarray(results[core][key]).T.reshape(4, BLK, DM)
        for j, b in enumerate(blocks):
            full[p, b * BLK:(b + 1) * BLK, :] = o[j]
    return full


_CACHE = {}


def kernel(**inputs):
    import concourse.bass_utils as _bu
    if "nc" not in _CACHE:
        nc, ctx, _ = build(stage="full", taps=())
        _CACHE["nc"] = nc
        _CACHE["ctx"] = ctx
    nc = _CACHE["nc"]
    in_maps = prep_inputs(inputs)
    res = _bu.run_bass_kernel_spmd(nc, in_maps, core_ids=list(range(8)))
    full = own_tokens_to_full(res.results, "out")
    return np.ascontiguousarray(full.astype(np.float32))


# version 06
# speedup vs baseline: 1.0615x; 1.0615x over previous
import contextlib
import numpy as np
import concourse.bass as bass
import concourse.mybir as mybir

F32 = mybir.dt.float32
BF16 = mybir.dt.bfloat16
ALU = mybir.AluOpType
AF = mybir.ActivationFunctionType

DM = 2048
DH = 2048
TOWN = 1024
L = 2048
S = 16
NCH = 4
LC = 512
BLK = 256
KT = 16
DTl = 16
HT = 64
NE = 8
HPE = 8
EPS = 1e-5
PAIRS = [[0, 1], [2, 3], [4, 5], [6, 7]]


def build(stage="full", taps=()):
    nc = bass.Bass("TRN2", target_bir_lowering=False, debug=False, detect_race_conditions=False)
    ctx = contextlib.ExitStack()

    dr = {}
    def din(name, shape):
        dr[name] = nc.dram_tensor(name, list(shape), F32, kind="ExternalInput").ap()
    din("x_own", (DM, TOWN))
    din("w_inx", (128, DTl * DM))
    din("w_ing", (128, DTl * DM))
    din("w_out", (128, KT * DH))
    din("w_xp", (128, KT * 160))
    din("w_dtp", (128, DH))
    din("w_g", (128, HT * DM))
    din("w_u", (128, HT * DM))
    din("w_d", (128, NE * KT * HPE * 128))
    din("A_sb", (128, DTl * S))
    din("cw_sb", (128, DTl * 4))
    din("cb_sb", (128, DTl))
    din("D_sb", (128, DTl))
    din("dtb_sb", (128, DTl))
    din("alpha", (128, 2))
    out = nc.dram_tensor("out", [DM, TOWN], F32, kind="ExternalOutput").ap()
    tap_out = {}
    for tname, tshape in taps:
        tap_out[tname] = nc.dram_tensor(tname, list(tshape), F32, kind="ExternalOutput").ap()

    def scr(name, shape, dt=F32):
        return nc.dram_tensor(name, list(shape), dt, kind="Internal").ap()
    gu_bf = scr("gu_bf", (128, 2 * HT * DM), BF16)
    d_bf = scr("d_bf", (128, NE * KT * HPE * 128), BF16)
    inx_bf = scr("inx_bf", (128, DTl * DM), BF16)
    ing_bf = scr("ing_bf", (128, DTl * DM), BF16)
    outp_bf = scr("outp_bf", (128, KT * DH), BF16)
    xres_h = scr("xres_h", (DM, TOWN))
    xt_hbm = scr("xt_hbm", (DTl, 128, L))
    ag_in = [scr(f"ag_in{h}", (1, DM, LC), BF16) for h in range(2)]
    ag_out = [scr(f"ag_out{h}", (2, DM, LC), BF16) for h in range(2)]
    ts_in = [scr(f"ts_in{c}", (160, LC)) for c in range(NCH)]
    ts_out = [scr(f"ts_out{c}", (160, LC)) for c in range(NCH)]
    op_in = [scr(f"op_in{c}", (DM, LC)) for c in range(NCH)]
    op_out = [scr(f"op_out{c}", (DM, LC)) for c in range(NCH)]
    state_snap = scr("state_snap", (2, 128, 2 * DTl * S))

    sb = {}
    def sbuf(name, shape, dt=F32):
        sb[name] = ctx.enter_context(nc.sbuf_tensor(name, list(shape), dt))
        return sb[name]

    ones = sbuf("ones", (128, 2))
    epsc = sbuf("epsc", (128, 1))
    onesr = sbuf("onesr", (1, 128))
    rows = sbuf("rows", (1, 4 * LC))
    A_s = sbuf("A_s", (128, DTl * S))
    cw_s = sbuf("cw_s", (128, DTl * 4))
    cb_s = sbuf("cb_s", (128, DTl))
    D_s = sbuf("D_s", (128, DTl))
    dtb_s = sbuf("dtb_s", (128, DTl))
    al_s = sbuf("al_s", (128, 2))
    xpT = sbuf("xpT", (128, KT * 160), BF16)
    dtpT = sbuf("dtpT", (128, DH), BF16)
    state = sbuf("state", (128, 2 * DTl * S))
    carry = sbuf("carry", (128, DTl * 4))
    wsh = sbuf("wsh", (128, 8192), BF16)
    wstrip = [wsh[:, i * 4096:(i + 1) * 4096] for i in range(2)]
    wd_st = [wsh[:, i * 4096:(i + 1) * 4096] for i in range(2)]
    wgu_st = [sbuf(f"wgu_st{i}", (128, 8192), BF16) for i in range(2)]
    arena = sbuf("arena", (128, 12288))
    acc_f = arena[:, 0:8192]
    gated = arena[:, 8192:10240].bitcast(BF16)
    xh2 = arena[:, 0:4096].bitcast(BF16)
    xt_bf = arena[:, 4096:8192].bitcast(BF16)
    yf_t = arena[:, 8192:12288].bitcast(BF16)
    cbuf = [arena[:, j * 2048:(j + 1) * 2048].bitcast(BF16) for j in range(4)]
    xh_f = sbuf("xh_f", (128, KT * LC), BF16)
    sq_t = sbuf("sq_t", (128, LC))
    ln_tmp = sbuf("ln_tmp", (128, LC))
    mkt = sbuf("mkt", (128, 8))
    res_c = arena[:, 0:KT * BLK]
    res_sl = sbuf("res_sl", (128, 2 * LC))
    xcin = sbuf("xcin", (128, 3 + LC + 1))
    cacc = sq_t[:, 0:LC]
    xt_t = sbuf("xt_t", (128, LC))
    ts_bf = sbuf("ts_bf", (128, LC), BF16)
    bc_sb = sbuf("bc_sb", (32, LC))
    bcflat = sbuf("bcflat", (1, 4 * LC))
    evac_t = sbuf("evac_t", (128, LC))
    dt_q = sbuf("dt_q", (128, 4 * LC))
    et_t = ln_tmp
    xt_sl = sbuf("xt_sl", (128, 2 * LC))
    u_q = sbuf("u_q", (128, 4 * LC))
    yac_q = sbuf("yac_q", (128, 4 * LC))
    SCW = LC + 8
    dA_t = sbuf("dA_t", (128, 3 * SCW))
    dBx_t = sbuf("dBx_t", (128, 2 * SCW))
    so_t = sbuf("so_t", (128, 2 * SCW))
    pr_t = sbuf("pr_t", (128, LC))
    opart = sbuf("opart", (128, 2 * LC))

    p_mm = ctx.enter_context(nc.psum_tensor("p_mm", [128, 1024], F32))
    p_bcB = ctx.enter_context(nc.psum_tensor("p_bcB", [128, 1024], F32))
    p_bcC = ctx.enter_context(nc.psum_tensor("p_bcC", [128, 1024], F32))
    p_f = ctx.enter_context(nc.psum_tensor("p_f", [128, 1024], F32))

    block = ctx.enter_context(nc.Block())

    class Sem:
        def __init__(self, name):
            self.h = ctx.enter_context(nc.semaphore(name))
            self.v = 0

    class DSem(Sem):
        def inc(self, inst):
            inst.then_inc(self.h, 16)
            self.v += 16
        def wait(self, e):
            e.wait_ge(self.h, self.v)

    sT = Sem("sT"); sV = Sem("sV"); sS = Sem("sS")
    d_small = DSem("d_small")
    d_smallg = DSem("d_smallg")
    d_cast = [DSem(f"d_cast{i}") for i in range(4)]
    d_cstore = [DSem(f"d_cstore{i}") for i in range(4)]
    d_strip = [DSem(f"d_strip{i}") for i in range(2)]
    d_wg = [DSem(f"d_wg{i}") for i in range(2)]
    d_wu = [DSem(f"d_wu{i}") for i in range(2)]
    d_wd = [DSem(f"d_wd{i}") for i in range(2)]
    d_xh2 = DSem("d_xh2")
    d_xt_st = DSem("d_xt_st")
    d_xt_ld = [DSem(f"d_xt_ld{i}") for i in range(2)]
    d_ts_st = DSem("d_ts_st")
    d_tsbf = DSem("d_tsbf")
    d_bcf = DSem("d_bcf")
    d_op_st = DSem("d_op_st")
    d_op_ld = [DSem(f"d_op_ld{i}") for i in range(2)]
    d_res = [DSem(f"d_res{i}") for i in range(2)]
    d_xmid_st = DSem("d_xmid_st")
    d_stres = DSem("d_stres")
    d_ldx = DSem("d_ldx")
    d_ag_st = DSem("d_ag_st")
    sO = DSem("sO")
    c_ag = [Sem(f"c_ag{i}") for i in range(2)]
    c_ts = [Sem(f"c_ts{i}") for i in range(NCH)]
    c_op = [Sem(f"c_op{i}") for i in range(NCH)]
    ALL_DSEMS = ([d_small, d_smallg, d_xh2, d_xt_st, d_ts_st, d_tsbf, d_bcf, d_op_st,
                  d_xmid_st, d_stres, d_ldx, d_ag_st, sO]
                 + d_cast + d_cstore + d_strip + d_wg + d_wu + d_wd
                 + d_xt_ld + d_op_ld + d_res)
    ALL_CSEMS = c_ag + c_ts + c_op

    def dma4(e, sem, out_ap, in_ap):
        for qq in range(4):
            sem.inc(e.dma_start(out=out_ap[qq * 32:(qq + 1) * 32], in_=in_ap[qq * 32:(qq + 1) * 32]))

    def barrier():
        def t(e): e.drain().then_inc(sT.h, 1)
        block.tensor(t); sT.v += 1
        def v(e): e.drain().then_inc(sV.h, 1)
        block.vector(v); sV.v += 1
        def s(e): e.drain().then_inc(sS.h, 1)
        block.scalar(s); sS.v += 1
        vals = [(sT, sT.v), (sV, sV.v), (sS, sS.v)] + \
               [(d, d.v) for d in ALL_DSEMS] + [(cs, cs.v) for cs in ALL_CSEMS]
        def waiter(e):
            for sm, vl in vals:
                e.wait_ge(sm.h, vl)
        for emit in (block.tensor, block.vector, block.scalar, block.gpsimd, block.sync):
            emit(waiter)

    def tap(name, src_dram):
        if name not in tap_out:
            return
        def t(e):
            sO.inc(e.dma_start(out=tap_out[name], in_=src_dram))
        block.gpsimd(t)

    def finish():
        ov = sO.v
        def fin(e, ov=ov):
            e.wait_ge(sO.h, ov)
        block.gpsimd(fin)
        return nc, ctx, tap_out

    def kt_rows(dram2d, i):
        return dram2d.rearrange("(kt p) t -> kt p t", p=128)[i]

    def _init_sync(e):
        d_small.inc(e.dma_start(out=A_s[:], in_=dr["A_sb"]))
        d_small.inc(e.dma_start(out=cw_s[:], in_=dr["cw_sb"]))
        d_small.inc(e.dma_start(out=cb_s[:], in_=dr["cb_sb"]))
        d_small.inc(e.dma_start(out=D_s[:], in_=dr["D_sb"]))
        d_small.inc(e.dma_start(out=dtb_s[:], in_=dr["dtb_sb"]))
        d_small.inc(e.dma_start(out=al_s[:], in_=dr["alpha"]))
        d_small.inc(e.dma_start(out=xres_h[:, :], in_=dr["x_own"]))
    block.sync(_init_sync)

    def _init_gps(e):
        d_smallg.inc(e.dma_start(out=xpT[:], in_=dr["w_xp"]))
        d_smallg.inc(e.dma_start(out=dtpT[:], in_=dr["w_dtp"]))
    block.gpsimd(_init_gps)

    def _init_vec(e):
        e.memset(ones[:], 1.0)
        e.memset(epsc[:], EPS)
        e.memset(mkt[:], 1.0)
        e.memset(onesr[:], 1.0)
        e.memset(state[:], 0.0)
        e.memset(dA_t[:], 0.0)
        e.memset(carry[:], 0.0).then_inc(sV.h, 1)
    block.vector(_init_vec); sV.v += 1

    CW = 4096
    CAST_JOBS = []
    for w0 in range(0, HT * DM, CW):
        CAST_JOBS.append((dr["w_g"][:, w0:w0 + CW], gu_bf[:, 2 * w0:2 * w0 + CW]))
        CAST_JOBS.append((dr["w_u"][:, w0:w0 + CW], gu_bf[:, 2 * w0 + CW:2 * w0 + 2 * CW]))
    for cls_in, cls_out in (("w_d", d_bf), ("w_inx", inx_bf), ("w_ing", ing_bf), ("w_out", outp_bf)):
        total = dr[cls_in].shape[-1]
        assert total % CW == 0
        for w0 in range(0, total, CW):
            CAST_JOBS.append((dr[cls_in][:, w0:w0 + CW], cls_out[:, w0:w0 + CW]))

    for j, (srca, dsta) in enumerate(CAST_JOBS):
        slot = j % 4
        def gld(e, srca=srca, slot=slot):
            d_cstore[slot].wait(e)
            d_cast[slot].inc(e.dma_start(out=cbuf[slot][:, 0:CW], in_=srca))
        block.gpsimd(gld)
        def wst(e, dsta=dsta, slot=slot):
            d_cast[slot].wait(e)
            d_cstore[slot].inc(e.dma_start(out=dsta, in_=cbuf[slot][:, 0:CW]))
        block.sync(wst)

    barrier()

    def layernorm(src_fn, dst_fn, n, ready):
        T0 = sT.v
        S0ln = sS.v
        slots = [sq_t[:, 0:n], xt_t[:, 0:n], evac_t[:, 0:n]]
        for kt in range(KT + 1):
            if kt < KT:
                def sq(e, kt=kt, ready=ready, T0=T0):
                    if kt == 0:
                        for w in ready:
                            w(e)
                    if kt >= 3:
                        e.wait_ge(sT.h, T0 + kt - 2)
                    e.activation(slots[kt % 3], src_fn(kt), AF.Square,
                                 bias=0.0, scale=1.0).then_inc(sS.h, 1)
                block.scalar(sq); sS.v += 1
            else:
                def sq(e):
                    e.activation(mkt[:, 0:8], mkt[:, 0:8], AF.Square, bias=0.0, scale=1.0).then_inc(sS.h, 1)
                block.scalar(sq); sS.v += 1
            if kt >= 1:
                ktm = kt - 1
                sv = S0ln + kt + 1
                def mm(e, ktm=ktm, sv=sv, ready=ready):
                    e.wait_ge(sS.h, sv)
                    if ktm == 0:
                        for w in ready:
                            w(e)
                    e.matmul(p_mm[0:1, :n], ones[:, 0:1], src_fn(ktm), start=(ktm == 0), stop=(ktm == KT - 1))
                    e.matmul(p_mm[0:1, 512:512 + n], ones[:, 0:1], slots[ktm % 3],
                             start=(ktm == 0), stop=(ktm == KT - 1)).then_inc(sT.h, 1)
                block.tensor(mm); sT.v += 1
        tv = sT.v
        def rowmath(e, tv=tv, n=n):
            e.wait_ge(sT.h, tv)
            m = rows[:, 0:n]; v = rows[:, LC:LC + n]
            e.tensor_scalar_mul(m, p_mm[0:1, :n], 1.0 / DM)
            e.tensor_tensor(v, m, m, ALU.mult)
            e.scalar_tensor_tensor(v, p_mm[0:1, 512:512 + n], 1.0 / DM, v, ALU.mult, ALU.subtract)
            e.tensor_scalar_add(v, v, EPS).then_inc(sV.h, 1)
        block.vector(rowmath); sV.v += 1
        vv = sV.v
        def rowsqrt(e, vv=vv, n=n):
            e.wait_ge(sV.h, vv)
            e.activation(rows[:, 2 * LC:2 * LC + n], rows[:, LC:LC + n], AF.Sqrt, bias=0.0, scale=1.0).then_inc(sS.h, 1)
        block.scalar(rowsqrt); sS.v += 1
        def mkrr(e):
            e.activation(mkt[:, 0:8], mkt[:, 0:8], AF.Square, bias=0.0, scale=1.0).then_inc(sS.h, 1)
        block.scalar(mkrr); sS.v += 1
        sv2 = sS.v
        def rowrec(e, sv2=sv2, n=n):
            e.wait_ge(sS.h, sv2)
            v = rows[:, LC:LC + n]
            r = rows[:, 2 * LC:2 * LC + n]
            nm = rows[:, 3 * LC:3 * LC + n]
            e.reciprocal(r, r)
            e.tensor_tensor(nm, r, r, ALU.mult)
            e.tensor_tensor(nm, nm, v, ALU.mult)
            e.tensor_scalar(nm, nm, -0.5, 1.5, ALU.mult, ALU.add)
            e.tensor_tensor(r, r, nm, ALU.mult)
            e.scalar_tensor_tensor(nm, rows[:, 0:n], -1.0, r, ALU.mult, ALU.mult).then_inc(sV.h, 1)
        block.vector(rowrec); sV.v += 1
        vv2 = sV.v
        scur = sS.v
        def bcast(e, vv2=vv2, n=n, scur=scur):
            e.wait_ge(sV.h, vv2)
            e.wait_ge(sS.h, scur)
            e.matmul(p_bcB[:, :n], onesr[:], rows[0:1, 2 * LC:2 * LC + n], start=True, stop=True)
            e.matmul(p_bcC[:, :n], onesr[:], rows[0:1, 3 * LC:3 * LC + n], start=True, stop=True).then_inc(sT.h, 1)
        block.tensor(bcast); sT.v += 1
        tv2 = sT.v
        def apply_(e, tv2=tv2, n=n):
            e.wait_ge(sT.h, tv2)
            for kt in range(KT):
                e.tensor_tensor(ln_tmp[:, :n], src_fn(kt), p_bcB[:, :n], ALU.mult)
                i_ = e.tensor_tensor(dst_fn(kt), ln_tmp[:, :n], p_bcC[:, :n], ALU.add)
                if kt == KT - 1:
                    i_.then_inc(sV.h, 1)
        block.vector(apply_); sV.v += 1

    def ffn(n, res_cols, is_ffn2):
        V_F0 = sV.v
        S_F0 = sS.v
        sil_cnt = {}
        gm_cnt = {}
        ac_end_prev = V_F0
        def z(e):
            d_stres.wait(e)
            e.memset(acc_f[:, : KT * n], 0.0).then_inc(sV.h, 1)
        block.vector(z); sV.v += 1
        for e8 in range(NE):
            for hh in range(HPE):
                ht = e8 * HPE + hh
                slot = (ht // 2) % 2
                sub = (ht % 2) * KT * 128
                pair = [p_f, p_mm][ht % 2]
                sgsl = [ln_tmp, evac_t][ht % 2]
                if ht % 2 == 0:
                    tcur = sT.v
                    def ld(e, ht=ht, slot=slot, tcur=tcur):
                        e.wait_ge(sT.h, tcur)
                        d_wg[slot].inc(e.dma_start(out=wgu_st[slot][:],
                                                   in_=gu_bf[:, ht * 4096:(ht + 2) * 4096]))
                    block.gpsimd(ld)
                if hh == 0 and e8 > 0:
                    vv_need = ac_end_prev
                else:
                    vv_need = gm_cnt[ht - 2] if ht >= 2 else V_F0
                sv_need = sil_cnt[ht - 2] if ht >= 2 else S_F0
                def gu(e, slot=slot, sub=sub, pair=pair, vv_need=vv_need, sv_need=sv_need, n=n):
                    d_wg[slot].wait(e)
                    e.wait_ge(sV.h, vv_need)
                    e.wait_ge(sS.h, sv_need)
                    for k in range(KT):
                        e.matmul(pair[:, 0:n], wgu_st[slot][:, sub + k * 128:sub + (k + 1) * 128],
                                 xh_f[:, k * LC:k * LC + n], start=(k == 0), stop=(k == KT - 1))
                    for k in range(KT):
                        i_ = e.matmul(pair[:, 512:512 + n], wgu_st[slot][:, 4096 + sub + k * 128:4096 + sub + (k + 1) * 128],
                                      xh_f[:, k * LC:k * LC + n], start=(k == 0), stop=(k == KT - 1))
                        if k == KT - 1:
                            i_.then_inc(sT.h, 1)
                block.tensor(gu); sT.v += 1
                tv = sT.v
                vprot = gm_cnt[ht - 2] if ht >= 2 else V_F0
                def sil(e, tv=tv, n=n, vprot=vprot, pair=pair, sgsl=sgsl):
                    e.wait_ge(sT.h, tv)
                    e.wait_ge(sV.h, vprot)
                    e.activation(sgsl[:, :n], pair[:, 0:n], AF.Silu, bias=0.0, scale=1.0).then_inc(sS.h, 1)
                block.scalar(sil); sS.v += 1
                sil_cnt[ht] = sS.v
                sv = sil_cnt[ht] + 1
                def gm(e, hh=hh, sv=sv, n=n, pair=pair, sgsl=sgsl):
                    e.wait_ge(sS.h, sv)
                    e.tensor_tensor(gated[:, hh * n:(hh + 1) * n], sgsl[:, :n], pair[:, 512:512 + n], ALU.mult).then_inc(sV.h, 1)
                block.vector(gm); sV.v += 1
                gm_cnt[ht] = sV.v
            def mkrf(e):
                e.activation(mkt[:, 0:8], mkt[:, 0:8], AF.Square, bias=0.0, scale=1.0).then_inc(sS.h, 1)
            block.scalar(mkrf); sS.v += 1
            for dmt in range(KT):
                slot = (dmt // 4) % 2
                sub = (dmt % 4) * HPE * 128
                if dmt % 4 == 0:
                    tcur = sT.v
                    def ldd(e, e8=e8, dmt=dmt, slot=slot, tcur=tcur):
                        e.wait_ge(sT.h, tcur)
                        base = (e8 * KT + dmt) * HPE * 128
                        d_wd[slot].inc(e.dma_start(out=wd_st[slot][:],
                                                   in_=d_bf[:, base:base + 4 * HPE * 128]))
                    block.gpsimd(ldd)
                vv_need = sV.v
                sv_need = sS.v
                def dn(e, slot=slot, sub=sub, vv_need=vv_need, sv_need=sv_need, n=n):
                    d_wd[slot].wait(e)
                    e.wait_ge(sV.h, vv_need)
                    e.wait_ge(sS.h, sv_need)
                    for h in range(HPE):
                        i_ = e.matmul(p_f[:, 0:n], wd_st[slot][:, sub + h * 128:sub + (h + 1) * 128],
                                      gated[:, h * n:(h + 1) * n], start=(h == 0), stop=(h == HPE - 1))
                        if h == HPE - 1:
                            i_.then_inc(sT.h, 1)
                block.tensor(dn); sT.v += 1
                tv = sT.v
                def ac(e, dmt=dmt, tv=tv, n=n):
                    e.wait_ge(sT.h, tv)
                    e.tensor_tensor(acc_f[:, dmt * n:(dmt + 1) * n], p_f[:, 0:n],
                                    acc_f[:, dmt * n:(dmt + 1) * n], ALU.add).then_inc(sV.h, 1)
                block.vector(ac); sV.v += 1
            ac_end_prev = sV.v
        for dmt in range(KT):
            slot = dmt % 2
            vcur = sV.v
            def ldr(e, dmt=dmt, slot=slot, res_cols=res_cols, n=n, vcur=vcur):
                e.wait_ge(sV.h, vcur)
                d_stres.wait(e)
                d_xmid_st.wait(e)
                d_res[slot].inc(e.dma_start(
                    out=res_sl[:, slot * LC:slot * LC + n],
                    in_=kt_rows(xres_h, dmt)[:, res_cols:res_cols + n]))
            block.gpsimd(ldr)
            def upd(e, dmt=dmt, slot=slot, n=n):
                d_res[slot].wait(e)
                e.scalar_tensor_tensor(res_sl[:, slot * LC:slot * LC + n],
                                       acc_f[:, dmt * n:(dmt + 1) * n], 0.5,
                                       res_sl[:, slot * LC:slot * LC + n], ALU.mult, ALU.add).then_inc(sV.h, 1)
            block.vector(upd); sV.v += 1
            vv = sV.v
            if is_ffn2:
                def st(e, dmt=dmt, slot=slot, vv=vv, res_cols=res_cols, n=n):
                    e.wait_ge(sV.h, vv)
                    sO.inc(e.dma_start(out=kt_rows(out, dmt)[:, res_cols:res_cols + n],
                                       in_=res_sl[:, slot * LC:slot * LC + n]))
                block.gpsimd(st)
            else:
                def st(e, dmt=dmt, slot=slot, vv=vv, res_cols=res_cols, n=n):
                    e.wait_ge(sV.h, vv)
                    d_xmid_st.inc(e.dma_start(out=kt_rows(xres_h, dmt)[:, res_cols:res_cols + n],
                                              in_=res_sl[:, slot * LC:slot * LC + n]))
                block.gpsimd(st)

    for h in range(2):
        base = h * LC
        def ldx(e, base=base):
            d_xmid_st.wait(e)
            d_ldx.inc(e.dma_start(
                out=acc_f[:, :KT * LC].rearrange("p (kt t) -> p kt t", t=LC),
                in_=xres_h.rearrange("(kt p) t -> p kt t", p=128)[:, :, base:base + LC]))
        block.gpsimd(ldx)
        layernorm(lambda kt: acc_f[:, kt * LC:(kt + 1) * LC],
                  lambda kt: xh_f[:, kt * LC:(kt + 1) * LC], LC,
                  [lambda e: d_ldx.wait(e)])
        ffn(LC, base, False)
        barrier()
    tap("tap_xmid", xres_h)
    if stage == "ffn1":
        return finish()

    for h in range(2):
        base = h * LC
        def ldx2(e, base=base):
            d_xmid_st.wait(e)
            d_ldx.inc(e.dma_start(
                out=acc_f[:, :KT * LC].rearrange("p (kt t) -> p kt t", t=LC),
                in_=xres_h.rearrange("(kt p) t -> p kt t", p=128)[:, :, base:base + LC]))
        block.gpsimd(ldx2)
        layernorm(lambda kt: acc_f[:, kt * LC:(kt + 1) * LC],
                  lambda kt: xh_f[:, kt * LC:(kt + 1) * LC], LC,
                  [lambda e: d_ldx.wait(e)])
        vv = sV.v
        def stag(e, h=h, vv=vv):
            e.wait_ge(sV.h, vv)
            d_ag_st.inc(e.dma_start(out=ag_in[h][0].rearrange("(kt p) t -> p kt t", p=128),
                                    in_=xh_f[:].rearrange("p (kt t) -> p kt t", t=LC)))
        block.gpsimd(stag)
        def agc(e, h=h):
            d_ag_st.wait(e)
            e.collective_compute("AllGather", ALU.bypass, replica_groups=PAIRS,
                                 ins=[ag_in[h]], outs=[ag_out[h]]).then_inc(c_ag[h].h, 1)
        block.gpsimd(agc); c_ag[h].v += 1
        barrier()

    for c in range(NCH):
        hf = c // 2
        coff = (c % 2) * BLK
        tcur0 = sT.v
        def ldxh2(e, hf=hf, coff=coff, tcur0=tcur0):
            e.wait_ge(c_ag[hf].h, c_ag[hf].v)
            e.wait_ge(sT.h, tcur0)
            for slot in range(2):
                d_xh2.inc(e.dma_start(
                    out=xh2.rearrange("p (kt t) -> p kt t", t=LC)[:, :, slot * BLK:(slot + 1) * BLK],
                    in_=ag_out[hf][slot].rearrange("(kt p) t -> p kt t", p=128)[:, :, coff:coff + BLK]))
        block.gpsimd(ldxh2)

        S_A0 = sS.v
        for d in range(DTl):
            slot = d % 2
            wsl = (d // 2) % 2
            wsub = (d % 2) * KT * 128
            if d % 2 == 0:
                tcur = sT.v
                def lds(e, d=d, wsl=wsl, tcur=tcur):
                    e.wait_ge(sT.h, tcur)
                    d_strip[wsl].inc(e.dma_start(out=wstrip[wsl][:], in_=inx_bf[:, d * DM:(d + 2) * DM]))
                block.sync(lds)
            scur = sS.v
            def mmx(e, slot=slot, wsl=wsl, wsub=wsub, scur=scur):
                d_strip[wsl].wait(e)
                d_xh2.wait(e)
                e.wait_ge(sS.h, scur)
                for k in range(KT):
                    i_ = e.matmul(p_mm[:, slot * 512:slot * 512 + LC], wstrip[wsl][:, wsub + k * 128:wsub + (k + 1) * 128],
                                  xh2[:, k * LC:(k + 1) * LC], start=(k == 0), stop=(k == KT - 1))
                    if k == KT - 1:
                        i_.then_inc(sT.h, 1)
            block.tensor(mmx); sT.v += 1
            tv = sT.v
            vcur = sV.v
            def evx(e, slot=slot, tv=tv, vcur=vcur):
                e.wait_ge(sT.h, tv)
                e.wait_ge(sV.h, vcur)
                e.copy(xcin[:, 3:3 + LC], p_mm[:, slot * 512:slot * 512 + LC]).then_inc(sS.h, 1)
            block.scalar(evx); sS.v += 1
            def mkre(e):
                e.activation(mkt[:, 0:8], mkt[:, 0:8], AF.Square, bias=0.0, scale=1.0).then_inc(sS.h, 1)
            block.scalar(mkre); sS.v += 1
            sv = sS.v
            def conv(e, d=d, sv=sv):
                e.wait_ge(sS.h, sv)
                e.tensor_copy(xcin[:, 0:3], carry[:, d * 4:d * 4 + 3])
                e.tensor_copy(carry[:, d * 4:d * 4 + 3], xcin[:, LC:LC + 3])
                e.tensor_scalar_mul(cacc, xcin[:, 0:LC], cw_s[:, d * 4:d * 4 + 1])
                for k in range(1, 4):
                    i_ = e.scalar_tensor_tensor(cacc, xcin[:, k:k + LC], cw_s[:, d * 4 + k:d * 4 + k + 1],
                                                cacc, ALU.mult, ALU.add)
                    if k == 3:
                        i_.then_inc(sV.h, 1)
            block.vector(conv); sV.v += 1
            vv = sV.v
            def silu_x(e, d=d, vv=vv):
                e.wait_ge(sV.h, vv)
                d_xt_st.wait(e)
                e.activation(xt_t[:], cacc, AF.Silu, bias=cb_s[:, d:d + 1], scale=1.0)
                e.copy(xt_bf[:, d * LC:(d + 1) * LC], xt_t[:]).then_inc(sS.h, 1)
            block.scalar(silu_x); sS.v += 1
            sv2 = S_A0 + (3 * d + 4 if d < DTl - 1 else 3 * DTl + 1)
            def stxt(e, d=d, c=c, sv2=sv2):
                e.wait_ge(sS.h, sv2)
                d_xt_st.inc(e.dma_start(out=xt_hbm[d][:, c * LC:(c + 1) * LC], in_=xt_t[:]))
            block.gpsimd(stxt)
        def mkra(e):
            e.activation(mkt[:, 0:8], mkt[:, 0:8], AF.Square, bias=0.0, scale=1.0).then_inc(sS.h, 1)
        block.scalar(mkra); sS.v += 1
        assert sS.v == S_A0 + 3 * DTl + 1
        s_xtbf = sS.v

        vcur = sV.v
        def xproj(e, s_xtbf=s_xtbf, vcur=vcur):
            e.wait_ge(sS.h, s_xtbf)
            e.wait_ge(sV.h, vcur)
            for k in range(DTl):
                e.matmul(p_mm[:, 0:LC], xpT[:, k * 160:k * 160 + 128],
                         xt_bf[:, k * LC:(k + 1) * LC], start=(k == 0), stop=(k == DTl - 1))
            for k in range(DTl):
                i_ = e.matmul(p_bcB[0:32, 0:LC], xpT[:, k * 160 + 128:(k + 1) * 160],
                              xt_bf[:, k * LC:(k + 1) * LC], start=(k == 0), stop=(k == DTl - 1))
                if k == DTl - 1:
                    i_.then_inc(sT.h, 1)
        block.tensor(xproj); sT.v += 1
        tv = sT.v
        def evts(e, tv=tv):
            e.wait_ge(sT.h, tv)
            e.copy(evac_t[:], p_mm[:, 0:LC])
            e.copy(bc_sb[:], p_bcB[0:32, 0:LC]).then_inc(sS.h, 1)
        block.scalar(evts); sS.v += 1
        def mkrt(e):
            e.activation(mkt[:, 0:8], mkt[:, 0:8], AF.Square, bias=0.0, scale=1.0).then_inc(sS.h, 1)
        block.scalar(mkrt); sS.v += 1
        sv = sS.v
        def sts(e, c=c, sv=sv):
            e.wait_ge(sS.h, sv)
            d_ts_st.inc(e.dma_start(out=ts_in[c][0:128, :], in_=evac_t[:]))
            d_ts_st.inc(e.dma_start(out=ts_in[c][128:160, :], in_=bc_sb[:]))
        block.gpsimd(sts)
        def arc(e, c=c):
            d_ts_st.wait(e)
            e.collective_compute("AllReduce", ALU.add, replica_groups=PAIRS,
                                 ins=[ts_in[c]], outs=[ts_out[c]]).then_inc(c_ts[c].h, 1)
        block.gpsimd(arc); c_ts[c].v += 1
        def ldts(e, c=c):
            e.wait_ge(c_ts[c].h, c_ts[c].v)
            d_tsbf.inc(e.dma_start(out=ts_bf[:], in_=ts_out[c][0:128, :]))
        block.gpsimd(ldts)

        if stage == "dbg3" and c == 1:
            def snap1(e):
                sO.inc(e.dma_start(out=state_snap[1], in_=state[:]))
            block.gpsimd(snap1)
        QD = 4
        v_prep_ok = sV.v
        for q in range(DTl // QD):
            S_prep0 = sS.v
            for dq in range(QD):
                d = q * QD + dq
                dsl = d % 2
                scur = sS.v
                def dtmm(e, d=d, scur=scur):
                    d_tsbf.wait(e)
                    e.wait_ge(sS.h, scur)
                    e.matmul(p_mm[:, 512:512 + LC], dtpT[:, d * 128:(d + 1) * 128], ts_bf[:],
                             start=True, stop=True).then_inc(sT.h, 1)
                block.tensor(dtmm); sT.v += 1
                tv = sT.v
                def softp(e, dq=dq, d=d, tv=tv, vfix=v_prep_ok):
                    e.wait_ge(sT.h, tv)
                    e.wait_ge(sV.h, vfix)
                    e.activation(et_t[:, 0:LC], p_mm[:, 512:512 + LC], AF.Exp, bias=dtb_s[:, d:d + 1], scale=1.0)
                    e.activation(dt_q[:, dq * LC:(dq + 1) * LC], et_t[:, 0:LC], AF.Ln, bias=1.0, scale=1.0).then_inc(sS.h, 1)
                block.scalar(softp); sS.v += 1
                assert sS.v == S_prep0 + dq + 1
                vcur = sV.v
                def ldxt(e, d=d, c=c, dsl=dsl, vcur=vcur):
                    e.wait_ge(sV.h, vcur)
                    d_xt_st.wait(e)
                    d_xt_ld[dsl].inc(e.dma_start(
                        out=xt_sl[:, dsl * LC:(dsl + 1) * LC],
                        in_=xt_hbm[d][:, c * LC:(c + 1) * LC]))
                block.gpsimd(ldxt)
                s_dt = S_prep0 + dq + 2
                def uprep(e, dq=dq, d=d, dsl=dsl, s_dt=s_dt):
                    e.wait_ge(sS.h, s_dt)
                    d_xt_ld[dsl].wait(e)
                    e.tensor_tensor(u_q[:, dq * LC:(dq + 1) * LC], dt_q[:, dq * LC:(dq + 1) * LC],
                                    xt_sl[:, dsl * LC:(dsl + 1) * LC], ALU.mult)
                    e.tensor_scalar_mul(yac_q[:, dq * LC:(dq + 1) * LC],
                                        xt_sl[:, dsl * LC:(dsl + 1) * LC], D_s[:, d:d + 1]).then_inc(sV.h, 1)
                block.vector(uprep); sV.v += 1
            def mkr2(e):
                e.activation(mkt[:, 0:8], mkt[:, 0:8], AF.Square, bias=0.0, scale=1.0).then_inc(sS.h, 1)
            block.scalar(mkr2); sS.v += 1
            assert sS.v == S_prep0 + QD + 1
            V0 = sV.v
            S0 = sS.v
            T0 = sT.v
            NREF = 8
            for r in range(NREF):
                def refill(e, c=c, r=r, T0=T0, tcur_r=sT.v):
                    e.wait_ge(c_ts[c].h, c_ts[c].v)
                    e.wait_ge(sT.h, T0 + 4 * r if r >= 1 else tcur_r)
                    d_bcf.inc(e.dma_start(
                        out=bcflat[:, 0:2 * LC],
                        in_=ts_out[c][128 + 2 * r:130 + 2 * r, :].rearrange("a b -> (a b)").unsqueeze(0)))
                    d_bcf.inc(e.dma_start(
                        out=bcflat[:, 2 * LC:4 * LC],
                        in_=ts_out[c][144 + 2 * r:146 + 2 * r, :].rearrange("a b -> (a b)").unsqueeze(0)))
                block.gpsimd(refill)
                for i2 in range(2):
                    s_ = 2 * r + i2
                    def bca(e, s_=s_, i2=i2, V0=V0):
                        d_bcf.wait(e)
                        if s_ >= 2:
                            e.wait_ge(sV.h, V0 + QD * (s_ - 1))
                        else:
                            e.wait_ge(sV.h, V0)
                        e.matmul(p_bcB[:, (s_ % 2) * 512:(s_ % 2) * 512 + LC], onesr[:],
                                 bcflat[0:1, i2 * LC:(i2 + 1) * LC], start=True, stop=True)
                        e.matmul(p_bcC[:, (s_ % 2) * 512:(s_ % 2) * 512 + LC], onesr[:],
                                 bcflat[0:1, 2 * LC + i2 * LC:2 * LC + (i2 + 1) * LC], start=True, stop=True).then_inc(sT.h, 2)
                    block.tensor(bca); sT.v += 2
            def dA_all(e, q=q, V0=V0):
                for s_ in range(S):
                    for dq in range(QD):
                        d = q * QD + dq
                        j = s_ * QD + dq
                        e.wait_ge(sV.h, V0 + j - 2 if j >= 3 else V0)
                        e.activation(dA_t[:, (j % 3) * SCW + 1:(j % 3) * SCW + 1 + LC],
                                     dt_q[:, dq * LC:(dq + 1) * LC], AF.Exp,
                                     bias=0.0, scale=A_s[:, d * S + s_:d * S + s_ + 1]).then_inc(sS.h, 1)
                e.activation(mkt[:, 0:8], mkt[:, 0:8], AF.Square, bias=0.0, scale=1.0).then_inc(sS.h, 1)
            block.scalar(dA_all); sS.v += S * QD + 1
            def scan_all(e, q=q, c=c, V0=V0, S0=S0, T0=T0):
                pend = None
                for s_ in range(S):
                    e.wait_ge(sT.h, T0 + 2 * min(s_ + 2, S))
                    for dq in range(QD):
                        d = q * QD + dq
                        j = s_ * QD + dq
                        sl = (j % 2) * SCW
                        bsl = (s_ % 2) * 512
                        rd_off = (c % 2) * DTl * S
                        wr_off = ((c + 1) % 2) * DTl * S
                        col = d * S + s_
                        e.wait_ge(sS.h, S0 + min(j + 2, S * QD + 1))
                        e.tensor_copy(dBx_t[:, sl:sl + 1], state[:, rd_off + col:rd_off + col + 1])
                        e.tensor_tensor(dBx_t[:, sl + 1:sl + 1 + LC], u_q[:, dq * LC:(dq + 1) * LC],
                                        p_bcB[:, bsl:bsl + LC], ALU.mult)
                        e.tensor_tensor_scan(so_t[:, sl:sl + 1 + LC], dA_t[:, (j % 3) * SCW:(j % 3) * SCW + 1 + LC],
                                             dBx_t[:, sl:sl + 1 + LC], 0.0, ALU.mult, ALU.add)
                        if pend is not None:
                            e.tensor_copy(state[:, wr_off + pend[0]:wr_off + pend[0] + 1],
                                          so_t[:, pend[1] + LC:pend[1] + LC + 1])
                        e.tensor_tensor(pr_t[:], so_t[:, sl + 1:sl + 1 + LC], p_bcC[:, bsl:bsl + LC], ALU.mult)
                        e.tensor_tensor(yac_q[:, dq * LC:(dq + 1) * LC], pr_t[:],
                                        yac_q[:, dq * LC:(dq + 1) * LC], ALU.add).then_inc(sV.h, 1)
                        pend = (col, sl)
                if s_ == S - 1:
                    e.tensor_copy(state[:, wr_off + pend[0]:wr_off + pend[0] + 1],
                                  so_t[:, pend[1] + LC:pend[1] + LC + 1])
            block.vector(scan_all); sV.v += S * QD
            v_scan = sV.v
            G_q0 = sS.v
            yfm_cnt_q = []
            for dq in range(QD):
                d = q * QD + dq
                dsl = d % 2
                wsl = (d // 2) % 2
                wsub = (d % 2) * KT * 128
                sgsl = [ln_tmp, evac_t][dq % 2]
                if dq % 2 == 0:
                    tcur = sT.v
                    def ldg(e, d=d, wsl=wsl, tcur=tcur):
                        e.wait_ge(sT.h, tcur)
                        d_strip[wsl].inc(e.dma_start(out=wstrip[wsl][:], in_=ing_bf[:, d * DM:(d + 2) * DM]))
                    block.sync(ldg)
                scur2 = sS.v
                def gmm(e, dsl=dsl, wsl=wsl, wsub=wsub, scur2=scur2, v_scan=v_scan):
                    d_strip[wsl].wait(e)
                    d_xh2.wait(e)
                    e.wait_ge(sS.h, scur2)
                    if dsl % 2 == 1:
                        e.wait_ge(sV.h, v_scan)
                    for k in range(KT):
                        i_ = e.matmul(p_mm[:, (dsl % 2) * 512:(dsl % 2) * 512 + LC],
                                      wstrip[wsl][:, wsub + k * 128:wsub + (k + 1) * 128],
                                      xh2[:, k * LC:(k + 1) * LC], start=(k == 0), stop=(k == KT - 1))
                        if k == KT - 1:
                            i_.then_inc(sT.h, 1)
                block.tensor(gmm); sT.v += 1
                tv = sT.v
                vfix = yfm_cnt_q[dq - 2] if dq >= 2 else v_prep_ok
                def sgx(e, dsl=dsl, tv=tv, vfix=vfix, sgsl=sgsl):
                    e.wait_ge(sT.h, tv)
                    e.wait_ge(sV.h, vfix)
                    e.activation(sgsl[:, :LC], p_mm[:, (dsl % 2) * 512:(dsl % 2) * 512 + LC],
                                 AF.Silu, bias=0.0, scale=1.0).then_inc(sS.h, 1)
                block.scalar(sgx); sS.v += 1
                assert sS.v == G_q0 + dq + 1
                sv = G_q0 + dq + 2
                def yfm(e, dq=dq, d=d, sv=sv, v_scan=v_scan, sgsl=sgsl):
                    e.wait_ge(sS.h, sv)
                    e.wait_ge(sV.h, v_scan)
                    e.tensor_tensor(yf_t[:, d * LC:(d + 1) * LC], yac_q[:, dq * LC:(dq + 1) * LC],
                                    sgsl[:, :LC], ALU.mult).then_inc(sV.h, 1)
                block.vector(yfm); sV.v += 1
                yfm_cnt_q.append(sV.v)
            def mkrg(e):
                e.activation(mkt[:, 0:8], mkt[:, 0:8], AF.Square, bias=0.0, scale=1.0).then_inc(sS.h, 1)
            block.scalar(mkrg); sS.v += 1
            assert sS.v == G_q0 + QD + 1
            v_prep_ok = sV.v
        v_yf = sV.v
        if stage == "dbg1" and c == 1 and "tap_yf" in tap_out:
            def tapyf(e, v_yf=v_yf):
                e.wait_ge(sV.h, v_yf)
                sO.inc(e.dma_start(out=tap_out["tap_yf"], in_=yf_t[:]))
                sO.inc(e.dma_start(out=tap_out["tap_ts"], in_=ts_out[1]))
            block.gpsimd(tapyf)

        S_C0 = sS.v
        op_base = d_op_st.v
        for i in range(KT):
            slot = i % 2
            wsl = (i // 2) % 2
            wsub = (i % 2) * KT * 128
            if i % 2 == 0:
                tcur = sT.v
                def ldo(e, i=i, wsl=wsl, tcur=tcur):
                    e.wait_ge(sT.h, tcur)
                    d_strip[wsl].inc(e.dma_start(out=wstrip[wsl][:], in_=outp_bf[:, i * DH:(i + 2) * DH]))
                block.sync(ldo)
            scur = sS.v
            def omm(e, slot=slot, wsl=wsl, wsub=wsub, v_yf=v_yf, scur=scur):
                d_strip[wsl].wait(e)
                e.wait_ge(sV.h, v_yf)
                e.wait_ge(sS.h, scur)
                for k in range(DTl):
                    i_ = e.matmul(p_mm[:, slot * 512:slot * 512 + LC], wstrip[wsl][:, wsub + k * 128:wsub + (k + 1) * 128],
                                  yf_t[:, k * LC:(k + 1) * LC], start=(k == 0), stop=(k == DTl - 1))
                    if k == DTl - 1:
                        i_.then_inc(sT.h, 1)
            block.tensor(omm); sT.v += 1
            tv = sT.v
            opw = op_base + 16 * (i - 1) if i >= 2 else op_base
            def evo(e, slot=slot, tv=tv, opw=opw):
                e.wait_ge(sT.h, tv)
                e.wait_ge(d_op_st.h, opw)
                e.copy(opart[:, slot * LC:(slot + 1) * LC], p_mm[:, slot * 512:slot * 512 + LC]).then_inc(sS.h, 1)
            block.scalar(evo); sS.v += 1
            assert sS.v == S_C0 + i + 1
            sv = S_C0 + i + 2
            def sto(e, i=i, c=c, slot=slot, sv=sv):
                e.wait_ge(sS.h, sv)
                d_op_st.inc(e.dma_start(out=op_in[c].rearrange("(kt p) t -> kt p t", p=128)[i],
                                        in_=opart[:, slot * LC:(slot + 1) * LC]))
            block.gpsimd(sto)
        def mkro(e):
            e.activation(mkt[:, 0:8], mkt[:, 0:8], AF.Square, bias=0.0, scale=1.0).then_inc(sS.h, 1)
        block.scalar(mkro); sS.v += 1
        assert sS.v == S_C0 + KT + 1
        def aro(e, c=c):
            d_op_st.wait(e)
            e.collective_compute("AllReduce", ALU.add, replica_groups=PAIRS,
                                 ins=[op_in[c]], outs=[op_out[c]]).then_inc(c_op[c].h, 1)
        block.gpsimd(aro); c_op[c].v += 1

        owncol = c * BLK
        for i in range(KT):
            slot = i % 2
            vcur = sV.v
            def ldop(e, i=i, c=c, slot=slot, vcur=vcur, owncol=owncol):
                e.wait_ge(c_op[c].h, c_op[c].v)
                e.wait_ge(sV.h, vcur)
                d_stres.wait(e)
                d_op_ld[slot].inc(e.dma_start(
                    out=opart[:, slot * LC:(slot + 1) * LC],
                    in_=op_out[c].rearrange("(kt p) t -> kt p t", p=128)[i]))
                d_op_ld[slot].inc(e.dma_start(
                    out=res_sl[:, slot * LC:slot * LC + BLK],
                    in_=kt_rows(xres_h, i)[:, owncol:owncol + BLK]))
            block.gpsimd(ldop)
            def resx(e, i=i, slot=slot):
                d_op_ld[slot].wait(e)
                e.scalar_tensor_tensor(res_sl[:, slot * LC:slot * LC + BLK],
                                       opart[:, slot * LC:slot * LC + BLK], al_s[:, 0:1],
                                       res_sl[:, slot * LC:slot * LC + BLK], ALU.mult, ALU.add)
                e.scalar_tensor_tensor(res_c[:, i * BLK:(i + 1) * BLK],
                                       opart[:, slot * LC + BLK:slot * LC + 2 * BLK], al_s[:, 1:2],
                                       res_sl[:, slot * LC:slot * LC + BLK], ALU.mult, ALU.add).then_inc(sV.h, 1)
            block.vector(resx); sV.v += 1
            vv = sV.v
            def stres(e, i=i, vv=vv, owncol=owncol):
                e.wait_ge(sV.h, vv)
                d_stres.inc(e.dma_start(out=kt_rows(xres_h, i)[:, owncol:owncol + BLK],
                                        in_=res_c[:, i * BLK:(i + 1) * BLK]))
            block.gpsimd(stres)
        v_res = sV.v
        layernorm(lambda kt: res_c[:, kt * BLK:(kt + 1) * BLK],
                  lambda kt: xh_f[:, kt * LC + (c % 2) * BLK:kt * LC + (c % 2) * BLK + BLK],
                  BLK, [lambda e: e.wait_ge(sV.h, v_res)])

        if c % 2 == 1 and stage == "full":
            g = c // 2
            ffn(LC, g * LC, True)
        if stage == "dbg1" and c == 1 and "tap_op" in tap_out:
            def tapop(e):
                e.wait_ge(c_op[1].h, c_op[1].v)
                sO.inc(e.dma_start(out=tap_out["tap_op"], in_=op_out[1]))
            block.gpsimd(tapop)
        barrier()
        if stage == "dbg3" and c == 0:
            def snap0(e):
                sO.inc(e.dma_start(out=state_snap[0], in_=state[:]))
            block.gpsimd(snap0)
            barrier()
        if stage == "dbg3" and c == 1:
            tap("tap_snap", state_snap.rearrange("a p n -> (a p) n"))
            tap("tap_xafter", xres_h)
            return finish()
        if stage == "dbg1" and c == 1:
            tap("tap_xafter", xres_h)
            return finish()
        if stage == "mamba1" and c == 0:
            tap("tap_xafter", xres_h)
            return finish()
    tap("tap_xafter", xres_h)
    tap("tap_xt", xt_hbm.rearrange("d p l -> (d p) l"))
    tap("tap_ts", ts_out[1])
    tap("tap_op", op_out[1])
    tap("tap_ag", ag_out[0].rearrange("s d t -> (s d) t"))
    if stage == "mamba":
        return finish()

    return finish()


def prep_inputs(inputs):
    x = np.ascontiguousarray(np.asarray(inputs["L0_attn_hidden_states"], np.float32))
    w_in = np.asarray(inputs["L0_attn_in_proj_w"], np.float32)
    conv_w = np.asarray(inputs["L0_attn_conv_w"], np.float32)
    conv_b = np.asarray(inputs["L0_attn_conv_b"], np.float32)
    x_proj = np.asarray(inputs["L0_attn_x_proj_w"], np.float32)
    dt_w = np.asarray(inputs["L0_attn_dt_proj_w"], np.float32)
    dt_b = np.asarray(inputs["L0_attn_dt_proj_b"], np.float32)
    A = np.asarray(inputs["L0_attn_A"], np.float32)
    Dv = np.asarray(inputs["L0_attn_D"], np.float32)
    w_out = np.asarray(inputs["L0_attn_out_proj_w"], np.float32)
    wg = np.asarray(inputs["L0_ffn_w_gate"], np.float32)
    wu = np.asarray(inputs["L0_ffn_w_up"], np.float32)
    wd = np.asarray(inputs["L0_ffn_w_down"], np.float32)

    def pmaj(W, nt):
        K = W.shape[1]
        return np.ascontiguousarray(
            W.reshape(nt, 128, K // 128, 128).transpose(3, 0, 2, 1).reshape(128, nt * K))
    wg_s = pmaj(wg, HT)
    wu_s = pmaj(wu, HT)
    wd_s = np.ascontiguousarray(
        wd.reshape(KT, 128, NE, HPE, 128).transpose(4, 2, 0, 3, 1).reshape(128, NE * KT * HPE * 128))

    in_maps = []
    for core in range(8):
        p = core // 2
        q = core & 1
        blocks = [q, q + 2, q + 4, q + 6]
        xb = x[p].reshape(8, BLK, DM)
        x_ownT = np.ascontiguousarray(
            np.concatenate([xb[b] for b in blocks], axis=0).T)
        xrows = slice(q * DH, (q + 1) * DH)
        grows = slice(4096 + q * DH, 4096 + (q + 1) * DH)
        w_inx = pmaj(w_in[xrows], DTl)
        w_ing = pmaj(w_in[grows], DTl)
        w_outs = pmaj(w_out[:, q * DH:(q + 1) * DH], KT)
        w_xp = np.ascontiguousarray(
            x_proj[:, q * DH:(q + 1) * DH].T.reshape(KT, 128, 160).transpose(1, 0, 2).reshape(128, KT * 160))
        w_dtp = np.ascontiguousarray(dt_w[q * DH:(q + 1) * DH, :].T)
        Ah = A[q * DH:(q + 1) * DH]
        A_sb = np.ascontiguousarray(Ah.reshape(DTl, 128, S).transpose(1, 0, 2).reshape(128, DTl * S))
        cwh = conv_w[q * DH:(q + 1) * DH]
        cw_sb = np.ascontiguousarray(cwh.reshape(DTl, 128, 4).transpose(1, 0, 2).reshape(128, DTl * 4))
        cb_sb = np.ascontiguousarray(conv_b[q * DH:(q + 1) * DH].reshape(DTl, 128).T)
        D_sb = np.ascontiguousarray(Dv[q * DH:(q + 1) * DH].reshape(DTl, 128).T)
        dtb_sb = np.ascontiguousarray(dt_b[q * DH:(q + 1) * DH].reshape(DTl, 128).T)
        alpha = np.zeros((128, 2), np.float32)
        alpha[:, 0] = 1.0 if q == 0 else 0.0
        alpha[:, 1] = 0.0 if q == 0 else 1.0
        in_maps.append({
            "x_own": x_ownT, "w_inx": w_inx, "w_ing": w_ing, "w_out": w_outs,
            "w_xp": w_xp, "w_dtp": w_dtp, "w_g": wg_s, "w_u": wu_s, "w_d": wd_s,
            "A_sb": A_sb, "cw_sb": cw_sb, "cb_sb": cb_sb, "D_sb": D_sb,
            "dtb_sb": dtb_sb, "alpha": alpha,
        })
    return in_maps


def own_tokens_to_full(results, key="out"):
    full = np.zeros((4, 2048, DM), np.float32)
    for core in range(8):
        p = core // 2
        q = core & 1
        blocks = [q, q + 2, q + 4, q + 6]
        o = np.asarray(results[core][key]).T.reshape(4, BLK, DM)
        for j, b in enumerate(blocks):
            full[p, b * BLK:(b + 1) * BLK, :] = o[j]
    return full


_CACHE = {}


def kernel(**inputs):
    import concourse.bass_utils as _bu
    if "nc" not in _CACHE:
        nc, ctx, _ = build(stage="full", taps=())
        _CACHE["nc"] = nc
        _CACHE["ctx"] = ctx
    nc = _CACHE["nc"]
    in_maps = prep_inputs(inputs)
    res = _bu.run_bass_kernel_spmd(nc, in_maps, core_ids=list(range(8)))
    full = own_tokens_to_full(res.results, "out")
    return np.ascontiguousarray(full.astype(np.float32))


# version 07
# speedup vs baseline: 1.1990x; 1.1295x over previous
import contextlib
import numpy as np
import concourse.bass as bass
import concourse.mybir as mybir

F32 = mybir.dt.float32
BF16 = mybir.dt.bfloat16
ALU = mybir.AluOpType
AF = mybir.ActivationFunctionType

DM = 2048
DH = 2048
TOWN = 1024
L = 2048
S = 16
NCH = 4
LC = 512
BLK = 256
KT = 16
DTl = 16
HT = 64
NE = 8
HPE = 8
EPS = 1e-5
PAIRS = [[0, 1], [2, 3], [4, 5], [6, 7]]


def build(stage="full", taps=()):
    nc = bass.Bass("TRN2", target_bir_lowering=False, debug=False, detect_race_conditions=False)
    ctx = contextlib.ExitStack()

    dr = {}
    def din(name, shape):
        dr[name] = nc.dram_tensor(name, list(shape), F32, kind="ExternalInput").ap()
    din("x_own", (DM, TOWN))
    din("w_inx", (128, DTl * DM))
    din("w_ing", (128, DTl * DM))
    din("w_out", (128, KT * DH))
    din("w_xp", (128, KT * 160))
    din("w_dtp", (128, DH))
    din("w_g", (128, HT * DM))
    din("w_u", (128, HT * DM))
    din("w_d", (128, NE * KT * HPE * 128))
    din("A_sb", (128, DTl * S))
    din("cw_sb", (128, DTl * 4))
    din("cb_sb", (128, DTl))
    din("D_sb", (128, DTl))
    din("dtb_sb", (128, DTl))
    din("alpha", (128, 2))
    out = nc.dram_tensor("out", [DM, TOWN], F32, kind="ExternalOutput").ap()
    tap_out = {}
    for tname, tshape in taps:
        tap_out[tname] = nc.dram_tensor(tname, list(tshape), F32, kind="ExternalOutput").ap()

    def scr(name, shape, dt=F32):
        return nc.dram_tensor(name, list(shape), dt, kind="Internal").ap()
    gu_bf = scr("gu_bf", (128, 2 * HT * DM), BF16)
    d_bf = scr("d_bf", (128, NE * KT * HPE * 128), BF16)
    inx_bf = scr("inx_bf", (128, DTl * DM), BF16)
    ing_bf = scr("ing_bf", (128, DTl * DM), BF16)
    outp_bf = scr("outp_bf", (128, KT * DH), BF16)
    xres_h = scr("xres_h", (DM, TOWN))
    xt_hbm = scr("xt_hbm", (DTl, 128, L))
    ag_in = [scr(f"ag_in{h}", (1, DM, LC), BF16) for h in range(2)]
    ag_out = [scr(f"ag_out{h}", (2, DM, LC), BF16) for h in range(2)]
    ts_in = [scr(f"ts_in{c}", (160, LC)) for c in range(NCH)]
    ts_out = [scr(f"ts_out{c}", (160, LC)) for c in range(NCH)]
    op_in = [scr(f"op_in{c}", (DM, LC)) for c in range(NCH)]
    op_out = [scr(f"op_out{c}", (DM, LC)) for c in range(NCH)]
    state_snap = scr("state_snap", (2, 128, 2 * DTl * S))

    sb = {}
    def sbuf(name, shape, dt=F32):
        sb[name] = ctx.enter_context(nc.sbuf_tensor(name, list(shape), dt))
        return sb[name]

    ones = sbuf("ones", (128, 2))
    epsc = sbuf("epsc", (128, 1))
    onesr = sbuf("onesr", (1, 128))
    rows = sbuf("rows", (1, 4 * LC))
    A_s = sbuf("A_s", (128, DTl * S))
    cw_s = sbuf("cw_s", (128, DTl * 4))
    cb_s = sbuf("cb_s", (128, DTl))
    D_s = sbuf("D_s", (128, DTl))
    dtb_s = sbuf("dtb_s", (128, DTl))
    al_s = sbuf("al_s", (128, 2))
    xpT = sbuf("xpT", (128, KT * 160), BF16)
    dtpT = sbuf("dtpT", (128, DH), BF16)
    state = sbuf("state", (128, 2 * DTl * S))
    carry = sbuf("carry", (128, DTl * 4))
    wsh = sbuf("wsh", (128, 8192), BF16)
    wstrip = [wsh[:, i * 2048:(i + 1) * 2048] for i in range(4)]
    wd_st = [wsh[:, i * 4096:(i + 1) * 4096] for i in range(2)]
    wgu_st = [sbuf(f"wgu_st{i}", (128, 8192), BF16) for i in range(2)]
    arena = sbuf("arena", (128, 12288))
    acc_f = arena[:, 0:8192]
    gated = arena[:, 8192:10240].bitcast(BF16)
    xh2 = arena[:, 0:4096].bitcast(BF16)
    xt_bf = arena[:, 4096:8192].bitcast(BF16)
    yf_t = arena[:, 8192:12288].bitcast(BF16)
    cbuf = [arena[:, j * 2048:(j + 1) * 2048].bitcast(BF16) for j in range(4)]
    xh_f = sbuf("xh_f", (128, KT * LC), BF16)
    sq_t = sbuf("sq_t", (128, LC))
    ln_tmp = sbuf("ln_tmp", (128, LC))
    mkt = sbuf("mkt", (128, 8))
    res_c = arena[:, 0:KT * BLK]
    res_sl = sbuf("res_sl", (128, 2 * LC))
    xcin = sbuf("xcin", (128, 3 + LC + 1))
    cacc = sq_t[:, 0:LC]
    xt_t = sbuf("xt_t", (128, LC))
    ts_bf = sbuf("ts_bf", (128, LC), BF16)
    bc_sb = sbuf("bc_sb", (32, LC))
    bcflat = sbuf("bcflat", (1, 4 * LC))
    evac_t = sbuf("evac_t", (128, LC))
    dt_q = sbuf("dt_q", (128, 4 * LC))
    et_t = ln_tmp
    xt_sl = sbuf("xt_sl", (128, 2 * LC))
    u_q = sbuf("u_q", (128, 4 * LC))
    yac_q = sbuf("yac_q", (128, 4 * LC))
    SCW = LC + 8
    dA_t = sbuf("dA_t", (128, 3 * SCW))
    dBx_t = sbuf("dBx_t", (128, 2 * SCW))
    so_t = sbuf("so_t", (128, 2 * SCW))
    pr_t = sbuf("pr_t", (128, LC))
    opart = sbuf("opart", (128, 2 * LC))

    p_mm = ctx.enter_context(nc.psum_tensor("p_mm", [128, 1024], F32))
    p_bcB = ctx.enter_context(nc.psum_tensor("p_bcB", [128, 1024], F32))
    p_bcC = ctx.enter_context(nc.psum_tensor("p_bcC", [128, 1024], F32))
    p_f = ctx.enter_context(nc.psum_tensor("p_f", [128, 1024], F32))

    block = ctx.enter_context(nc.Block())

    class Sem:
        def __init__(self, name):
            self.h = ctx.enter_context(nc.semaphore(name))
            self.v = 0

    class DSem(Sem):
        def inc(self, inst):
            inst.then_inc(self.h, 16)
            self.v += 16
        def wait(self, e):
            e.wait_ge(self.h, self.v)

    sT = Sem("sT"); sV = Sem("sV"); sS = Sem("sS")
    d_small = DSem("d_small")
    d_smallg = DSem("d_smallg")
    d_cast = [DSem(f"d_cast{i}") for i in range(4)]
    d_cstore = [DSem(f"d_cstore{i}") for i in range(4)]
    d_strip = [DSem(f"d_strip{i}") for i in range(4)]
    d_wg = [DSem(f"d_wg{i}") for i in range(2)]
    d_wu = [DSem(f"d_wu{i}") for i in range(2)]
    d_wd = [DSem(f"d_wd{i}") for i in range(2)]
    d_xh2 = DSem("d_xh2")
    d_xt_st = DSem("d_xt_st")
    d_xt_ld = [DSem(f"d_xt_ld{i}") for i in range(2)]
    d_ts_st = DSem("d_ts_st")
    d_tsbf = DSem("d_tsbf")
    d_bcf = DSem("d_bcf")
    d_op_st = DSem("d_op_st")
    d_op_ld = [DSem(f"d_op_ld{i}") for i in range(2)]
    d_res = [DSem(f"d_res{i}") for i in range(2)]
    d_xmid_st = DSem("d_xmid_st")
    d_stres = DSem("d_stres")
    d_ldx = DSem("d_ldx")
    d_ag_st = DSem("d_ag_st")
    sO = DSem("sO")
    c_ag = [Sem(f"c_ag{i}") for i in range(2)]
    c_ts = [Sem(f"c_ts{i}") for i in range(NCH)]
    c_op = [Sem(f"c_op{i}") for i in range(NCH)]
    ALL_DSEMS = ([d_small, d_smallg, d_xh2, d_xt_st, d_ts_st, d_tsbf, d_bcf, d_op_st,
                  d_xmid_st, d_stres, d_ldx, d_ag_st, sO]
                 + d_cast + d_cstore + d_strip + d_wg + d_wu + d_wd
                 + d_xt_ld + d_op_ld + d_res)
    ALL_CSEMS = c_ag + c_ts + c_op

    def dma4(e, sem, out_ap, in_ap):
        for qq in range(4):
            sem.inc(e.dma_start(out=out_ap[qq * 32:(qq + 1) * 32], in_=in_ap[qq * 32:(qq + 1) * 32]))

    def barrier():
        def t(e): e.drain().then_inc(sT.h, 1)
        block.tensor(t); sT.v += 1
        def v(e): e.drain().then_inc(sV.h, 1)
        block.vector(v); sV.v += 1
        def s(e): e.drain().then_inc(sS.h, 1)
        block.scalar(s); sS.v += 1
        vals = [(sT, sT.v), (sV, sV.v), (sS, sS.v)] + \
               [(d, d.v) for d in ALL_DSEMS] + [(cs, cs.v) for cs in ALL_CSEMS]
        def waiter(e):
            for sm, vl in vals:
                e.wait_ge(sm.h, vl)
        for emit in (block.tensor, block.vector, block.scalar, block.gpsimd, block.sync):
            emit(waiter)

    def tap(name, src_dram):
        if name not in tap_out:
            return
        def t(e):
            sO.inc(e.dma_start(out=tap_out[name], in_=src_dram))
        block.gpsimd(t)

    def finish():
        ov = sO.v
        def fin(e, ov=ov):
            e.wait_ge(sO.h, ov)
        block.gpsimd(fin)
        return nc, ctx, tap_out

    def kt_rows(dram2d, i):
        return dram2d.rearrange("(kt p) t -> kt p t", p=128)[i]

    def _init_sync(e):
        d_small.inc(e.dma_start(out=A_s[:], in_=dr["A_sb"]))
        d_small.inc(e.dma_start(out=cw_s[:], in_=dr["cw_sb"]))
        d_small.inc(e.dma_start(out=cb_s[:], in_=dr["cb_sb"]))
        d_small.inc(e.dma_start(out=D_s[:], in_=dr["D_sb"]))
        d_small.inc(e.dma_start(out=dtb_s[:], in_=dr["dtb_sb"]))
        d_small.inc(e.dma_start(out=al_s[:], in_=dr["alpha"]))
        d_small.inc(e.dma_start(out=xres_h[:, :], in_=dr["x_own"]))
    block.sync(_init_sync)

    def _init_gps(e):
        d_smallg.inc(e.dma_start(out=xpT[:], in_=dr["w_xp"]))
        d_smallg.inc(e.dma_start(out=dtpT[:], in_=dr["w_dtp"]))
    block.gpsimd(_init_gps)

    def _init_vec(e):
        e.memset(ones[:], 1.0)
        e.memset(epsc[:], EPS)
        e.memset(mkt[:], 1.0)
        e.memset(onesr[:], 1.0)
        e.memset(state[:], 0.0)
        e.memset(dA_t[:], 0.0)
        e.memset(carry[:], 0.0).then_inc(sV.h, 1)
    block.vector(_init_vec); sV.v += 1

    CW = 4096
    CAST_JOBS = []
    for w0 in range(0, HT * DM, CW):
        CAST_JOBS.append((dr["w_g"][:, w0:w0 + CW], gu_bf[:, 2 * w0:2 * w0 + CW]))
        CAST_JOBS.append((dr["w_u"][:, w0:w0 + CW], gu_bf[:, 2 * w0 + CW:2 * w0 + 2 * CW]))
    for cls_in, cls_out in (("w_d", d_bf), ("w_inx", inx_bf), ("w_ing", ing_bf), ("w_out", outp_bf)):
        total = dr[cls_in].shape[-1]
        assert total % CW == 0
        for w0 in range(0, total, CW):
            CAST_JOBS.append((dr[cls_in][:, w0:w0 + CW], cls_out[:, w0:w0 + CW]))

    for j, (srca, dsta) in enumerate(CAST_JOBS):
        slot = j % 4
        def gld(e, srca=srca, slot=slot):
            d_cstore[slot].wait(e)
            d_cast[slot].inc(e.dma_start(out=cbuf[slot][:, 0:CW], in_=srca))
        block.gpsimd(gld)
        def wst(e, dsta=dsta, slot=slot):
            d_cast[slot].wait(e)
            d_cstore[slot].inc(e.dma_start(out=dsta, in_=cbuf[slot][:, 0:CW]))
        block.sync(wst)

    barrier()

    def layernorm(src_fn, dst_fn, n, ready):
        T0 = sT.v
        S0ln = sS.v
        slots = [sq_t[:, 0:n], xt_t[:, 0:n], evac_t[:, 0:n]]
        for kt in range(KT + 1):
            if kt < KT:
                def sq(e, kt=kt, ready=ready, T0=T0):
                    if kt == 0:
                        for w in ready:
                            w(e)
                    if kt >= 3:
                        e.wait_ge(sT.h, T0 + kt - 2)
                    e.activation(slots[kt % 3], src_fn(kt), AF.Square,
                                 bias=0.0, scale=1.0).then_inc(sS.h, 1)
                block.scalar(sq); sS.v += 1
            else:
                def sq(e):
                    e.activation(mkt[:, 0:8], mkt[:, 0:8], AF.Square, bias=0.0, scale=1.0).then_inc(sS.h, 1)
                block.scalar(sq); sS.v += 1
            if kt >= 1:
                ktm = kt - 1
                sv = S0ln + kt + 1
                def mm(e, ktm=ktm, sv=sv, ready=ready):
                    e.wait_ge(sS.h, sv)
                    if ktm == 0:
                        for w in ready:
                            w(e)
                    e.matmul(p_mm[0:1, :n], ones[:, 0:1], src_fn(ktm), start=(ktm == 0), stop=(ktm == KT - 1))
                    e.matmul(p_mm[0:1, 512:512 + n], ones[:, 0:1], slots[ktm % 3],
                             start=(ktm == 0), stop=(ktm == KT - 1)).then_inc(sT.h, 1)
                block.tensor(mm); sT.v += 1
        tv = sT.v
        def rowmath(e, tv=tv, n=n):
            e.wait_ge(sT.h, tv)
            m = rows[:, 0:n]; v = rows[:, LC:LC + n]
            e.tensor_scalar_mul(m, p_mm[0:1, :n], 1.0 / DM)
            e.tensor_tensor(v, m, m, ALU.mult)
            e.scalar_tensor_tensor(v, p_mm[0:1, 512:512 + n], 1.0 / DM, v, ALU.mult, ALU.subtract)
            e.tensor_scalar_add(v, v, EPS).then_inc(sV.h, 1)
        block.vector(rowmath); sV.v += 1
        vv = sV.v
        def rowsqrt(e, vv=vv, n=n):
            e.wait_ge(sV.h, vv)
            e.activation(rows[:, 2 * LC:2 * LC + n], rows[:, LC:LC + n], AF.Sqrt, bias=0.0, scale=1.0).then_inc(sS.h, 1)
        block.scalar(rowsqrt); sS.v += 1
        def mkrr(e):
            e.activation(mkt[:, 0:8], mkt[:, 0:8], AF.Square, bias=0.0, scale=1.0).then_inc(sS.h, 1)
        block.scalar(mkrr); sS.v += 1
        sv2 = sS.v
        def rowrec(e, sv2=sv2, n=n):
            e.wait_ge(sS.h, sv2)
            v = rows[:, LC:LC + n]
            r = rows[:, 2 * LC:2 * LC + n]
            nm = rows[:, 3 * LC:3 * LC + n]
            e.reciprocal(r, r)
            e.tensor_tensor(nm, r, r, ALU.mult)
            e.tensor_tensor(nm, nm, v, ALU.mult)
            e.tensor_scalar(nm, nm, -0.5, 1.5, ALU.mult, ALU.add)
            e.tensor_tensor(r, r, nm, ALU.mult)
            e.scalar_tensor_tensor(nm, rows[:, 0:n], -1.0, r, ALU.mult, ALU.mult).then_inc(sV.h, 1)
        block.vector(rowrec); sV.v += 1
        vv2 = sV.v
        scur = sS.v
        def bcast(e, vv2=vv2, n=n, scur=scur):
            e.wait_ge(sV.h, vv2)
            e.wait_ge(sS.h, scur)
            e.matmul(p_bcB[:, :n], onesr[:], rows[0:1, 2 * LC:2 * LC + n], start=True, stop=True)
            e.matmul(p_bcC[:, :n], onesr[:], rows[0:1, 3 * LC:3 * LC + n], start=True, stop=True).then_inc(sT.h, 1)
        block.tensor(bcast); sT.v += 1
        tv2 = sT.v
        def apply_(e, tv2=tv2, n=n):
            e.wait_ge(sT.h, tv2)
            for kt in range(KT):
                e.tensor_tensor(ln_tmp[:, :n], src_fn(kt), p_bcB[:, :n], ALU.mult)
                i_ = e.tensor_tensor(dst_fn(kt), ln_tmp[:, :n], p_bcC[:, :n], ALU.add)
                if kt == KT - 1:
                    i_.then_inc(sV.h, 1)
        block.vector(apply_); sV.v += 1

    def ffn(n, res_cols, is_ffn2):
        V_F0 = sV.v
        S_F0 = sS.v
        sil_cnt = {}
        gm_cnt = {}
        ac_end_prev = V_F0
        def z(e):
            d_stres.wait(e)
            e.memset(acc_f[:, : KT * n], 0.0).then_inc(sV.h, 1)
        block.vector(z); sV.v += 1
        gu_cnt = {}
        dn_grp_cnt = {}
        for e8 in range(NE):
            for hh in range(HPE):
                ht = e8 * HPE + hh
                slot = (ht // 2) % 2
                sub = (ht % 2) * KT * 128
                pair = [p_f, p_mm][ht % 2]
                sgsl = [ln_tmp, evac_t][ht % 2]
                if ht % 2 == 0:
                    tw = gu_cnt.get(ht - 3, sT.v)
                    def ld(e, ht=ht, slot=slot, tw=tw):
                        e.wait_ge(sT.h, tw)
                        d_wg[slot].inc(e.dma_start(out=wgu_st[slot][:],
                                                   in_=gu_bf[:, ht * 4096:(ht + 2) * 4096]))
                    block.gpsimd(ld)
                if hh == 0 and e8 > 0:
                    vv_need = ac_end_prev
                else:
                    vv_need = gm_cnt[ht - 2] if ht >= 2 else V_F0
                sv_need = sil_cnt[ht - 2] if ht >= 2 else S_F0
                def gu(e, slot=slot, sub=sub, pair=pair, vv_need=vv_need, sv_need=sv_need, n=n):
                    d_wg[slot].wait(e)
                    e.wait_ge(sV.h, vv_need)
                    e.wait_ge(sS.h, sv_need)
                    for k in range(KT):
                        e.matmul(pair[:, 0:n], wgu_st[slot][:, sub + k * 128:sub + (k + 1) * 128],
                                 xh_f[:, k * LC:k * LC + n], start=(k == 0), stop=(k == KT - 1))
                    for k in range(KT):
                        i_ = e.matmul(pair[:, 512:512 + n], wgu_st[slot][:, 4096 + sub + k * 128:4096 + sub + (k + 1) * 128],
                                      xh_f[:, k * LC:k * LC + n], start=(k == 0), stop=(k == KT - 1))
                        if k == KT - 1:
                            i_.then_inc(sT.h, 1)
                block.tensor(gu); sT.v += 1
                gu_cnt[ht] = sT.v
                tv = sT.v
                vprot = gm_cnt[ht - 2] if ht >= 2 else V_F0
                def sil(e, tv=tv, n=n, vprot=vprot, pair=pair, sgsl=sgsl):
                    e.wait_ge(sT.h, tv)
                    e.wait_ge(sV.h, vprot)
                    e.activation(sgsl[:, :n], pair[:, 0:n], AF.Silu, bias=0.0, scale=1.0).then_inc(sS.h, 1)
                block.scalar(sil); sS.v += 1
                sil_cnt[ht] = sS.v
                sv = sil_cnt[ht] + 1
                def gm(e, hh=hh, sv=sv, n=n, pair=pair, sgsl=sgsl):
                    e.wait_ge(sS.h, sv)
                    e.tensor_tensor(gated[:, hh * n:(hh + 1) * n], sgsl[:, :n], pair[:, 512:512 + n], ALU.mult).then_inc(sV.h, 1)
                block.vector(gm); sV.v += 1
                gm_cnt[ht] = sV.v
            def mkrf(e):
                e.activation(mkt[:, 0:8], mkt[:, 0:8], AF.Square, bias=0.0, scale=1.0).then_inc(sS.h, 1)
            block.scalar(mkrf); sS.v += 1
            for dmt in range(KT):
                slot = (dmt // 4) % 2
                sub = (dmt % 4) * HPE * 128
                if dmt % 4 == 0:
                    grp = e8 * 4 + dmt // 4
                    tw = dn_grp_cnt.get(grp - 2, sT.v)
                    def ldd(e, e8=e8, dmt=dmt, slot=slot, tw=tw):
                        e.wait_ge(sT.h, tw)
                        base = (e8 * KT + dmt) * HPE * 128
                        d_wd[slot].inc(e.dma_start(out=wd_st[slot][:],
                                                   in_=d_bf[:, base:base + 4 * HPE * 128]))
                    block.gpsimd(ldd)
                vv_need = sV.v
                sv_need = sS.v
                def dn(e, slot=slot, sub=sub, vv_need=vv_need, sv_need=sv_need, n=n):
                    d_wd[slot].wait(e)
                    e.wait_ge(sV.h, vv_need)
                    e.wait_ge(sS.h, sv_need)
                    for h in range(HPE):
                        i_ = e.matmul(p_f[:, 0:n], wd_st[slot][:, sub + h * 128:sub + (h + 1) * 128],
                                      gated[:, h * n:(h + 1) * n], start=(h == 0), stop=(h == HPE - 1))
                        if h == HPE - 1:
                            i_.then_inc(sT.h, 1)
                block.tensor(dn); sT.v += 1
                if dmt % 4 == 3:
                    dn_grp_cnt[e8 * 4 + dmt // 4] = sT.v
                tv = sT.v
                def ac(e, dmt=dmt, tv=tv, n=n):
                    e.wait_ge(sT.h, tv)
                    e.tensor_tensor(acc_f[:, dmt * n:(dmt + 1) * n], p_f[:, 0:n],
                                    acc_f[:, dmt * n:(dmt + 1) * n], ALU.add).then_inc(sV.h, 1)
                block.vector(ac); sV.v += 1
            ac_end_prev = sV.v
        for dmt in range(KT):
            slot = dmt % 2
            vcur = sV.v
            def ldr(e, dmt=dmt, slot=slot, res_cols=res_cols, n=n, vcur=vcur):
                e.wait_ge(sV.h, vcur)
                d_stres.wait(e)
                d_xmid_st.wait(e)
                d_res[slot].inc(e.dma_start(
                    out=res_sl[:, slot * LC:slot * LC + n],
                    in_=kt_rows(xres_h, dmt)[:, res_cols:res_cols + n]))
            block.gpsimd(ldr)
            def upd(e, dmt=dmt, slot=slot, n=n):
                d_res[slot].wait(e)
                e.scalar_tensor_tensor(res_sl[:, slot * LC:slot * LC + n],
                                       acc_f[:, dmt * n:(dmt + 1) * n], 0.5,
                                       res_sl[:, slot * LC:slot * LC + n], ALU.mult, ALU.add).then_inc(sV.h, 1)
            block.vector(upd); sV.v += 1
            vv = sV.v
            if is_ffn2:
                def st(e, dmt=dmt, slot=slot, vv=vv, res_cols=res_cols, n=n):
                    e.wait_ge(sV.h, vv)
                    sO.inc(e.dma_start(out=kt_rows(out, dmt)[:, res_cols:res_cols + n],
                                       in_=res_sl[:, slot * LC:slot * LC + n]))
                block.gpsimd(st)
            else:
                def st(e, dmt=dmt, slot=slot, vv=vv, res_cols=res_cols, n=n):
                    e.wait_ge(sV.h, vv)
                    d_xmid_st.inc(e.dma_start(out=kt_rows(xres_h, dmt)[:, res_cols:res_cols + n],
                                              in_=res_sl[:, slot * LC:slot * LC + n]))
                block.gpsimd(st)

    for h in range(2):
        base = h * LC
        def ldx(e, base=base):
            d_xmid_st.wait(e)
            d_ldx.inc(e.dma_start(
                out=acc_f[:, :KT * LC].rearrange("p (kt t) -> p kt t", t=LC),
                in_=xres_h.rearrange("(kt p) t -> p kt t", p=128)[:, :, base:base + LC]))
        block.gpsimd(ldx)
        layernorm(lambda kt: acc_f[:, kt * LC:(kt + 1) * LC],
                  lambda kt: xh_f[:, kt * LC:(kt + 1) * LC], LC,
                  [lambda e: d_ldx.wait(e)])
        ffn(LC, base, False)
        barrier()
    tap("tap_xmid", xres_h)
    if stage == "ffn1":
        return finish()

    for h in range(2):
        base = h * LC
        def ldx2(e, base=base):
            d_xmid_st.wait(e)
            d_ldx.inc(e.dma_start(
                out=acc_f[:, :KT * LC].rearrange("p (kt t) -> p kt t", t=LC),
                in_=xres_h.rearrange("(kt p) t -> p kt t", p=128)[:, :, base:base + LC]))
        block.gpsimd(ldx2)
        layernorm(lambda kt: acc_f[:, kt * LC:(kt + 1) * LC],
                  lambda kt: xh_f[:, kt * LC:(kt + 1) * LC], LC,
                  [lambda e: d_ldx.wait(e)])
        vv = sV.v
        def stag(e, h=h, vv=vv):
            e.wait_ge(sV.h, vv)
            d_ag_st.inc(e.dma_start(out=ag_in[h][0].rearrange("(kt p) t -> p kt t", p=128),
                                    in_=xh_f[:].rearrange("p (kt t) -> p kt t", t=LC)))
        block.gpsimd(stag)
        def agc(e, h=h):
            d_ag_st.wait(e)
            e.collective_compute("AllGather", ALU.bypass, replica_groups=PAIRS,
                                 ins=[ag_in[h]], outs=[ag_out[h]]).then_inc(c_ag[h].h, 1)
        block.gpsimd(agc); c_ag[h].v += 1
        barrier()

    for c in range(NCH):
        hf = c // 2
        coff = (c % 2) * BLK
        tcur0 = sT.v
        def ldxh2(e, hf=hf, coff=coff, tcur0=tcur0):
            e.wait_ge(c_ag[hf].h, c_ag[hf].v)
            e.wait_ge(sT.h, tcur0)
            for slot in range(2):
                d_xh2.inc(e.dma_start(
                    out=xh2.rearrange("p (kt t) -> p kt t", t=LC)[:, :, slot * BLK:(slot + 1) * BLK],
                    in_=ag_out[hf][slot].rearrange("(kt p) t -> p kt t", p=128)[:, :, coff:coff + BLK]))
        block.gpsimd(ldxh2)

        S_A0 = sS.v
        T_A0 = sT.v
        for d in range(DTl):
            slot = d % 2
            wsl = d % 4
            tw = T_A0 + d - 3 if d >= 4 else sT.v
            def lds(e, d=d, wsl=wsl, tw=tw):
                e.wait_ge(sT.h, tw)
                d_strip[wsl].inc(e.dma_start(out=wstrip[wsl][:], in_=inx_bf[:, d * DM:(d + 1) * DM]))
            block.sync(lds)
            scur = sS.v
            def mmx(e, slot=slot, wsl=wsl, scur=scur):
                d_strip[wsl].wait(e)
                d_xh2.wait(e)
                e.wait_ge(sS.h, scur)
                for k in range(KT):
                    i_ = e.matmul(p_mm[:, slot * 512:slot * 512 + LC], wstrip[wsl][:, k * 128:(k + 1) * 128],
                                  xh2[:, k * LC:(k + 1) * LC], start=(k == 0), stop=(k == KT - 1))
                    if k == KT - 1:
                        i_.then_inc(sT.h, 1)
            block.tensor(mmx); sT.v += 1
            assert sT.v == T_A0 + d + 1
            tv = sT.v
            vcur = sV.v
            def evx(e, slot=slot, tv=tv, vcur=vcur):
                e.wait_ge(sT.h, tv)
                e.wait_ge(sV.h, vcur)
                e.copy(xcin[:, 3:3 + LC], p_mm[:, slot * 512:slot * 512 + LC]).then_inc(sS.h, 1)
            block.scalar(evx); sS.v += 1
            def mkre(e):
                e.activation(mkt[:, 0:8], mkt[:, 0:8], AF.Square, bias=0.0, scale=1.0).then_inc(sS.h, 1)
            block.scalar(mkre); sS.v += 1
            sv = sS.v
            def conv(e, d=d, sv=sv):
                e.wait_ge(sS.h, sv)
                e.tensor_copy(xcin[:, 0:3], carry[:, d * 4:d * 4 + 3])
                e.tensor_copy(carry[:, d * 4:d * 4 + 3], xcin[:, LC:LC + 3])
                e.tensor_scalar_mul(cacc, xcin[:, 0:LC], cw_s[:, d * 4:d * 4 + 1])
                for k in range(1, 4):
                    i_ = e.scalar_tensor_tensor(cacc, xcin[:, k:k + LC], cw_s[:, d * 4 + k:d * 4 + k + 1],
                                                cacc, ALU.mult, ALU.add)
                    if k == 3:
                        i_.then_inc(sV.h, 1)
            block.vector(conv); sV.v += 1
            vv = sV.v
            def silu_x(e, d=d, vv=vv):
                e.wait_ge(sV.h, vv)
                d_xt_st.wait(e)
                e.activation(xt_t[:], cacc, AF.Silu, bias=cb_s[:, d:d + 1], scale=1.0)
                e.copy(xt_bf[:, d * LC:(d + 1) * LC], xt_t[:]).then_inc(sS.h, 1)
            block.scalar(silu_x); sS.v += 1
            sv2 = S_A0 + (3 * d + 4 if d < DTl - 1 else 3 * DTl + 1)
            def stxt(e, d=d, c=c, sv2=sv2):
                e.wait_ge(sS.h, sv2)
                d_xt_st.inc(e.dma_start(out=xt_hbm[d][:, c * LC:(c + 1) * LC], in_=xt_t[:]))
            block.gpsimd(stxt)
        def mkra(e):
            e.activation(mkt[:, 0:8], mkt[:, 0:8], AF.Square, bias=0.0, scale=1.0).then_inc(sS.h, 1)
        block.scalar(mkra); sS.v += 1
        assert sS.v == S_A0 + 3 * DTl + 1
        s_xtbf = sS.v

        vcur = sV.v
        def xproj(e, s_xtbf=s_xtbf, vcur=vcur):
            e.wait_ge(sS.h, s_xtbf)
            e.wait_ge(sV.h, vcur)
            for k in range(DTl):
                e.matmul(p_mm[:, 0:LC], xpT[:, k * 160:k * 160 + 128],
                         xt_bf[:, k * LC:(k + 1) * LC], start=(k == 0), stop=(k == DTl - 1))
            for k in range(DTl):
                i_ = e.matmul(p_bcB[0:32, 0:LC], xpT[:, k * 160 + 128:(k + 1) * 160],
                              xt_bf[:, k * LC:(k + 1) * LC], start=(k == 0), stop=(k == DTl - 1))
                if k == DTl - 1:
                    i_.then_inc(sT.h, 1)
        block.tensor(xproj); sT.v += 1
        tv = sT.v
        def evts(e, tv=tv):
            e.wait_ge(sT.h, tv)
            e.copy(evac_t[:], p_mm[:, 0:LC])
            e.copy(bc_sb[:], p_bcB[0:32, 0:LC]).then_inc(sS.h, 1)
        block.scalar(evts); sS.v += 1
        def mkrt(e):
            e.activation(mkt[:, 0:8], mkt[:, 0:8], AF.Square, bias=0.0, scale=1.0).then_inc(sS.h, 1)
        block.scalar(mkrt); sS.v += 1
        sv = sS.v
        def sts(e, c=c, sv=sv):
            e.wait_ge(sS.h, sv)
            d_ts_st.inc(e.dma_start(out=ts_in[c][0:128, :], in_=evac_t[:]))
            d_ts_st.inc(e.dma_start(out=ts_in[c][128:160, :], in_=bc_sb[:]))
        block.gpsimd(sts)
        def arc(e, c=c):
            d_ts_st.wait(e)
            e.collective_compute("AllReduce", ALU.add, replica_groups=PAIRS,
                                 ins=[ts_in[c]], outs=[ts_out[c]]).then_inc(c_ts[c].h, 1)
        block.gpsimd(arc); c_ts[c].v += 1
        def ldts(e, c=c):
            e.wait_ge(c_ts[c].h, c_ts[c].v)
            d_tsbf.inc(e.dma_start(out=ts_bf[:], in_=ts_out[c][0:128, :]))
        block.gpsimd(ldts)

        if stage == "dbg3" and c == 1:
            def snap1(e):
                sO.inc(e.dma_start(out=state_snap[1], in_=state[:]))
            block.gpsimd(snap1)
        QD = 4
        v_prep_ok = sV.v
        gmm_cnt = {}
        for q in range(DTl // QD):
            S_prep0 = sS.v
            for dq in range(QD):
                d = q * QD + dq
                dsl = d % 2
                scur = sS.v
                def dtmm(e, d=d, scur=scur):
                    d_tsbf.wait(e)
                    e.wait_ge(sS.h, scur)
                    e.matmul(p_mm[:, 512:512 + LC], dtpT[:, d * 128:(d + 1) * 128], ts_bf[:],
                             start=True, stop=True).then_inc(sT.h, 1)
                block.tensor(dtmm); sT.v += 1
                tv = sT.v
                def softp(e, dq=dq, d=d, tv=tv, vfix=v_prep_ok):
                    e.wait_ge(sT.h, tv)
                    e.wait_ge(sV.h, vfix)
                    e.activation(et_t[:, 0:LC], p_mm[:, 512:512 + LC], AF.Exp, bias=dtb_s[:, d:d + 1], scale=1.0)
                    e.activation(dt_q[:, dq * LC:(dq + 1) * LC], et_t[:, 0:LC], AF.Ln, bias=1.0, scale=1.0).then_inc(sS.h, 1)
                block.scalar(softp); sS.v += 1
                assert sS.v == S_prep0 + dq + 1
                vcur = sV.v
                def ldxt(e, d=d, c=c, dsl=dsl, vcur=vcur):
                    e.wait_ge(sV.h, vcur)
                    d_xt_st.wait(e)
                    d_xt_ld[dsl].inc(e.dma_start(
                        out=xt_sl[:, dsl * LC:(dsl + 1) * LC],
                        in_=xt_hbm[d][:, c * LC:(c + 1) * LC]))
                block.gpsimd(ldxt)
                s_dt = S_prep0 + dq + 2
                def uprep(e, dq=dq, d=d, dsl=dsl, s_dt=s_dt):
                    e.wait_ge(sS.h, s_dt)
                    d_xt_ld[dsl].wait(e)
                    e.tensor_tensor(u_q[:, dq * LC:(dq + 1) * LC], dt_q[:, dq * LC:(dq + 1) * LC],
                                    xt_sl[:, dsl * LC:(dsl + 1) * LC], ALU.mult)
                    e.tensor_scalar_mul(yac_q[:, dq * LC:(dq + 1) * LC],
                                        xt_sl[:, dsl * LC:(dsl + 1) * LC], D_s[:, d:d + 1]).then_inc(sV.h, 1)
                block.vector(uprep); sV.v += 1
            def mkr2(e):
                e.activation(mkt[:, 0:8], mkt[:, 0:8], AF.Square, bias=0.0, scale=1.0).then_inc(sS.h, 1)
            block.scalar(mkr2); sS.v += 1
            assert sS.v == S_prep0 + QD + 1
            V0 = sV.v
            S0 = sS.v
            T0 = sT.v
            NREF = 8
            for r in range(NREF):
                def refill(e, c=c, r=r, T0=T0, tcur_r=sT.v):
                    e.wait_ge(c_ts[c].h, c_ts[c].v)
                    e.wait_ge(sT.h, T0 + 4 * r if r >= 1 else tcur_r)
                    d_bcf.inc(e.dma_start(
                        out=bcflat[:, 0:2 * LC],
                        in_=ts_out[c][128 + 2 * r:130 + 2 * r, :].rearrange("a b -> (a b)").unsqueeze(0)))
                    d_bcf.inc(e.dma_start(
                        out=bcflat[:, 2 * LC:4 * LC],
                        in_=ts_out[c][144 + 2 * r:146 + 2 * r, :].rearrange("a b -> (a b)").unsqueeze(0)))
                block.gpsimd(refill)
                for i2 in range(2):
                    s_ = 2 * r + i2
                    def bca(e, s_=s_, i2=i2, V0=V0):
                        d_bcf.wait(e)
                        if s_ >= 2:
                            e.wait_ge(sV.h, V0 + QD * (s_ - 1))
                        else:
                            e.wait_ge(sV.h, V0)
                        e.matmul(p_bcB[:, (s_ % 2) * 512:(s_ % 2) * 512 + LC], onesr[:],
                                 bcflat[0:1, i2 * LC:(i2 + 1) * LC], start=True, stop=True)
                        e.matmul(p_bcC[:, (s_ % 2) * 512:(s_ % 2) * 512 + LC], onesr[:],
                                 bcflat[0:1, 2 * LC + i2 * LC:2 * LC + (i2 + 1) * LC], start=True, stop=True).then_inc(sT.h, 2)
                    block.tensor(bca); sT.v += 2
            def dA_all(e, q=q, V0=V0):
                for s_ in range(S):
                    for dq in range(QD):
                        d = q * QD + dq
                        j = s_ * QD + dq
                        e.wait_ge(sV.h, V0 + j - 2 if j >= 3 else V0)
                        e.activation(dA_t[:, (j % 3) * SCW + 1:(j % 3) * SCW + 1 + LC],
                                     dt_q[:, dq * LC:(dq + 1) * LC], AF.Exp,
                                     bias=0.0, scale=A_s[:, d * S + s_:d * S + s_ + 1]).then_inc(sS.h, 1)
                e.activation(mkt[:, 0:8], mkt[:, 0:8], AF.Square, bias=0.0, scale=1.0).then_inc(sS.h, 1)
            block.scalar(dA_all); sS.v += S * QD + 1
            def scan_all(e, q=q, c=c, V0=V0, S0=S0, T0=T0):
                pend = None
                for s_ in range(S):
                    e.wait_ge(sT.h, T0 + 2 * min(s_ + 2, S))
                    for dq in range(QD):
                        d = q * QD + dq
                        j = s_ * QD + dq
                        sl = (j % 2) * SCW
                        bsl = (s_ % 2) * 512
                        rd_off = (c % 2) * DTl * S
                        wr_off = ((c + 1) % 2) * DTl * S
                        col = d * S + s_
                        e.wait_ge(sS.h, S0 + min(j + 2, S * QD + 1))
                        e.tensor_copy(dBx_t[:, sl:sl + 1], state[:, rd_off + col:rd_off + col + 1])
                        e.tensor_tensor(dBx_t[:, sl + 1:sl + 1 + LC], u_q[:, dq * LC:(dq + 1) * LC],
                                        p_bcB[:, bsl:bsl + LC], ALU.mult)
                        e.tensor_tensor_scan(so_t[:, sl:sl + 1 + LC], dA_t[:, (j % 3) * SCW:(j % 3) * SCW + 1 + LC],
                                             dBx_t[:, sl:sl + 1 + LC], 0.0, ALU.mult, ALU.add)
                        if pend is not None:
                            e.tensor_copy(state[:, wr_off + pend[0]:wr_off + pend[0] + 1],
                                          so_t[:, pend[1] + LC:pend[1] + LC + 1])
                        e.tensor_tensor(pr_t[:], so_t[:, sl + 1:sl + 1 + LC], p_bcC[:, bsl:bsl + LC], ALU.mult)
                        e.tensor_tensor(yac_q[:, dq * LC:(dq + 1) * LC], pr_t[:],
                                        yac_q[:, dq * LC:(dq + 1) * LC], ALU.add).then_inc(sV.h, 1)
                        pend = (col, sl)
                if s_ == S - 1:
                    e.tensor_copy(state[:, wr_off + pend[0]:wr_off + pend[0] + 1],
                                  so_t[:, pend[1] + LC:pend[1] + LC + 1])
            block.vector(scan_all); sV.v += S * QD
            v_scan = sV.v
            G_q0 = sS.v
            yfm_cnt_q = []
            for dq in range(QD):
                d = q * QD + dq
                dsl = d % 2
                wsl = dq
                sgsl = [ln_tmp, evac_t][dq % 2]
                tw = gmm_cnt.get((q - 1, dq), sT.v)
                def ldg(e, d=d, wsl=wsl, tw=tw):
                    e.wait_ge(sT.h, tw)
                    d_strip[wsl].inc(e.dma_start(out=wstrip[wsl][:], in_=ing_bf[:, d * DM:(d + 1) * DM]))
                block.sync(ldg)
                scur2 = sS.v
                def gmm(e, dsl=dsl, wsl=wsl, scur2=scur2, v_scan=v_scan):
                    d_strip[wsl].wait(e)
                    d_xh2.wait(e)
                    e.wait_ge(sS.h, scur2)
                    if dsl % 2 == 1:
                        e.wait_ge(sV.h, v_scan)
                    for k in range(KT):
                        i_ = e.matmul(p_mm[:, (dsl % 2) * 512:(dsl % 2) * 512 + LC],
                                      wstrip[wsl][:, k * 128:(k + 1) * 128],
                                      xh2[:, k * LC:(k + 1) * LC], start=(k == 0), stop=(k == KT - 1))
                        if k == KT - 1:
                            i_.then_inc(sT.h, 1)
                block.tensor(gmm); sT.v += 1
                gmm_cnt[(q, dq)] = sT.v
                tv = sT.v
                vfix = yfm_cnt_q[dq - 2] if dq >= 2 else v_prep_ok
                def sgx(e, dsl=dsl, tv=tv, vfix=vfix, sgsl=sgsl):
                    e.wait_ge(sT.h, tv)
                    e.wait_ge(sV.h, vfix)
                    e.activation(sgsl[:, :LC], p_mm[:, (dsl % 2) * 512:(dsl % 2) * 512 + LC],
                                 AF.Silu, bias=0.0, scale=1.0).then_inc(sS.h, 1)
                block.scalar(sgx); sS.v += 1
                assert sS.v == G_q0 + dq + 1
                sv = G_q0 + dq + 2
                def yfm(e, dq=dq, d=d, sv=sv, v_scan=v_scan, sgsl=sgsl):
                    e.wait_ge(sS.h, sv)
                    e.wait_ge(sV.h, v_scan)
                    e.tensor_tensor(yf_t[:, d * LC:(d + 1) * LC], yac_q[:, dq * LC:(dq + 1) * LC],
                                    sgsl[:, :LC], ALU.mult).then_inc(sV.h, 1)
                block.vector(yfm); sV.v += 1
                yfm_cnt_q.append(sV.v)
            def mkrg(e):
                e.activation(mkt[:, 0:8], mkt[:, 0:8], AF.Square, bias=0.0, scale=1.0).then_inc(sS.h, 1)
            block.scalar(mkrg); sS.v += 1
            assert sS.v == G_q0 + QD + 1
            v_prep_ok = sV.v
        v_yf = sV.v
        if stage == "dbg1" and c == 1 and "tap_yf" in tap_out:
            def tapyf(e, v_yf=v_yf):
                e.wait_ge(sV.h, v_yf)
                sO.inc(e.dma_start(out=tap_out["tap_yf"], in_=yf_t[:]))
                sO.inc(e.dma_start(out=tap_out["tap_ts"], in_=ts_out[1]))
            block.gpsimd(tapyf)

        S_C0 = sS.v
        op_base = d_op_st.v
        T_C0 = sT.v
        for i in range(KT):
            slot = i % 2
            wsl = i % 4
            tw = T_C0 + i - 3 if i >= 4 else sT.v
            def ldo(e, i=i, wsl=wsl, tw=tw):
                e.wait_ge(sT.h, tw)
                d_strip[wsl].inc(e.dma_start(out=wstrip[wsl][:], in_=outp_bf[:, i * DH:(i + 1) * DH]))
            block.sync(ldo)
            scur = sS.v
            def omm(e, slot=slot, wsl=wsl, v_yf=v_yf, scur=scur):
                d_strip[wsl].wait(e)
                e.wait_ge(sV.h, v_yf)
                e.wait_ge(sS.h, scur)
                for k in range(DTl):
                    i_ = e.matmul(p_mm[:, slot * 512:slot * 512 + LC], wstrip[wsl][:, k * 128:(k + 1) * 128],
                                  yf_t[:, k * LC:(k + 1) * LC], start=(k == 0), stop=(k == DTl - 1))
                    if k == DTl - 1:
                        i_.then_inc(sT.h, 1)
            block.tensor(omm); sT.v += 1
            assert sT.v == T_C0 + i + 1
            tv = sT.v
            opw = op_base + 16 * (i - 1) if i >= 2 else op_base
            def evo(e, slot=slot, tv=tv, opw=opw):
                e.wait_ge(sT.h, tv)
                e.wait_ge(d_op_st.h, opw)
                e.copy(opart[:, slot * LC:(slot + 1) * LC], p_mm[:, slot * 512:slot * 512 + LC]).then_inc(sS.h, 1)
            block.scalar(evo); sS.v += 1
            assert sS.v == S_C0 + i + 1
            sv = S_C0 + i + 2
            def sto(e, i=i, c=c, slot=slot, sv=sv):
                e.wait_ge(sS.h, sv)
                d_op_st.inc(e.dma_start(out=op_in[c].rearrange("(kt p) t -> kt p t", p=128)[i],
                                        in_=opart[:, slot * LC:(slot + 1) * LC]))
            block.gpsimd(sto)
        def mkro(e):
            e.activation(mkt[:, 0:8], mkt[:, 0:8], AF.Square, bias=0.0, scale=1.0).then_inc(sS.h, 1)
        block.scalar(mkro); sS.v += 1
        assert sS.v == S_C0 + KT + 1
        def aro(e, c=c):
            d_op_st.wait(e)
            e.collective_compute("AllReduce", ALU.add, replica_groups=PAIRS,
                                 ins=[op_in[c]], outs=[op_out[c]]).then_inc(c_op[c].h, 1)
        block.gpsimd(aro); c_op[c].v += 1

        owncol = c * BLK
        for i in range(KT):
            slot = i % 2
            vcur = sV.v
            def ldop(e, i=i, c=c, slot=slot, vcur=vcur, owncol=owncol):
                e.wait_ge(c_op[c].h, c_op[c].v)
                e.wait_ge(sV.h, vcur)
                d_stres.wait(e)
                d_op_ld[slot].inc(e.dma_start(
                    out=opart[:, slot * LC:(slot + 1) * LC],
                    in_=op_out[c].rearrange("(kt p) t -> kt p t", p=128)[i]))
                d_op_ld[slot].inc(e.dma_start(
                    out=res_sl[:, slot * LC:slot * LC + BLK],
                    in_=kt_rows(xres_h, i)[:, owncol:owncol + BLK]))
            block.gpsimd(ldop)
            def resx(e, i=i, slot=slot):
                d_op_ld[slot].wait(e)
                e.scalar_tensor_tensor(res_sl[:, slot * LC:slot * LC + BLK],
                                       opart[:, slot * LC:slot * LC + BLK], al_s[:, 0:1],
                                       res_sl[:, slot * LC:slot * LC + BLK], ALU.mult, ALU.add)
                e.scalar_tensor_tensor(res_c[:, i * BLK:(i + 1) * BLK],
                                       opart[:, slot * LC + BLK:slot * LC + 2 * BLK], al_s[:, 1:2],
                                       res_sl[:, slot * LC:slot * LC + BLK], ALU.mult, ALU.add).then_inc(sV.h, 1)
            block.vector(resx); sV.v += 1
            vv = sV.v
            def stres(e, i=i, vv=vv, owncol=owncol):
                e.wait_ge(sV.h, vv)
                d_stres.inc(e.dma_start(out=kt_rows(xres_h, i)[:, owncol:owncol + BLK],
                                        in_=res_c[:, i * BLK:(i + 1) * BLK]))
            block.gpsimd(stres)
        v_res = sV.v
        layernorm(lambda kt: res_c[:, kt * BLK:(kt + 1) * BLK],
                  lambda kt: xh_f[:, kt * LC + (c % 2) * BLK:kt * LC + (c % 2) * BLK + BLK],
                  BLK, [lambda e: e.wait_ge(sV.h, v_res)])

        if c % 2 == 1 and stage == "full":
            g = c // 2
            ffn(LC, g * LC, True)
        if stage == "dbg1" and c == 1 and "tap_op" in tap_out:
            def tapop(e):
                e.wait_ge(c_op[1].h, c_op[1].v)
                sO.inc(e.dma_start(out=tap_out["tap_op"], in_=op_out[1]))
            block.gpsimd(tapop)
        barrier()
        if stage == "dbg3" and c == 0:
            def snap0(e):
                sO.inc(e.dma_start(out=state_snap[0], in_=state[:]))
            block.gpsimd(snap0)
            barrier()
        if stage == "dbg3" and c == 1:
            tap("tap_snap", state_snap.rearrange("a p n -> (a p) n"))
            tap("tap_xafter", xres_h)
            return finish()
        if stage == "dbg1" and c == 1:
            tap("tap_xafter", xres_h)
            return finish()
        if stage == "mamba1" and c == 0:
            tap("tap_xafter", xres_h)
            return finish()
    tap("tap_xafter", xres_h)
    tap("tap_xt", xt_hbm.rearrange("d p l -> (d p) l"))
    tap("tap_ts", ts_out[1])
    tap("tap_op", op_out[1])
    tap("tap_ag", ag_out[0].rearrange("s d t -> (s d) t"))
    if stage == "mamba":
        return finish()

    return finish()


def prep_inputs(inputs):
    x = np.ascontiguousarray(np.asarray(inputs["L0_attn_hidden_states"], np.float32))
    w_in = np.asarray(inputs["L0_attn_in_proj_w"], np.float32)
    conv_w = np.asarray(inputs["L0_attn_conv_w"], np.float32)
    conv_b = np.asarray(inputs["L0_attn_conv_b"], np.float32)
    x_proj = np.asarray(inputs["L0_attn_x_proj_w"], np.float32)
    dt_w = np.asarray(inputs["L0_attn_dt_proj_w"], np.float32)
    dt_b = np.asarray(inputs["L0_attn_dt_proj_b"], np.float32)
    A = np.asarray(inputs["L0_attn_A"], np.float32)
    Dv = np.asarray(inputs["L0_attn_D"], np.float32)
    w_out = np.asarray(inputs["L0_attn_out_proj_w"], np.float32)
    wg = np.asarray(inputs["L0_ffn_w_gate"], np.float32)
    wu = np.asarray(inputs["L0_ffn_w_up"], np.float32)
    wd = np.asarray(inputs["L0_ffn_w_down"], np.float32)

    def pmaj(W, nt):
        K = W.shape[1]
        return np.ascontiguousarray(
            W.reshape(nt, 128, K // 128, 128).transpose(3, 0, 2, 1).reshape(128, nt * K))
    wg_s = pmaj(wg, HT)
    wu_s = pmaj(wu, HT)
    wd_s = np.ascontiguousarray(
        wd.reshape(KT, 128, NE, HPE, 128).transpose(4, 2, 0, 3, 1).reshape(128, NE * KT * HPE * 128))

    in_maps = []
    for core in range(8):
        p = core // 2
        q = core & 1
        blocks = [q, q + 2, q + 4, q + 6]
        xb = x[p].reshape(8, BLK, DM)
        x_ownT = np.ascontiguousarray(
            np.concatenate([xb[b] for b in blocks], axis=0).T)
        xrows = slice(q * DH, (q + 1) * DH)
        grows = slice(4096 + q * DH, 4096 + (q + 1) * DH)
        w_inx = pmaj(w_in[xrows], DTl)
        w_ing = pmaj(w_in[grows], DTl)
        w_outs = pmaj(w_out[:, q * DH:(q + 1) * DH], KT)
        w_xp = np.ascontiguousarray(
            x_proj[:, q * DH:(q + 1) * DH].T.reshape(KT, 128, 160).transpose(1, 0, 2).reshape(128, KT * 160))
        w_dtp = np.ascontiguousarray(dt_w[q * DH:(q + 1) * DH, :].T)
        Ah = A[q * DH:(q + 1) * DH]
        A_sb = np.ascontiguousarray(Ah.reshape(DTl, 128, S).transpose(1, 0, 2).reshape(128, DTl * S))
        cwh = conv_w[q * DH:(q + 1) * DH]
        cw_sb = np.ascontiguousarray(cwh.reshape(DTl, 128, 4).transpose(1, 0, 2).reshape(128, DTl * 4))
        cb_sb = np.ascontiguousarray(conv_b[q * DH:(q + 1) * DH].reshape(DTl, 128).T)
        D_sb = np.ascontiguousarray(Dv[q * DH:(q + 1) * DH].reshape(DTl, 128).T)
        dtb_sb = np.ascontiguousarray(dt_b[q * DH:(q + 1) * DH].reshape(DTl, 128).T)
        alpha = np.zeros((128, 2), np.float32)
        alpha[:, 0] = 1.0 if q == 0 else 0.0
        alpha[:, 1] = 0.0 if q == 0 else 1.0
        in_maps.append({
            "x_own": x_ownT, "w_inx": w_inx, "w_ing": w_ing, "w_out": w_outs,
            "w_xp": w_xp, "w_dtp": w_dtp, "w_g": wg_s, "w_u": wu_s, "w_d": wd_s,
            "A_sb": A_sb, "cw_sb": cw_sb, "cb_sb": cb_sb, "D_sb": D_sb,
            "dtb_sb": dtb_sb, "alpha": alpha,
        })
    return in_maps


def own_tokens_to_full(results, key="out"):
    full = np.zeros((4, 2048, DM), np.float32)
    for core in range(8):
        p = core // 2
        q = core & 1
        blocks = [q, q + 2, q + 4, q + 6]
        o = np.asarray(results[core][key]).T.reshape(4, BLK, DM)
        for j, b in enumerate(blocks):
            full[p, b * BLK:(b + 1) * BLK, :] = o[j]
    return full


_CACHE = {}


def kernel(**inputs):
    import concourse.bass_utils as _bu
    if "nc" not in _CACHE:
        nc, ctx, _ = build(stage="full", taps=())
        _CACHE["nc"] = nc
        _CACHE["ctx"] = ctx
    nc = _CACHE["nc"]
    in_maps = prep_inputs(inputs)
    res = _bu.run_bass_kernel_spmd(nc, in_maps, core_ids=list(range(8)))
    full = own_tokens_to_full(res.results, "out")
    return np.ascontiguousarray(full.astype(np.float32))


# version 08
# speedup vs baseline: 1.2369x; 1.0315x over previous
import contextlib
import numpy as np
import concourse.bass as bass
import concourse.mybir as mybir

F32 = mybir.dt.float32
BF16 = mybir.dt.bfloat16
ALU = mybir.AluOpType
AF = mybir.ActivationFunctionType

DM = 2048
DH = 2048
TOWN = 1024
L = 2048
S = 16
NCH = 4
LC = 512
BLK = 256
KT = 16
DTl = 16
HT = 64
NE = 8
HPE = 8
EPS = 1e-5
PAIRS = [[0, 1], [2, 3], [4, 5], [6, 7]]


def build(stage="full", taps=()):
    nc = bass.Bass("TRN2", target_bir_lowering=False, debug=False, detect_race_conditions=False)
    ctx = contextlib.ExitStack()

    dr = {}
    def din(name, shape):
        dr[name] = nc.dram_tensor(name, list(shape), F32, kind="ExternalInput").ap()
    din("x_own", (DM, TOWN))
    din("w_inx", (128, DTl * DM))
    din("w_ing", (128, DTl * DM))
    din("w_out", (128, KT * DH))
    din("w_xp", (128, KT * 160))
    din("w_dtp", (128, DH))
    din("w_g", (128, HT * DM))
    din("w_u", (128, HT * DM))
    din("w_d", (128, NE * KT * HPE * 128))
    din("A_sb", (128, DTl * S))
    din("cw_sb", (128, DTl * 4))
    din("cb_sb", (128, DTl))
    din("D_sb", (128, DTl))
    din("dtb_sb", (128, DTl))
    din("alpha", (128, 2))
    out = nc.dram_tensor("out", [DM, TOWN], F32, kind="ExternalOutput").ap()
    tap_out = {}
    for tname, tshape in taps:
        tap_out[tname] = nc.dram_tensor(tname, list(tshape), F32, kind="ExternalOutput").ap()

    def scr(name, shape, dt=F32):
        return nc.dram_tensor(name, list(shape), dt, kind="Internal").ap()
    gu_bf = scr("gu_bf", (128, 2 * HT * DM), BF16)
    d_bf = scr("d_bf", (128, NE * KT * HPE * 128), BF16)
    inx_bf = scr("inx_bf", (128, DTl * DM), BF16)
    ing_bf = scr("ing_bf", (128, DTl * DM), BF16)
    outp_bf = scr("outp_bf", (128, KT * DH), BF16)
    xres_h = scr("xres_h", (DM, TOWN))
    xt_hbm = scr("xt_hbm", (DTl, 128, L))
    ag_in = [scr(f"ag_in{h}", (1, DM, LC), BF16) for h in range(2)]
    ag_out = [scr(f"ag_out{h}", (2, DM, LC), BF16) for h in range(2)]
    ts_in = [scr(f"ts_in{c}", (160, LC)) for c in range(NCH)]
    ts_out = [scr(f"ts_out{c}", (160, LC)) for c in range(NCH)]
    op_in = [scr(f"op_in{c}", (DM, LC)) for c in range(NCH)]
    op_out = [scr(f"op_out{c}", (DM, LC)) for c in range(NCH)]
    state_snap = scr("state_snap", (2, 128, 2 * DTl * S))

    sb = {}
    def sbuf(name, shape, dt=F32):
        sb[name] = ctx.enter_context(nc.sbuf_tensor(name, list(shape), dt))
        return sb[name]

    ones = sbuf("ones", (128, 2))
    epsc = sbuf("epsc", (128, 1))
    onesr = sbuf("onesr", (1, 128))
    rows = sbuf("rows", (1, 2 * LC))
    A_s = sbuf("A_s", (128, DTl * S))
    cw_s = sbuf("cw_s", (128, DTl * 4))
    cb_s = sbuf("cb_s", (128, DTl))
    D_s = sbuf("D_s", (128, DTl))
    dtb_s = sbuf("dtb_s", (128, DTl))
    al_s = sbuf("al_s", (128, 2))
    xpT = sbuf("xpT", (128, KT * 160), BF16)
    dtpT = sbuf("dtpT", (128, DH), BF16)
    state = sbuf("state", (128, 2 * DTl * S))
    carry = sbuf("carry", (128, DTl * 4))
    wsh = sbuf("wsh", (128, 8192), BF16)
    wstrip = [wsh[:, i * 2048:(i + 1) * 2048] for i in range(4)]
    wd_st = [wsh[:, i * 4096:(i + 1) * 4096] for i in range(2)]
    wgu_st = [sbuf(f"wgu_st{i}", (128, 8192), BF16) for i in range(2)]
    arena = sbuf("arena", (128, 12288))
    acc_f = arena[:, 0:8192]
    gated = arena[:, 8192:10240].bitcast(BF16)
    xh2 = arena[:, 0:4096].bitcast(BF16)
    xt_bf = arena[:, 4096:8192].bitcast(BF16)
    yf_t = arena[:, 8192:12288].bitcast(BF16)
    cbuf = [arena[:, j * 2048:(j + 1) * 2048].bitcast(BF16) for j in range(6)]
    xh_f = sbuf("xh_f", (128, KT * LC), BF16)
    sq_t = sbuf("sq_t", (128, LC))
    ln_tmp = sbuf("ln_tmp", (128, LC))
    mkt = sbuf("mkt", (128, 8))
    res_c = arena[:, 0:KT * BLK]
    res_sl = sbuf("res_sl", (128, 2 * LC))
    cacc = sq_t[:, 0:LC]
    xt_t = sbuf("xt_t", (128, LC))
    ts_bf = sbuf("ts_bf", (128, LC), BF16)
    bcflat = sbuf("bcflat", (1, 4 * LC))
    evac_t = sbuf("evac_t", (128, LC))
    dt_q = sbuf("dt_q", (128, 4 * LC))
    et_t = ln_tmp
    xt_sl = sbuf("xt_sl", (128, 2 * LC))
    u_q = sbuf("u_q", (128, 4 * LC))
    yac_q = sbuf("yac_q", (128, 4 * LC))
    SCW = LC + 8
    dA_t = sbuf("dA_t", (128, 6 * SCW))
    dBx4 = sbuf("dBx4", (128, 4 * (LC + 1)))
    so4 = sbuf("so4", (128, 4 * (LC + 1)))
    xcin = so4[:, 0:3 + LC + 1]
    opart_sl = [evac_t, xt_t]
    bc_sb = xt_t[0:32, 0:LC]

    p_mm = ctx.enter_context(nc.psum_tensor("p_mm", [128, 1024], F32))
    p_bcB = ctx.enter_context(nc.psum_tensor("p_bcB", [128, 1024], F32))
    p_bcC = ctx.enter_context(nc.psum_tensor("p_bcC", [128, 1024], F32))
    p_f = ctx.enter_context(nc.psum_tensor("p_f", [128, 1024], F32))

    block = ctx.enter_context(nc.Block())

    class Sem:
        def __init__(self, name):
            self.h = ctx.enter_context(nc.semaphore(name))
            self.v = 0

    class DSem(Sem):
        def inc(self, inst):
            inst.then_inc(self.h, 16)
            self.v += 16
        def wait(self, e):
            e.wait_ge(self.h, self.v)

    sT = Sem("sT"); sV = Sem("sV"); sS = Sem("sS")
    d_small = DSem("d_small")
    d_smallg = DSem("d_smallg")
    d_cast = [DSem(f"d_cast{i}") for i in range(6)]
    d_cstore = [DSem(f"d_cstore{i}") for i in range(6)]
    d_strip = [DSem(f"d_strip{i}") for i in range(4)]
    d_wg = [DSem(f"d_wg{i}") for i in range(2)]
    d_wu = [DSem(f"d_wu{i}") for i in range(2)]
    d_wd = [DSem(f"d_wd{i}") for i in range(2)]
    d_xh2 = DSem("d_xh2")
    d_xt_st = DSem("d_xt_st")
    d_xt_ld = [DSem(f"d_xt_ld{i}") for i in range(2)]
    d_ts_st = DSem("d_ts_st")
    d_tsbf = DSem("d_tsbf")
    d_bcf = DSem("d_bcf")
    d_op_st = DSem("d_op_st")
    d_op_ld = [DSem(f"d_op_ld{i}") for i in range(2)]
    d_res = [DSem(f"d_res{i}") for i in range(2)]
    d_xmid_st = DSem("d_xmid_st")
    d_stres = DSem("d_stres")
    d_ldx = DSem("d_ldx")
    d_ag_st = DSem("d_ag_st")
    sO = DSem("sO")
    c_ag = [Sem(f"c_ag{i}") for i in range(2)]
    c_ts = [Sem(f"c_ts{i}") for i in range(NCH)]
    c_op = [Sem(f"c_op{i}") for i in range(NCH)]
    ALL_DSEMS = ([d_small, d_smallg, d_xh2, d_xt_st, d_ts_st, d_tsbf, d_bcf, d_op_st,
                  d_xmid_st, d_stres, d_ldx, d_ag_st, sO]
                 + d_cast + d_cstore + d_strip + d_wg + d_wu + d_wd
                 + d_xt_ld + d_op_ld + d_res)
    ALL_CSEMS = c_ag + c_ts + c_op

    def dma4(e, sem, out_ap, in_ap):
        for qq in range(4):
            sem.inc(e.dma_start(out=out_ap[qq * 32:(qq + 1) * 32], in_=in_ap[qq * 32:(qq + 1) * 32]))

    def barrier():
        def t(e): e.drain().then_inc(sT.h, 1)
        block.tensor(t); sT.v += 1
        def v(e): e.drain().then_inc(sV.h, 1)
        block.vector(v); sV.v += 1
        def s(e): e.drain().then_inc(sS.h, 1)
        block.scalar(s); sS.v += 1
        vals = [(sT, sT.v), (sV, sV.v), (sS, sS.v)] + \
               [(d, d.v) for d in ALL_DSEMS] + [(cs, cs.v) for cs in ALL_CSEMS]
        def waiter(e):
            for sm, vl in vals:
                e.wait_ge(sm.h, vl)
        for emit in (block.tensor, block.vector, block.scalar, block.gpsimd, block.sync):
            emit(waiter)

    def tap(name, src_dram):
        if name not in tap_out:
            return
        def t(e):
            sO.inc(e.dma_start(out=tap_out[name], in_=src_dram))
        block.gpsimd(t)

    def finish():
        ov = sO.v
        def fin(e, ov=ov):
            e.wait_ge(sO.h, ov)
        block.gpsimd(fin)
        return nc, ctx, tap_out

    def kt_rows(dram2d, i):
        return dram2d.rearrange("(kt p) t -> kt p t", p=128)[i]

    def _init_sync(e):
        d_small.inc(e.dma_start(out=A_s[:], in_=dr["A_sb"]))
        d_small.inc(e.dma_start(out=cw_s[:], in_=dr["cw_sb"]))
        d_small.inc(e.dma_start(out=cb_s[:], in_=dr["cb_sb"]))
        d_small.inc(e.dma_start(out=D_s[:], in_=dr["D_sb"]))
        d_small.inc(e.dma_start(out=dtb_s[:], in_=dr["dtb_sb"]))
        d_small.inc(e.dma_start(out=al_s[:], in_=dr["alpha"]))
        d_small.inc(e.dma_start(out=xres_h[:, :], in_=dr["x_own"]))
    block.sync(_init_sync)

    def _init_gps(e):
        d_smallg.inc(e.dma_start(out=xpT[:], in_=dr["w_xp"]))
        d_smallg.inc(e.dma_start(out=dtpT[:], in_=dr["w_dtp"]))
    block.gpsimd(_init_gps)

    def _init_vec(e):
        e.memset(ones[:], 1.0)
        e.memset(epsc[:], EPS)
        e.memset(mkt[:], 1.0)
        e.memset(onesr[:], 1.0)
        e.memset(state[:], 0.0)
        e.memset(dA_t[:], 0.0)
        e.memset(carry[:], 0.0).then_inc(sV.h, 1)
    block.vector(_init_vec); sV.v += 1

    CW = 4096
    CAST_JOBS = []
    for w0 in range(0, HT * DM, CW):
        CAST_JOBS.append((dr["w_g"][:, w0:w0 + CW], gu_bf[:, 2 * w0:2 * w0 + CW]))
        CAST_JOBS.append((dr["w_u"][:, w0:w0 + CW], gu_bf[:, 2 * w0 + CW:2 * w0 + 2 * CW]))
    for cls_in, cls_out in (("w_d", d_bf), ("w_inx", inx_bf), ("w_ing", ing_bf), ("w_out", outp_bf)):
        total = dr[cls_in].shape[-1]
        assert total % CW == 0
        for w0 in range(0, total, CW):
            CAST_JOBS.append((dr[cls_in][:, w0:w0 + CW], cls_out[:, w0:w0 + CW]))

    for j, (srca, dsta) in enumerate(CAST_JOBS):
        slot = j % 6
        def gld(e, srca=srca, slot=slot):
            d_cstore[slot].wait(e)
            d_cast[slot].inc(e.dma_start(out=cbuf[slot][:, 0:CW], in_=srca))
        block.gpsimd(gld)
        def wst(e, dsta=dsta, slot=slot):
            d_cast[slot].wait(e)
            d_cstore[slot].inc(e.dma_start(out=dsta, in_=cbuf[slot][:, 0:CW]))
        block.sync(wst)

    barrier()

    def layernorm(src_fn, dst_fn, n, ready):
        T0 = sT.v
        S0ln = sS.v
        slots = [sq_t[:, 0:n], xt_t[:, 0:n], evac_t[:, 0:n]]
        for kt in range(KT + 1):
            if kt < KT:
                def sq(e, kt=kt, ready=ready, T0=T0):
                    if kt == 0:
                        for w in ready:
                            w(e)
                    if kt >= 3:
                        e.wait_ge(sT.h, T0 + kt - 2)
                    e.activation(slots[kt % 3], src_fn(kt), AF.Square,
                                 bias=0.0, scale=1.0).then_inc(sS.h, 1)
                block.scalar(sq); sS.v += 1
            else:
                def sq(e):
                    e.activation(mkt[:, 0:8], mkt[:, 0:8], AF.Square, bias=0.0, scale=1.0).then_inc(sS.h, 1)
                block.scalar(sq); sS.v += 1
            if kt >= 1:
                ktm = kt - 1
                sv = S0ln + kt + 1
                def mm(e, ktm=ktm, sv=sv, ready=ready):
                    e.wait_ge(sS.h, sv)
                    if ktm == 0:
                        for w in ready:
                            w(e)
                    e.matmul(p_mm[0:1, :n], ones[:, 0:1], src_fn(ktm), start=(ktm == 0), stop=(ktm == KT - 1))
                    e.matmul(p_mm[0:1, 512:512 + n], ones[:, 0:1], slots[ktm % 3],
                             start=(ktm == 0), stop=(ktm == KT - 1)).then_inc(sT.h, 1)
                block.tensor(mm); sT.v += 1
        tv = sT.v
        def rowmath(e, tv=tv, n=n):
            e.wait_ge(sT.h, tv)
            m = rows[:, 0:n]; v = rows[:, LC:LC + n]
            e.tensor_scalar_mul(m, p_mm[0:1, :n], 1.0 / DM)
            e.tensor_tensor(v, m, m, ALU.mult)
            e.scalar_tensor_tensor(v, p_mm[0:1, 512:512 + n], 1.0 / DM, v, ALU.mult, ALU.subtract)
            e.tensor_scalar_add(v, v, EPS).then_inc(sV.h, 1)
        block.vector(rowmath); sV.v += 1
        vv = sV.v
        def rowsqrt(e, vv=vv, n=n):
            e.wait_ge(sV.h, vv)
            e.activation(sq_t[0:1, 0:n], rows[:, LC:LC + n], AF.Sqrt, bias=0.0, scale=1.0).then_inc(sS.h, 1)
        block.scalar(rowsqrt); sS.v += 1
        def mkrr(e):
            e.activation(mkt[:, 0:8], mkt[:, 0:8], AF.Square, bias=0.0, scale=1.0).then_inc(sS.h, 1)
        block.scalar(mkrr); sS.v += 1
        sv2 = sS.v
        def rowrec(e, sv2=sv2, n=n):
            e.wait_ge(sS.h, sv2)
            v = rows[:, LC:LC + n]
            r = sq_t[0:1, 0:n]
            nm = xt_t[0:1, 0:n]
            e.reciprocal(r, r)
            e.tensor_tensor(nm, r, r, ALU.mult)
            e.tensor_tensor(nm, nm, v, ALU.mult)
            e.tensor_scalar(nm, nm, -0.5, 1.5, ALU.mult, ALU.add)
            e.tensor_tensor(r, r, nm, ALU.mult)
            e.scalar_tensor_tensor(nm, rows[:, 0:n], -1.0, r, ALU.mult, ALU.mult).then_inc(sV.h, 1)
        block.vector(rowrec); sV.v += 1
        vv2 = sV.v
        scur = sS.v
        def bcast(e, vv2=vv2, n=n, scur=scur):
            e.wait_ge(sV.h, vv2)
            e.wait_ge(sS.h, scur)
            e.matmul(p_bcB[:, :n], onesr[:], sq_t[0:1, 0:n], start=True, stop=True)
            e.matmul(p_bcC[:, :n], onesr[:], xt_t[0:1, 0:n], start=True, stop=True).then_inc(sT.h, 1)
        block.tensor(bcast); sT.v += 1
        tv2 = sT.v
        def apply_(e, tv2=tv2, n=n):
            e.wait_ge(sT.h, tv2)
            for kt in range(KT):
                e.tensor_tensor(ln_tmp[:, :n], src_fn(kt), p_bcB[:, :n], ALU.mult)
                i_ = e.tensor_tensor(dst_fn(kt), ln_tmp[:, :n], p_bcC[:, :n], ALU.add)
                if kt == KT - 1:
                    i_.then_inc(sV.h, 1)
        block.vector(apply_); sV.v += 1

    def ffn(n, res_cols, is_ffn2):
        V_F0 = sV.v
        S_F0 = sS.v
        sil_cnt = {}
        gm_cnt = {}
        ac_end_prev = V_F0
        def z(e):
            d_stres.wait(e)
            e.memset(acc_f[:, : KT * n], 0.0).then_inc(sV.h, 1)
        block.vector(z); sV.v += 1
        gu_cnt = {}
        dn_grp_cnt = {}
        for e8 in range(NE):
            for hh in range(HPE):
                ht = e8 * HPE + hh
                slot = (ht // 2) % 2
                sub = (ht % 2) * KT * 128
                pair = [p_f, p_mm][ht % 2]
                sgsl = [ln_tmp, evac_t][ht % 2]
                if ht % 2 == 0:
                    tw = gu_cnt.get(ht - 3, sT.v)
                    def ld(e, ht=ht, slot=slot, tw=tw):
                        e.wait_ge(sT.h, tw)
                        d_wg[slot].inc(e.dma_start(out=wgu_st[slot][:],
                                                   in_=gu_bf[:, ht * 4096:(ht + 2) * 4096]))
                    block.gpsimd(ld)
                if hh == 0 and e8 > 0:
                    vv_need = ac_end_prev
                else:
                    vv_need = gm_cnt[ht - 2] if ht >= 2 else V_F0
                sv_need = sil_cnt[ht - 2] if ht >= 2 else S_F0
                def gu(e, slot=slot, sub=sub, pair=pair, vv_need=vv_need, sv_need=sv_need, n=n):
                    d_wg[slot].wait(e)
                    e.wait_ge(sV.h, vv_need)
                    e.wait_ge(sS.h, sv_need)
                    for k in range(KT):
                        e.matmul(pair[:, 0:n], wgu_st[slot][:, sub + k * 128:sub + (k + 1) * 128],
                                 xh_f[:, k * LC:k * LC + n], start=(k == 0), stop=(k == KT - 1))
                    for k in range(KT):
                        i_ = e.matmul(pair[:, 512:512 + n], wgu_st[slot][:, 4096 + sub + k * 128:4096 + sub + (k + 1) * 128],
                                      xh_f[:, k * LC:k * LC + n], start=(k == 0), stop=(k == KT - 1))
                        if k == KT - 1:
                            i_.then_inc(sT.h, 1)
                block.tensor(gu); sT.v += 1
                gu_cnt[ht] = sT.v
                tv = sT.v
                vprot = gm_cnt[ht - 2] if ht >= 2 else V_F0
                def sil(e, tv=tv, n=n, vprot=vprot, pair=pair, sgsl=sgsl):
                    e.wait_ge(sT.h, tv)
                    e.wait_ge(sV.h, vprot)
                    e.activation(sgsl[:, :n], pair[:, 0:n], AF.Silu, bias=0.0, scale=1.0).then_inc(sS.h, 1)
                block.scalar(sil); sS.v += 1
                sil_cnt[ht] = sS.v
                sv = sil_cnt[ht] + 1
                def gm(e, hh=hh, sv=sv, n=n, pair=pair, sgsl=sgsl):
                    e.wait_ge(sS.h, sv)
                    e.tensor_tensor(gated[:, hh * n:(hh + 1) * n], sgsl[:, :n], pair[:, 512:512 + n], ALU.mult).then_inc(sV.h, 1)
                block.vector(gm); sV.v += 1
                gm_cnt[ht] = sV.v
            def mkrf(e):
                e.activation(mkt[:, 0:8], mkt[:, 0:8], AF.Square, bias=0.0, scale=1.0).then_inc(sS.h, 1)
            block.scalar(mkrf); sS.v += 1
            for dmt in range(KT):
                slot = (dmt // 4) % 2
                sub = (dmt % 4) * HPE * 128
                if dmt % 4 == 0:
                    grp = e8 * 4 + dmt // 4
                    tw = dn_grp_cnt.get(grp - 2, sT.v)
                    def ldd(e, e8=e8, dmt=dmt, slot=slot, tw=tw):
                        e.wait_ge(sT.h, tw)
                        base = (e8 * KT + dmt) * HPE * 128
                        d_wd[slot].inc(e.dma_start(out=wd_st[slot][:],
                                                   in_=d_bf[:, base:base + 4 * HPE * 128]))
                    block.gpsimd(ldd)
                vv_need = sV.v
                sv_need = sS.v
                def dn(e, slot=slot, sub=sub, vv_need=vv_need, sv_need=sv_need, n=n):
                    d_wd[slot].wait(e)
                    e.wait_ge(sV.h, vv_need)
                    e.wait_ge(sS.h, sv_need)
                    for h in range(HPE):
                        i_ = e.matmul(p_f[:, 0:n], wd_st[slot][:, sub + h * 128:sub + (h + 1) * 128],
                                      gated[:, h * n:(h + 1) * n], start=(h == 0), stop=(h == HPE - 1))
                        if h == HPE - 1:
                            i_.then_inc(sT.h, 1)
                block.tensor(dn); sT.v += 1
                if dmt % 4 == 3:
                    dn_grp_cnt[e8 * 4 + dmt // 4] = sT.v
                tv = sT.v
                def ac(e, dmt=dmt, tv=tv, n=n):
                    e.wait_ge(sT.h, tv)
                    e.tensor_tensor(acc_f[:, dmt * n:(dmt + 1) * n], p_f[:, 0:n],
                                    acc_f[:, dmt * n:(dmt + 1) * n], ALU.add).then_inc(sV.h, 1)
                block.vector(ac); sV.v += 1
            ac_end_prev = sV.v
        for dmt in range(KT):
            slot = dmt % 2
            vcur = sV.v
            def ldr(e, dmt=dmt, slot=slot, res_cols=res_cols, n=n, vcur=vcur):
                e.wait_ge(sV.h, vcur)
                d_stres.wait(e)
                d_xmid_st.wait(e)
                d_res[slot].inc(e.dma_start(
                    out=res_sl[:, slot * LC:slot * LC + n],
                    in_=kt_rows(xres_h, dmt)[:, res_cols:res_cols + n]))
            block.gpsimd(ldr)
            def upd(e, dmt=dmt, slot=slot, n=n):
                d_res[slot].wait(e)
                e.scalar_tensor_tensor(res_sl[:, slot * LC:slot * LC + n],
                                       acc_f[:, dmt * n:(dmt + 1) * n], 0.5,
                                       res_sl[:, slot * LC:slot * LC + n], ALU.mult, ALU.add).then_inc(sV.h, 1)
            block.vector(upd); sV.v += 1
            vv = sV.v
            if is_ffn2:
                def st(e, dmt=dmt, slot=slot, vv=vv, res_cols=res_cols, n=n):
                    e.wait_ge(sV.h, vv)
                    sO.inc(e.dma_start(out=kt_rows(out, dmt)[:, res_cols:res_cols + n],
                                       in_=res_sl[:, slot * LC:slot * LC + n]))
                block.gpsimd(st)
            else:
                def st(e, dmt=dmt, slot=slot, vv=vv, res_cols=res_cols, n=n):
                    e.wait_ge(sV.h, vv)
                    d_xmid_st.inc(e.dma_start(out=kt_rows(xres_h, dmt)[:, res_cols:res_cols + n],
                                              in_=res_sl[:, slot * LC:slot * LC + n]))
                block.gpsimd(st)

    for h in range(2):
        base = h * LC
        def ldx(e, base=base):
            d_xmid_st.wait(e)
            d_ldx.inc(e.dma_start(
                out=acc_f[:, :KT * LC].rearrange("p (kt t) -> p kt t", t=LC),
                in_=xres_h.rearrange("(kt p) t -> p kt t", p=128)[:, :, base:base + LC]))
        block.gpsimd(ldx)
        layernorm(lambda kt: acc_f[:, kt * LC:(kt + 1) * LC],
                  lambda kt: xh_f[:, kt * LC:(kt + 1) * LC], LC,
                  [lambda e: d_ldx.wait(e)])
        ffn(LC, base, False)
        barrier()
    tap("tap_xmid", xres_h)
    if stage == "ffn1":
        return finish()

    for h in range(2):
        base = h * LC
        def ldx2(e, base=base):
            d_xmid_st.wait(e)
            d_ldx.inc(e.dma_start(
                out=acc_f[:, :KT * LC].rearrange("p (kt t) -> p kt t", t=LC),
                in_=xres_h.rearrange("(kt p) t -> p kt t", p=128)[:, :, base:base + LC]))
        block.gpsimd(ldx2)
        layernorm(lambda kt: acc_f[:, kt * LC:(kt + 1) * LC],
                  lambda kt: xh_f[:, kt * LC:(kt + 1) * LC], LC,
                  [lambda e: d_ldx.wait(e)])
        vv = sV.v
        def stag(e, h=h, vv=vv):
            e.wait_ge(sV.h, vv)
            d_ag_st.inc(e.dma_start(out=ag_in[h][0].rearrange("(kt p) t -> p kt t", p=128),
                                    in_=xh_f[:].rearrange("p (kt t) -> p kt t", t=LC)))
        block.gpsimd(stag)
        def agc(e, h=h):
            d_ag_st.wait(e)
            e.collective_compute("AllGather", ALU.bypass, replica_groups=PAIRS,
                                 ins=[ag_in[h]], outs=[ag_out[h]]).then_inc(c_ag[h].h, 1)
        block.gpsimd(agc); c_ag[h].v += 1
        barrier()

    for c in range(NCH):
        hf = c // 2
        coff = (c % 2) * BLK
        tcur0 = sT.v
        def ldxh2(e, hf=hf, coff=coff, tcur0=tcur0):
            e.wait_ge(c_ag[hf].h, c_ag[hf].v)
            e.wait_ge(sT.h, tcur0)
            for slot in range(2):
                d_xh2.inc(e.dma_start(
                    out=xh2.rearrange("p (kt t) -> p kt t", t=LC)[:, :, slot * BLK:(slot + 1) * BLK],
                    in_=ag_out[hf][slot].rearrange("(kt p) t -> p kt t", p=128)[:, :, coff:coff + BLK]))
        block.gpsimd(ldxh2)

        S_A0 = sS.v
        T_A0 = sT.v
        for d in range(DTl):
            slot = d % 2
            wsl = d % 4
            tw = T_A0 + d - 3 if d >= 4 else sT.v
            def lds(e, d=d, wsl=wsl, tw=tw):
                e.wait_ge(sT.h, tw)
                d_strip[wsl].inc(e.dma_start(out=wstrip[wsl][:], in_=inx_bf[:, d * DM:(d + 1) * DM]))
            block.sync(lds)
            scur = sS.v
            def mmx(e, slot=slot, wsl=wsl, scur=scur):
                d_strip[wsl].wait(e)
                d_xh2.wait(e)
                e.wait_ge(sS.h, scur)
                for k in range(KT):
                    i_ = e.matmul(p_mm[:, slot * 512:slot * 512 + LC], wstrip[wsl][:, k * 128:(k + 1) * 128],
                                  xh2[:, k * LC:(k + 1) * LC], start=(k == 0), stop=(k == KT - 1))
                    if k == KT - 1:
                        i_.then_inc(sT.h, 1)
            block.tensor(mmx); sT.v += 1
            assert sT.v == T_A0 + d + 1
            tv = sT.v
            vcur = sV.v
            def evx(e, slot=slot, tv=tv, vcur=vcur):
                e.wait_ge(sT.h, tv)
                e.wait_ge(sV.h, vcur)
                e.copy(xcin[:, 3:3 + LC], p_mm[:, slot * 512:slot * 512 + LC]).then_inc(sS.h, 1)
            block.scalar(evx); sS.v += 1
            def mkre(e):
                e.activation(mkt[:, 0:8], mkt[:, 0:8], AF.Square, bias=0.0, scale=1.0).then_inc(sS.h, 1)
            block.scalar(mkre); sS.v += 1
            sv = sS.v
            def conv(e, d=d, sv=sv):
                e.wait_ge(sS.h, sv)
                e.tensor_copy(xcin[:, 0:3], carry[:, d * 4:d * 4 + 3])
                e.tensor_copy(carry[:, d * 4:d * 4 + 3], xcin[:, LC:LC + 3])
                e.tensor_scalar_mul(cacc, xcin[:, 0:LC], cw_s[:, d * 4:d * 4 + 1])
                for k in range(1, 4):
                    i_ = e.scalar_tensor_tensor(cacc, xcin[:, k:k + LC], cw_s[:, d * 4 + k:d * 4 + k + 1],
                                                cacc, ALU.mult, ALU.add)
                    if k == 3:
                        i_.then_inc(sV.h, 1)
            block.vector(conv); sV.v += 1
            vv = sV.v
            def silu_x(e, d=d, vv=vv):
                e.wait_ge(sV.h, vv)
                d_xt_st.wait(e)
                e.activation(xt_t[:], cacc, AF.Silu, bias=cb_s[:, d:d + 1], scale=1.0)
                e.copy(xt_bf[:, d * LC:(d + 1) * LC], xt_t[:]).then_inc(sS.h, 1)
            block.scalar(silu_x); sS.v += 1
            sv2 = S_A0 + (3 * d + 4 if d < DTl - 1 else 3 * DTl + 1)
            def stxt(e, d=d, c=c, sv2=sv2):
                e.wait_ge(sS.h, sv2)
                d_xt_st.inc(e.dma_start(out=xt_hbm[d][:, c * LC:(c + 1) * LC], in_=xt_t[:]))
            block.gpsimd(stxt)
        def mkra(e):
            e.activation(mkt[:, 0:8], mkt[:, 0:8], AF.Square, bias=0.0, scale=1.0).then_inc(sS.h, 1)
        block.scalar(mkra); sS.v += 1
        assert sS.v == S_A0 + 3 * DTl + 1
        s_xtbf = sS.v

        vcur = sV.v
        def xproj(e, s_xtbf=s_xtbf, vcur=vcur):
            e.wait_ge(sS.h, s_xtbf)
            e.wait_ge(sV.h, vcur)
            for k in range(DTl):
                e.matmul(p_mm[:, 0:LC], xpT[:, k * 160:k * 160 + 128],
                         xt_bf[:, k * LC:(k + 1) * LC], start=(k == 0), stop=(k == DTl - 1))
            for k in range(DTl):
                i_ = e.matmul(p_bcB[0:32, 0:LC], xpT[:, k * 160 + 128:(k + 1) * 160],
                              xt_bf[:, k * LC:(k + 1) * LC], start=(k == 0), stop=(k == DTl - 1))
                if k == DTl - 1:
                    i_.then_inc(sT.h, 1)
        block.tensor(xproj); sT.v += 1
        tv = sT.v
        def evts(e, tv=tv):
            e.wait_ge(sT.h, tv)
            d_xt_st.wait(e)
            e.copy(evac_t[:], p_mm[:, 0:LC])
            e.copy(bc_sb[:], p_bcB[0:32, 0:LC]).then_inc(sS.h, 1)
        block.scalar(evts); sS.v += 1
        def mkrt(e):
            e.activation(mkt[:, 0:8], mkt[:, 0:8], AF.Square, bias=0.0, scale=1.0).then_inc(sS.h, 1)
        block.scalar(mkrt); sS.v += 1
        sv = sS.v
        def sts(e, c=c, sv=sv):
            e.wait_ge(sS.h, sv)
            d_ts_st.inc(e.dma_start(out=ts_in[c][0:128, :], in_=evac_t[:]))
            d_ts_st.inc(e.dma_start(out=ts_in[c][128:160, :], in_=bc_sb[:]))
        block.gpsimd(sts)
        def arc(e, c=c):
            d_ts_st.wait(e)
            e.collective_compute("AllReduce", ALU.add, replica_groups=PAIRS,
                                 ins=[ts_in[c]], outs=[ts_out[c]]).then_inc(c_ts[c].h, 1)
        block.gpsimd(arc); c_ts[c].v += 1
        def ldts(e, c=c):
            e.wait_ge(c_ts[c].h, c_ts[c].v)
            d_tsbf.inc(e.dma_start(out=ts_bf[:], in_=ts_out[c][0:128, :]))
        block.gpsimd(ldts)

        if stage == "dbg3" and c == 1:
            def snap1(e):
                sO.inc(e.dma_start(out=state_snap[1], in_=state[:]))
            block.gpsimd(snap1)
        QD = 4
        v_prep_ok = sV.v
        gmm_cnt = {}
        for q in range(DTl // QD):
            S_prep0 = sS.v
            for dq in range(QD):
                d = q * QD + dq
                dsl = d % 2
                scur = sS.v
                def dtmm(e, d=d, scur=scur):
                    d_tsbf.wait(e)
                    e.wait_ge(sS.h, scur)
                    e.matmul(p_mm[:, 512:512 + LC], dtpT[:, d * 128:(d + 1) * 128], ts_bf[:],
                             start=True, stop=True).then_inc(sT.h, 1)
                block.tensor(dtmm); sT.v += 1
                tv = sT.v
                def softp(e, dq=dq, d=d, tv=tv, vfix=v_prep_ok):
                    e.wait_ge(sT.h, tv)
                    e.wait_ge(sV.h, vfix)
                    e.activation(et_t[:, 0:LC], p_mm[:, 512:512 + LC], AF.Exp, bias=dtb_s[:, d:d + 1], scale=1.0)
                    e.activation(dt_q[:, dq * LC:(dq + 1) * LC], et_t[:, 0:LC], AF.Ln, bias=1.0, scale=1.0).then_inc(sS.h, 1)
                block.scalar(softp); sS.v += 1
                assert sS.v == S_prep0 + dq + 1
                vcur = sV.v
                def ldxt(e, d=d, c=c, dsl=dsl, vcur=vcur):
                    e.wait_ge(sV.h, vcur)
                    d_xt_st.wait(e)
                    d_xt_ld[dsl].inc(e.dma_start(
                        out=xt_sl[:, dsl * LC:(dsl + 1) * LC],
                        in_=xt_hbm[d][:, c * LC:(c + 1) * LC]))
                block.gpsimd(ldxt)
                s_dt = S_prep0 + dq + 2
                def uprep(e, dq=dq, d=d, dsl=dsl, s_dt=s_dt):
                    e.wait_ge(sS.h, s_dt)
                    d_xt_ld[dsl].wait(e)
                    e.tensor_tensor(u_q[:, dq * LC:(dq + 1) * LC], dt_q[:, dq * LC:(dq + 1) * LC],
                                    xt_sl[:, dsl * LC:(dsl + 1) * LC], ALU.mult)
                    e.tensor_scalar_mul(yac_q[:, dq * LC:(dq + 1) * LC],
                                        xt_sl[:, dsl * LC:(dsl + 1) * LC], D_s[:, d:d + 1]).then_inc(sV.h, 1)
                block.vector(uprep); sV.v += 1
            def mkr2(e):
                e.activation(mkt[:, 0:8], mkt[:, 0:8], AF.Square, bias=0.0, scale=1.0).then_inc(sS.h, 1)
            block.scalar(mkr2); sS.v += 1
            assert sS.v == S_prep0 + QD + 1
            V0 = sV.v
            S0 = sS.v
            T0 = sT.v
            NREF = 8
            for r in range(NREF):
                def refill(e, c=c, r=r, T0=T0, tcur_r=sT.v):
                    e.wait_ge(c_ts[c].h, c_ts[c].v)
                    e.wait_ge(sT.h, T0 + 4 * r if r >= 1 else tcur_r)
                    d_bcf.inc(e.dma_start(
                        out=bcflat[:, 0:2 * LC],
                        in_=ts_out[c][128 + 2 * r:130 + 2 * r, :].rearrange("a b -> (a b)").unsqueeze(0)))
                    d_bcf.inc(e.dma_start(
                        out=bcflat[:, 2 * LC:4 * LC],
                        in_=ts_out[c][144 + 2 * r:146 + 2 * r, :].rearrange("a b -> (a b)").unsqueeze(0)))
                block.gpsimd(refill)
                for i2 in range(2):
                    s_ = 2 * r + i2
                    def bca(e, s_=s_, i2=i2, V0=V0):
                        d_bcf.wait(e)
                        if s_ >= 2:
                            e.wait_ge(sV.h, V0 + QD * (s_ - 1))
                        else:
                            e.wait_ge(sV.h, V0)
                        e.matmul(p_bcB[:, (s_ % 2) * 512:(s_ % 2) * 512 + LC], onesr[:],
                                 bcflat[0:1, i2 * LC:(i2 + 1) * LC], start=True, stop=True)
                        e.matmul(p_bcC[:, (s_ % 2) * 512:(s_ % 2) * 512 + LC], onesr[:],
                                 bcflat[0:1, 2 * LC + i2 * LC:2 * LC + (i2 + 1) * LC], start=True, stop=True).then_inc(sT.h, 2)
                    block.tensor(bca); sT.v += 2
            def dA_all(e, q=q, V0=V0):
                for s_ in range(S):
                    for dq in range(QD):
                        d = q * QD + dq
                        j = s_ * QD + dq
                        e.wait_ge(sV.h, V0 + j - 5 if j >= 6 else V0)
                        e.activation(dA_t[:, (j % 6) * SCW + 1:(j % 6) * SCW + 1 + LC],
                                     dt_q[:, dq * LC:(dq + 1) * LC], AF.Exp,
                                     bias=0.0, scale=A_s[:, d * S + s_:d * S + s_ + 1]).then_inc(sS.h, 1)
                e.activation(mkt[:, 0:8], mkt[:, 0:8], AF.Square, bias=0.0, scale=1.0).then_inc(sS.h, 1)
            block.scalar(dA_all); sS.v += S * QD + 1
            def scan_all(e, q=q, c=c, V0=V0, S0=S0, T0=T0):
                W1 = LC + 1
                rd_c = c % 2
                wr_c = (c + 1) % 2
                state_r = state.rearrange("p (c2 d s) -> p c2 d s", c2=2, s=S)
                dBx_r = dBx4.rearrange("p (q2 w) -> p q2 w", w=W1)
                so_r = so4.rearrange("p (q2 w) -> p q2 w", w=W1)
                u_r = u_q[:, 0:QD * LC].rearrange("p (q2 w) -> p q2 w", w=LC)
                yac_r = yac_q[:, 0:QD * LC].rearrange("p (q2 w) -> p q2 w", w=LC)
                for s_ in range(S):
                    bsl = (s_ % 2) * 512
                    Bb = p_bcB[:, bsl:bsl + LC].unsqueeze(1).broadcast_to((128, QD, LC))
                    Cb = p_bcC[:, bsl:bsl + LC].unsqueeze(1).broadcast_to((128, QD, LC))
                    e.wait_ge(sT.h, T0 + 2 * min(s_ + 2, S))
                    e.wait_ge(sS.h, S0 + min(QD * s_ + QD + 1, S * QD + 1))
                    e.tensor_copy(dBx_r[:, :, 0:1], state_r[:, rd_c, q * QD:(q + 1) * QD, s_:s_ + 1])
                    e.tensor_tensor(dBx_r[:, :, 1:W1], u_r, Bb, ALU.mult)
                    for dq in range(QD):
                        j = s_ * QD + dq
                        i_ = e.tensor_tensor_scan(so4[:, dq * W1:(dq + 1) * W1],
                                                  dA_t[:, (j % 6) * SCW:(j % 6) * SCW + 1 + LC],
                                                  dBx4[:, dq * W1:(dq + 1) * W1], 0.0, ALU.mult, ALU.add)
                        if dq < QD - 1:
                            i_.then_inc(sV.h, 1)
                    e.tensor_tensor(dBx_r[:, :, 1:W1], so_r[:, :, 1:W1], Cb, ALU.mult)
                    i_ = e.tensor_tensor(yac_r, dBx_r[:, :, 1:W1], yac_r, ALU.add)
                    i_.then_inc(sV.h, 1)
                    e.tensor_copy(state_r[:, wr_c, q * QD:(q + 1) * QD, s_:s_ + 1],
                                  so_r[:, :, LC:W1])
            block.vector(scan_all); sV.v += S * QD
            v_scan = sV.v
            G_q0 = sS.v
            yfm_cnt_q = []
            for dq in range(QD):
                d = q * QD + dq
                dsl = d % 2
                wsl = dq
                sgsl = [ln_tmp, evac_t][dq % 2]
                tw = gmm_cnt.get((q - 1, dq), sT.v)
                def ldg(e, d=d, wsl=wsl, tw=tw):
                    e.wait_ge(sT.h, tw)
                    d_strip[wsl].inc(e.dma_start(out=wstrip[wsl][:], in_=ing_bf[:, d * DM:(d + 1) * DM]))
                block.sync(ldg)
                scur2 = sS.v
                def gmm(e, dsl=dsl, wsl=wsl, scur2=scur2, v_scan=v_scan):
                    d_strip[wsl].wait(e)
                    d_xh2.wait(e)
                    e.wait_ge(sS.h, scur2)
                    if dsl % 2 == 1:
                        e.wait_ge(sV.h, v_scan)
                    for k in range(KT):
                        i_ = e.matmul(p_mm[:, (dsl % 2) * 512:(dsl % 2) * 512 + LC],
                                      wstrip[wsl][:, k * 128:(k + 1) * 128],
                                      xh2[:, k * LC:(k + 1) * LC], start=(k == 0), stop=(k == KT - 1))
                        if k == KT - 1:
                            i_.then_inc(sT.h, 1)
                block.tensor(gmm); sT.v += 1
                gmm_cnt[(q, dq)] = sT.v
                tv = sT.v
                vfix = yfm_cnt_q[dq - 2] if dq >= 2 else v_prep_ok
                def sgx(e, dsl=dsl, tv=tv, vfix=vfix, sgsl=sgsl):
                    e.wait_ge(sT.h, tv)
                    e.wait_ge(sV.h, vfix)
                    e.activation(sgsl[:, :LC], p_mm[:, (dsl % 2) * 512:(dsl % 2) * 512 + LC],
                                 AF.Silu, bias=0.0, scale=1.0).then_inc(sS.h, 1)
                block.scalar(sgx); sS.v += 1
                assert sS.v == G_q0 + dq + 1
                sv = G_q0 + dq + 2
                def yfm(e, dq=dq, d=d, sv=sv, v_scan=v_scan, sgsl=sgsl):
                    e.wait_ge(sS.h, sv)
                    e.wait_ge(sV.h, v_scan)
                    e.tensor_tensor(yf_t[:, d * LC:(d + 1) * LC], yac_q[:, dq * LC:(dq + 1) * LC],
                                    sgsl[:, :LC], ALU.mult).then_inc(sV.h, 1)
                block.vector(yfm); sV.v += 1
                yfm_cnt_q.append(sV.v)
            def mkrg(e):
                e.activation(mkt[:, 0:8], mkt[:, 0:8], AF.Square, bias=0.0, scale=1.0).then_inc(sS.h, 1)
            block.scalar(mkrg); sS.v += 1
            assert sS.v == G_q0 + QD + 1
            v_prep_ok = sV.v
        v_yf = sV.v
        if stage == "dbg1" and c == 1 and "tap_yf" in tap_out:
            def tapyf(e, v_yf=v_yf):
                e.wait_ge(sV.h, v_yf)
                sO.inc(e.dma_start(out=tap_out["tap_yf"], in_=yf_t[:]))
                sO.inc(e.dma_start(out=tap_out["tap_ts"], in_=ts_out[1]))
            block.gpsimd(tapyf)

        S_C0 = sS.v
        op_base = d_op_st.v
        T_C0 = sT.v
        for i in range(KT):
            slot = i % 2
            wsl = i % 4
            tw = T_C0 + i - 3 if i >= 4 else sT.v
            def ldo(e, i=i, wsl=wsl, tw=tw):
                e.wait_ge(sT.h, tw)
                d_strip[wsl].inc(e.dma_start(out=wstrip[wsl][:], in_=outp_bf[:, i * DH:(i + 1) * DH]))
            block.sync(ldo)
            scur = sS.v
            def omm(e, slot=slot, wsl=wsl, v_yf=v_yf, scur=scur):
                d_strip[wsl].wait(e)
                e.wait_ge(sV.h, v_yf)
                e.wait_ge(sS.h, scur)
                for k in range(DTl):
                    i_ = e.matmul(p_mm[:, slot * 512:slot * 512 + LC], wstrip[wsl][:, k * 128:(k + 1) * 128],
                                  yf_t[:, k * LC:(k + 1) * LC], start=(k == 0), stop=(k == DTl - 1))
                    if k == DTl - 1:
                        i_.then_inc(sT.h, 1)
            block.tensor(omm); sT.v += 1
            assert sT.v == T_C0 + i + 1
            tv = sT.v
            opw = op_base + 16 * (i - 1) if i >= 2 else op_base
            def evo(e, slot=slot, tv=tv, opw=opw):
                e.wait_ge(sT.h, tv)
                e.wait_ge(d_op_st.h, opw)
                e.copy(opart_sl[slot][:, 0:LC], p_mm[:, slot * 512:slot * 512 + LC]).then_inc(sS.h, 1)
            block.scalar(evo); sS.v += 1
            assert sS.v == S_C0 + i + 1
            sv = S_C0 + i + 2
            def sto(e, i=i, c=c, slot=slot, sv=sv):
                e.wait_ge(sS.h, sv)
                d_op_st.inc(e.dma_start(out=op_in[c].rearrange("(kt p) t -> kt p t", p=128)[i],
                                        in_=opart_sl[slot][:, 0:LC]))
            block.gpsimd(sto)
        def mkro(e):
            e.activation(mkt[:, 0:8], mkt[:, 0:8], AF.Square, bias=0.0, scale=1.0).then_inc(sS.h, 1)
        block.scalar(mkro); sS.v += 1
        assert sS.v == S_C0 + KT + 1
        def aro(e, c=c):
            d_op_st.wait(e)
            e.collective_compute("AllReduce", ALU.add, replica_groups=PAIRS,
                                 ins=[op_in[c]], outs=[op_out[c]]).then_inc(c_op[c].h, 1)
        block.gpsimd(aro); c_op[c].v += 1

        owncol = c * BLK
        for i in range(KT):
            slot = i % 2
            vcur = sV.v
            def ldop(e, i=i, c=c, slot=slot, vcur=vcur, owncol=owncol):
                e.wait_ge(c_op[c].h, c_op[c].v)
                e.wait_ge(sV.h, vcur)
                d_stres.wait(e)
                d_op_ld[slot].inc(e.dma_start(
                    out=opart_sl[slot][:, 0:LC],
                    in_=op_out[c].rearrange("(kt p) t -> kt p t", p=128)[i]))
                d_op_ld[slot].inc(e.dma_start(
                    out=res_sl[:, slot * LC:slot * LC + BLK],
                    in_=kt_rows(xres_h, i)[:, owncol:owncol + BLK]))
            block.gpsimd(ldop)
            def resx(e, i=i, slot=slot):
                d_op_ld[slot].wait(e)
                e.scalar_tensor_tensor(res_sl[:, slot * LC:slot * LC + BLK],
                                       opart_sl[slot][:, 0:BLK], al_s[:, 0:1],
                                       res_sl[:, slot * LC:slot * LC + BLK], ALU.mult, ALU.add)
                e.scalar_tensor_tensor(res_c[:, i * BLK:(i + 1) * BLK],
                                       opart_sl[slot][:, BLK:2 * BLK], al_s[:, 1:2],
                                       res_sl[:, slot * LC:slot * LC + BLK], ALU.mult, ALU.add).then_inc(sV.h, 1)
            block.vector(resx); sV.v += 1
            vv = sV.v
            def stres(e, i=i, vv=vv, owncol=owncol):
                e.wait_ge(sV.h, vv)
                d_stres.inc(e.dma_start(out=kt_rows(xres_h, i)[:, owncol:owncol + BLK],
                                        in_=res_c[:, i * BLK:(i + 1) * BLK]))
            block.gpsimd(stres)
        v_res = sV.v
        layernorm(lambda kt: res_c[:, kt * BLK:(kt + 1) * BLK],
                  lambda kt: xh_f[:, kt * LC + (c % 2) * BLK:kt * LC + (c % 2) * BLK + BLK],
                  BLK, [lambda e: e.wait_ge(sV.h, v_res)])

        if c % 2 == 1 and stage == "full":
            g = c // 2
            ffn(LC, g * LC, True)
        if stage == "dbg1" and c == 1 and "tap_op" in tap_out:
            def tapop(e):
                e.wait_ge(c_op[1].h, c_op[1].v)
                sO.inc(e.dma_start(out=tap_out["tap_op"], in_=op_out[1]))
            block.gpsimd(tapop)
        barrier()
        if stage == "dbg3" and c == 0:
            def snap0(e):
                sO.inc(e.dma_start(out=state_snap[0], in_=state[:]))
            block.gpsimd(snap0)
            barrier()
        if stage == "dbg3" and c == 1:
            tap("tap_snap", state_snap.rearrange("a p n -> (a p) n"))
            tap("tap_xafter", xres_h)
            return finish()
        if stage == "dbg1" and c == 1:
            tap("tap_xafter", xres_h)
            return finish()
        if stage == "mamba1" and c == 0:
            tap("tap_xafter", xres_h)
            return finish()
    tap("tap_xafter", xres_h)
    tap("tap_xt", xt_hbm.rearrange("d p l -> (d p) l"))
    tap("tap_ts", ts_out[1])
    tap("tap_op", op_out[1])
    tap("tap_ag", ag_out[0].rearrange("s d t -> (s d) t"))
    if stage == "mamba":
        return finish()

    return finish()


def prep_inputs(inputs):
    x = np.ascontiguousarray(np.asarray(inputs["L0_attn_hidden_states"], np.float32))
    w_in = np.asarray(inputs["L0_attn_in_proj_w"], np.float32)
    conv_w = np.asarray(inputs["L0_attn_conv_w"], np.float32)
    conv_b = np.asarray(inputs["L0_attn_conv_b"], np.float32)
    x_proj = np.asarray(inputs["L0_attn_x_proj_w"], np.float32)
    dt_w = np.asarray(inputs["L0_attn_dt_proj_w"], np.float32)
    dt_b = np.asarray(inputs["L0_attn_dt_proj_b"], np.float32)
    A = np.asarray(inputs["L0_attn_A"], np.float32)
    Dv = np.asarray(inputs["L0_attn_D"], np.float32)
    w_out = np.asarray(inputs["L0_attn_out_proj_w"], np.float32)
    wg = np.asarray(inputs["L0_ffn_w_gate"], np.float32)
    wu = np.asarray(inputs["L0_ffn_w_up"], np.float32)
    wd = np.asarray(inputs["L0_ffn_w_down"], np.float32)

    def pmaj(W, nt):
        K = W.shape[1]
        return np.ascontiguousarray(
            W.reshape(nt, 128, K // 128, 128).transpose(3, 0, 2, 1).reshape(128, nt * K))
    wg_s = pmaj(wg, HT)
    wu_s = pmaj(wu, HT)
    wd_s = np.ascontiguousarray(
        wd.reshape(KT, 128, NE, HPE, 128).transpose(4, 2, 0, 3, 1).reshape(128, NE * KT * HPE * 128))

    in_maps = []
    for core in range(8):
        p = core // 2
        q = core & 1
        blocks = [q, q + 2, q + 4, q + 6]
        xb = x[p].reshape(8, BLK, DM)
        x_ownT = np.ascontiguousarray(
            np.concatenate([xb[b] for b in blocks], axis=0).T)
        xrows = slice(q * DH, (q + 1) * DH)
        grows = slice(4096 + q * DH, 4096 + (q + 1) * DH)
        w_inx = pmaj(w_in[xrows], DTl)
        w_ing = pmaj(w_in[grows], DTl)
        w_outs = pmaj(w_out[:, q * DH:(q + 1) * DH], KT)
        w_xp = np.ascontiguousarray(
            x_proj[:, q * DH:(q + 1) * DH].T.reshape(KT, 128, 160).transpose(1, 0, 2).reshape(128, KT * 160))
        w_dtp = np.ascontiguousarray(dt_w[q * DH:(q + 1) * DH, :].T)
        Ah = A[q * DH:(q + 1) * DH]
        A_sb = np.ascontiguousarray(Ah.reshape(DTl, 128, S).transpose(1, 0, 2).reshape(128, DTl * S))
        cwh = conv_w[q * DH:(q + 1) * DH]
        cw_sb = np.ascontiguousarray(cwh.reshape(DTl, 128, 4).transpose(1, 0, 2).reshape(128, DTl * 4))
        cb_sb = np.ascontiguousarray(conv_b[q * DH:(q + 1) * DH].reshape(DTl, 128).T)
        D_sb = np.ascontiguousarray(Dv[q * DH:(q + 1) * DH].reshape(DTl, 128).T)
        dtb_sb = np.ascontiguousarray(dt_b[q * DH:(q + 1) * DH].reshape(DTl, 128).T)
        alpha = np.zeros((128, 2), np.float32)
        alpha[:, 0] = 1.0 if q == 0 else 0.0
        alpha[:, 1] = 0.0 if q == 0 else 1.0
        in_maps.append({
            "x_own": x_ownT, "w_inx": w_inx, "w_ing": w_ing, "w_out": w_outs,
            "w_xp": w_xp, "w_dtp": w_dtp, "w_g": wg_s, "w_u": wu_s, "w_d": wd_s,
            "A_sb": A_sb, "cw_sb": cw_sb, "cb_sb": cb_sb, "D_sb": D_sb,
            "dtb_sb": dtb_sb, "alpha": alpha,
        })
    return in_maps


def own_tokens_to_full(results, key="out"):
    full = np.zeros((4, 2048, DM), np.float32)
    for core in range(8):
        p = core // 2
        q = core & 1
        blocks = [q, q + 2, q + 4, q + 6]
        o = np.asarray(results[core][key]).T.reshape(4, BLK, DM)
        for j, b in enumerate(blocks):
            full[p, b * BLK:(b + 1) * BLK, :] = o[j]
    return full


_CACHE = {}


def kernel(**inputs):
    import concourse.bass_utils as _bu
    if "nc" not in _CACHE:
        nc, ctx, _ = build(stage="full", taps=())
        _CACHE["nc"] = nc
        _CACHE["ctx"] = ctx
    nc = _CACHE["nc"]
    in_maps = prep_inputs(inputs)
    res = _bu.run_bass_kernel_spmd(nc, in_maps, core_ids=list(range(8)))
    full = own_tokens_to_full(res.results, "out")
    return np.ascontiguousarray(full.astype(np.float32))


# version 09 (final)
# speedup vs baseline: 1.2464x; 1.0077x over previous
import contextlib
import numpy as np
import concourse.bass as bass
import concourse.mybir as mybir

F32 = mybir.dt.float32
BF16 = mybir.dt.bfloat16
ALU = mybir.AluOpType
AF = mybir.ActivationFunctionType

DM = 2048
DH = 2048
TOWN = 1024
L = 2048
S = 16
NCH = 4
LC = 512
BLK = 256
KT = 16
DTl = 16
HT = 64
NE = 8
HPE = 8
EPS = 1e-5
PAIRS = [[0, 1], [2, 3], [4, 5], [6, 7]]


def build(stage="full", taps=()):
    nc = bass.Bass("TRN2", target_bir_lowering=False, debug=False, detect_race_conditions=False)
    ctx = contextlib.ExitStack()

    dr = {}
    def din(name, shape):
        dr[name] = nc.dram_tensor(name, list(shape), F32, kind="ExternalInput").ap()
    din("x_own", (DM, TOWN))
    din("w_inx", (128, DTl * DM))
    din("w_ing", (128, DTl * DM))
    din("w_out", (128, KT * DH))
    din("w_xp", (128, KT * 160))
    din("w_dtp", (128, DH))
    din("w_g", (128, HT * DM))
    din("w_u", (128, HT * DM))
    din("w_d", (128, NE * KT * HPE * 128))
    din("A_sb", (128, DTl * S))
    din("cw_sb", (128, DTl * 4))
    din("cb_sb", (128, DTl))
    din("D_sb", (128, DTl))
    din("dtb_sb", (128, DTl))
    din("alpha", (128, 2))
    out = nc.dram_tensor("out", [DM, TOWN], F32, kind="ExternalOutput").ap()
    tap_out = {}
    for tname, tshape in taps:
        tap_out[tname] = nc.dram_tensor(tname, list(tshape), F32, kind="ExternalOutput").ap()

    def scr(name, shape, dt=F32):
        return nc.dram_tensor(name, list(shape), dt, kind="Internal").ap()
    gu_bf = scr("gu_bf", (128, 2 * HT * DM), BF16)
    d_bf = scr("d_bf", (128, NE * KT * HPE * 128), BF16)
    inx_bf = scr("inx_bf", (128, DTl * DM), BF16)
    ing_bf = scr("ing_bf", (128, DTl * DM), BF16)
    outp_bf = scr("outp_bf", (128, KT * DH), BF16)
    xres_h = scr("xres_h", (DM, TOWN))
    xt_hbm = scr("xt_hbm", (DTl, 128, L))
    ag_in = [scr(f"ag_in{h}", (1, DM, LC), BF16) for h in range(2)]
    ag_out = [scr(f"ag_out{h}", (2, DM, LC), BF16) for h in range(2)]
    ts_in = [scr(f"ts_in{c}", (160, LC)) for c in range(NCH)]
    ts_out = [scr(f"ts_out{c}", (160, LC)) for c in range(NCH)]
    op_in = [scr(f"op_in{c}", (DM, LC)) for c in range(NCH)]
    op_out = [scr(f"op_out{c}", (DM, LC)) for c in range(NCH)]
    state_snap = scr("state_snap", (2, 128, 2 * DTl * S))

    sb = {}
    def sbuf(name, shape, dt=F32):
        sb[name] = ctx.enter_context(nc.sbuf_tensor(name, list(shape), dt))
        return sb[name]

    ones = sbuf("ones", (128, 2))
    epsc = sbuf("epsc", (128, 1))
    onesr = sbuf("onesr", (1, 128))
    rows = sbuf("rows", (1, 2 * LC))
    A_s = sbuf("A_s", (128, DTl * S))
    cw_s = sbuf("cw_s", (128, DTl * 4))
    cb_s = sbuf("cb_s", (128, DTl))
    D_s = sbuf("D_s", (128, DTl))
    dtb_s = sbuf("dtb_s", (128, DTl))
    al_s = sbuf("al_s", (128, 2))
    xpT = sbuf("xpT", (128, KT * 160), BF16)
    dtpT = sbuf("dtpT", (128, DH), BF16)
    state = sbuf("state", (128, 2 * DTl * S))
    carry = sbuf("carry", (128, DTl * 4))
    wsh = sbuf("wsh", (128, 8192), BF16)
    wstrip = [wsh[:, i * 2048:(i + 1) * 2048] for i in range(4)]
    wd_st = [wsh[:, i * 4096:(i + 1) * 4096] for i in range(2)]
    wgu_st = [sbuf(f"wgu_st{i}", (128, 8192), BF16) for i in range(2)]
    arena = sbuf("arena", (128, 12288))
    acc_f = arena[:, 0:8192]
    gated = arena[:, 8192:10240].bitcast(BF16)
    xh2 = arena[:, 0:4096].bitcast(BF16)
    xt_bf = arena[:, 4096:8192].bitcast(BF16)
    yf_t = arena[:, 8192:12288].bitcast(BF16)
    cbuf = [arena[:, j * 2048:(j + 1) * 2048].bitcast(BF16) for j in range(6)]
    xh_f = sbuf("xh_f", (128, KT * LC), BF16)
    sq_t = sbuf("sq_t", (128, LC))
    ln_tmp = sbuf("ln_tmp", (128, LC))
    mkt = sbuf("mkt", (128, 8))
    res_c = arena[:, 0:KT * BLK]
    res_sl = sbuf("res_sl", (128, 2 * LC))
    cacc = sq_t[:, 0:LC]
    xt_t = sbuf("xt_t", (128, LC))
    ts_bf = sbuf("ts_bf", (128, LC), BF16)
    bcflat = sbuf("bcflat", (1, 4 * LC))
    evac_t = sbuf("evac_t", (128, LC))
    dt_q = sbuf("dt_q", (128, 4 * LC))
    et_t = ln_tmp
    xt_sl = sbuf("xt_sl", (128, 2 * LC))
    u_q = sbuf("u_q", (128, 4 * LC))
    yac_q = sbuf("yac_q", (128, 4 * LC))
    SCW = LC + 8
    dA_t = sbuf("dA_t", (128, 6 * SCW))
    dBx4 = sbuf("dBx4", (128, 4 * (LC + 1)))
    so4 = sbuf("so4", (128, 4 * (LC + 1)))
    xcin = so4[:, 0:3 + LC + 1]
    opart_sl = [evac_t, xt_t]
    bc_sb = xt_t[0:32, 0:LC]

    p_mm = ctx.enter_context(nc.psum_tensor("p_mm", [128, 1024], F32))
    p_bcB = ctx.enter_context(nc.psum_tensor("p_bcB", [128, 1024], F32))
    p_bcC = ctx.enter_context(nc.psum_tensor("p_bcC", [128, 1024], F32))
    p_f = ctx.enter_context(nc.psum_tensor("p_f", [128, 1024], F32))

    block = ctx.enter_context(nc.Block())

    class Sem:
        def __init__(self, name):
            self.h = ctx.enter_context(nc.semaphore(name))
            self.v = 0

    class DSem(Sem):
        def inc(self, inst):
            inst.then_inc(self.h, 16)
            self.v += 16
        def wait(self, e):
            e.wait_ge(self.h, self.v)

    sT = Sem("sT"); sV = Sem("sV"); sS = Sem("sS")
    d_small = DSem("d_small")
    d_smallg = DSem("d_smallg")
    d_cast = [DSem(f"d_cast{i}") for i in range(6)]
    d_cstore = [DSem(f"d_cstore{i}") for i in range(6)]
    d_strip = [DSem(f"d_strip{i}") for i in range(4)]
    d_wg = [DSem(f"d_wg{i}") for i in range(2)]
    d_wu = [DSem(f"d_wu{i}") for i in range(2)]
    d_wd = [DSem(f"d_wd{i}") for i in range(2)]
    d_xh2 = DSem("d_xh2")
    d_xt_st = DSem("d_xt_st")
    d_xt_ld = [DSem(f"d_xt_ld{i}") for i in range(2)]
    d_ts_st = DSem("d_ts_st")
    d_tsbf = DSem("d_tsbf")
    d_bcf = DSem("d_bcf")
    d_op_st = DSem("d_op_st")
    d_op_ld = [DSem(f"d_op_ld{i}") for i in range(2)]
    d_res = [DSem(f"d_res{i}") for i in range(2)]
    d_xmid_st = DSem("d_xmid_st")
    d_stres = DSem("d_stres")
    d_ldx = DSem("d_ldx")
    d_ag_st = DSem("d_ag_st")
    sO = DSem("sO")
    c_ag = [Sem(f"c_ag{i}") for i in range(2)]
    c_ts = [Sem(f"c_ts{i}") for i in range(NCH)]
    c_op = [Sem(f"c_op{i}") for i in range(NCH)]
    ALL_DSEMS = ([d_small, d_smallg, d_xh2, d_xt_st, d_ts_st, d_tsbf, d_bcf, d_op_st,
                  d_xmid_st, d_stres, d_ldx, d_ag_st, sO]
                 + d_cast + d_cstore + d_strip + d_wg + d_wu + d_wd
                 + d_xt_ld + d_op_ld + d_res)
    ALL_CSEMS = c_ag + c_ts + c_op

    def dma4(e, sem, out_ap, in_ap):
        for qq in range(4):
            sem.inc(e.dma_start(out=out_ap[qq * 32:(qq + 1) * 32], in_=in_ap[qq * 32:(qq + 1) * 32]))

    def barrier():
        def t(e): e.drain().then_inc(sT.h, 1)
        block.tensor(t); sT.v += 1
        def v(e): e.drain().then_inc(sV.h, 1)
        block.vector(v); sV.v += 1
        def s(e): e.drain().then_inc(sS.h, 1)
        block.scalar(s); sS.v += 1
        vals = [(sT, sT.v), (sV, sV.v), (sS, sS.v)] + \
               [(d, d.v) for d in ALL_DSEMS] + [(cs, cs.v) for cs in ALL_CSEMS]
        def waiter(e):
            for sm, vl in vals:
                e.wait_ge(sm.h, vl)
        for emit in (block.tensor, block.vector, block.scalar, block.gpsimd, block.sync):
            emit(waiter)

    def tap(name, src_dram):
        if name not in tap_out:
            return
        def t(e):
            sO.inc(e.dma_start(out=tap_out[name], in_=src_dram))
        block.gpsimd(t)

    def finish():
        ov = sO.v
        def fin(e, ov=ov):
            e.wait_ge(sO.h, ov)
        block.gpsimd(fin)
        return nc, ctx, tap_out

    def kt_rows(dram2d, i):
        return dram2d.rearrange("(kt p) t -> kt p t", p=128)[i]

    def _init_sync(e):
        d_small.inc(e.dma_start(out=A_s[:], in_=dr["A_sb"]))
        d_small.inc(e.dma_start(out=cw_s[:], in_=dr["cw_sb"]))
        d_small.inc(e.dma_start(out=cb_s[:], in_=dr["cb_sb"]))
        d_small.inc(e.dma_start(out=D_s[:], in_=dr["D_sb"]))
        d_small.inc(e.dma_start(out=dtb_s[:], in_=dr["dtb_sb"]))
        d_small.inc(e.dma_start(out=al_s[:], in_=dr["alpha"]))
        d_small.inc(e.dma_start(out=xres_h[:, :], in_=dr["x_own"]))
    block.sync(_init_sync)

    def _init_gps(e):
        d_smallg.inc(e.dma_start(out=xpT[:], in_=dr["w_xp"]))
        d_smallg.inc(e.dma_start(out=dtpT[:], in_=dr["w_dtp"]))
    block.gpsimd(_init_gps)

    def _init_vec(e):
        e.memset(ones[:], 1.0)
        e.memset(epsc[:], EPS)
        e.memset(mkt[:], 1.0)
        e.memset(onesr[:], 1.0)
        e.memset(state[:], 0.0)
        e.memset(dA_t[:], 0.0)
        e.memset(carry[:], 0.0).then_inc(sV.h, 1)
    block.vector(_init_vec); sV.v += 1

    CW = 4096
    CAST_JOBS = []
    for w0 in range(0, HT * DM, CW):
        CAST_JOBS.append((dr["w_g"][:, w0:w0 + CW], gu_bf[:, 2 * w0:2 * w0 + CW]))
        CAST_JOBS.append((dr["w_u"][:, w0:w0 + CW], gu_bf[:, 2 * w0 + CW:2 * w0 + 2 * CW]))
    for cls_in, cls_out in (("w_d", d_bf), ("w_inx", inx_bf), ("w_ing", ing_bf), ("w_out", outp_bf)):
        total = dr[cls_in].shape[-1]
        assert total % CW == 0
        for w0 in range(0, total, CW):
            CAST_JOBS.append((dr[cls_in][:, w0:w0 + CW], cls_out[:, w0:w0 + CW]))

    for j, (srca, dsta) in enumerate(CAST_JOBS):
        slot = j % 6
        def gld(e, srca=srca, slot=slot):
            d_cstore[slot].wait(e)
            d_cast[slot].inc(e.dma_start(out=cbuf[slot][:, 0:CW], in_=srca))
        block.gpsimd(gld)
        def wst(e, dsta=dsta, slot=slot):
            d_cast[slot].wait(e)
            d_cstore[slot].inc(e.dma_start(out=dsta, in_=cbuf[slot][:, 0:CW]))
        block.sync(wst)

    barrier()

    def layernorm(src_fn, dst_fn, n, ready):
        T0 = sT.v
        S0ln = sS.v
        slots = [sq_t[:, 0:n], xt_t[:, 0:n], evac_t[:, 0:n]]
        for kt in range(KT + 1):
            if kt < KT:
                def sq(e, kt=kt, ready=ready, T0=T0):
                    if kt == 0:
                        for w in ready:
                            w(e)
                    if kt >= 3:
                        e.wait_ge(sT.h, T0 + kt - 2)
                    e.activation(slots[kt % 3], src_fn(kt), AF.Square,
                                 bias=0.0, scale=1.0).then_inc(sS.h, 1)
                block.scalar(sq); sS.v += 1
            else:
                def sq(e):
                    e.activation(mkt[:, 0:8], mkt[:, 0:8], AF.Square, bias=0.0, scale=1.0).then_inc(sS.h, 1)
                block.scalar(sq); sS.v += 1
            if kt >= 1:
                ktm = kt - 1
                sv = S0ln + kt + 1
                def mm(e, ktm=ktm, sv=sv, ready=ready):
                    e.wait_ge(sS.h, sv)
                    if ktm == 0:
                        for w in ready:
                            w(e)
                    e.matmul(p_mm[0:1, :n], ones[:, 0:1], src_fn(ktm), start=(ktm == 0), stop=(ktm == KT - 1))
                    e.matmul(p_mm[0:1, 512:512 + n], ones[:, 0:1], slots[ktm % 3],
                             start=(ktm == 0), stop=(ktm == KT - 1)).then_inc(sT.h, 1)
                block.tensor(mm); sT.v += 1
        tv = sT.v
        def rowmath(e, tv=tv, n=n):
            e.wait_ge(sT.h, tv)
            m = rows[:, 0:n]; v = rows[:, LC:LC + n]
            e.tensor_scalar_mul(m, p_mm[0:1, :n], 1.0 / DM)
            e.tensor_tensor(v, m, m, ALU.mult)
            e.scalar_tensor_tensor(v, p_mm[0:1, 512:512 + n], 1.0 / DM, v, ALU.mult, ALU.subtract)
            e.tensor_scalar_add(v, v, EPS).then_inc(sV.h, 1)
        block.vector(rowmath); sV.v += 1
        vv = sV.v
        def rowsqrt(e, vv=vv, n=n):
            e.wait_ge(sV.h, vv)
            e.activation(sq_t[0:1, 0:n], rows[:, LC:LC + n], AF.Sqrt, bias=0.0, scale=1.0).then_inc(sS.h, 1)
        block.scalar(rowsqrt); sS.v += 1
        def mkrr(e):
            e.activation(mkt[:, 0:8], mkt[:, 0:8], AF.Square, bias=0.0, scale=1.0).then_inc(sS.h, 1)
        block.scalar(mkrr); sS.v += 1
        sv2 = sS.v
        def rowrec(e, sv2=sv2, n=n):
            e.wait_ge(sS.h, sv2)
            v = rows[:, LC:LC + n]
            r = sq_t[0:1, 0:n]
            nm = xt_t[0:1, 0:n]
            e.reciprocal(r, r)
            e.tensor_tensor(nm, r, r, ALU.mult)
            e.tensor_tensor(nm, nm, v, ALU.mult)
            e.tensor_scalar(nm, nm, -0.5, 1.5, ALU.mult, ALU.add)
            e.tensor_tensor(r, r, nm, ALU.mult)
            e.scalar_tensor_tensor(nm, rows[:, 0:n], -1.0, r, ALU.mult, ALU.mult).then_inc(sV.h, 1)
        block.vector(rowrec); sV.v += 1
        vv2 = sV.v
        scur = sS.v
        def bcast(e, vv2=vv2, n=n, scur=scur):
            e.wait_ge(sV.h, vv2)
            e.wait_ge(sS.h, scur)
            e.matmul(p_bcB[:, :n], onesr[:], sq_t[0:1, 0:n], start=True, stop=True)
            e.matmul(p_bcC[:, :n], onesr[:], xt_t[0:1, 0:n], start=True, stop=True).then_inc(sT.h, 1)
        block.tensor(bcast); sT.v += 1
        tv2 = sT.v
        def apply_(e, tv2=tv2, n=n):
            e.wait_ge(sT.h, tv2)
            for kt in range(KT):
                e.tensor_tensor(ln_tmp[:, :n], src_fn(kt), p_bcB[:, :n], ALU.mult)
                i_ = e.tensor_tensor(dst_fn(kt), ln_tmp[:, :n], p_bcC[:, :n], ALU.add)
                if kt == KT - 1:
                    i_.then_inc(sV.h, 1)
        block.vector(apply_); sV.v += 1

    def ffn(n, res_cols, is_ffn2):
        V_F0 = sV.v
        S_F0 = sS.v
        sil_cnt = {}
        gm_cnt = {}
        ac_end_prev = V_F0
        def z(e):
            d_stres.wait(e)
            e.memset(acc_f[:, : KT * n], 0.0).then_inc(sV.h, 1)
        block.vector(z); sV.v += 1
        gu_cnt = {}
        dn_grp_cnt = {}
        for e8 in range(NE):
            for hh in range(HPE):
                ht = e8 * HPE + hh
                slot = (ht // 2) % 2
                sub = (ht % 2) * KT * 128
                pair = [p_f, p_mm][ht % 2]
                sgsl = [ln_tmp, evac_t][ht % 2]
                if ht % 2 == 0:
                    tw = gu_cnt.get(ht - 3, sT.v)
                    def ld(e, ht=ht, slot=slot, tw=tw):
                        e.wait_ge(sT.h, tw)
                        d_wg[slot].inc(e.dma_start(out=wgu_st[slot][:],
                                                   in_=gu_bf[:, ht * 4096:(ht + 2) * 4096]))
                    block.gpsimd(ld)
                if hh == 0 and e8 > 0:
                    vv_need = ac_end_prev
                else:
                    vv_need = gm_cnt[ht - 2] if ht >= 2 else V_F0
                sv_need = sil_cnt[ht - 2] if ht >= 2 else S_F0
                def gu(e, slot=slot, sub=sub, pair=pair, vv_need=vv_need, sv_need=sv_need, n=n):
                    d_wg[slot].wait(e)
                    e.wait_ge(sV.h, vv_need)
                    e.wait_ge(sS.h, sv_need)
                    for k in range(KT):
                        e.matmul(pair[:, 0:n], wgu_st[slot][:, sub + k * 128:sub + (k + 1) * 128],
                                 xh_f[:, k * LC:k * LC + n], start=(k == 0), stop=(k == KT - 1))
                    for k in range(KT):
                        i_ = e.matmul(pair[:, 512:512 + n], wgu_st[slot][:, 4096 + sub + k * 128:4096 + sub + (k + 1) * 128],
                                      xh_f[:, k * LC:k * LC + n], start=(k == 0), stop=(k == KT - 1))
                        if k == KT - 1:
                            i_.then_inc(sT.h, 1)
                block.tensor(gu); sT.v += 1
                gu_cnt[ht] = sT.v
                tv = sT.v
                vprot = gm_cnt[ht - 2] if ht >= 2 else V_F0
                def sil(e, tv=tv, n=n, vprot=vprot, pair=pair, sgsl=sgsl):
                    e.wait_ge(sT.h, tv)
                    e.wait_ge(sV.h, vprot)
                    e.activation(sgsl[:, :n], pair[:, 0:n], AF.Silu, bias=0.0, scale=1.0).then_inc(sS.h, 1)
                block.scalar(sil); sS.v += 1
                sil_cnt[ht] = sS.v
                sv = sil_cnt[ht] + 1
                def gm(e, hh=hh, sv=sv, n=n, pair=pair, sgsl=sgsl):
                    e.wait_ge(sS.h, sv)
                    e.tensor_tensor(gated[:, hh * n:(hh + 1) * n], sgsl[:, :n], pair[:, 512:512 + n], ALU.mult).then_inc(sV.h, 1)
                block.vector(gm); sV.v += 1
                gm_cnt[ht] = sV.v
            def mkrf(e):
                e.activation(mkt[:, 0:8], mkt[:, 0:8], AF.Square, bias=0.0, scale=1.0).then_inc(sS.h, 1)
            block.scalar(mkrf); sS.v += 1
            for dmt in range(KT):
                slot = (dmt // 4) % 2
                sub = (dmt % 4) * HPE * 128
                if dmt % 4 == 0:
                    grp = e8 * 4 + dmt // 4
                    tw = dn_grp_cnt.get(grp - 2, sT.v)
                    def ldd(e, e8=e8, dmt=dmt, slot=slot, tw=tw):
                        e.wait_ge(sT.h, tw)
                        base = (e8 * KT + dmt) * HPE * 128
                        d_wd[slot].inc(e.dma_start(out=wd_st[slot][:],
                                                   in_=d_bf[:, base:base + 4 * HPE * 128]))
                    block.gpsimd(ldd)
                vv_need = sV.v
                sv_need = sS.v
                def dn(e, slot=slot, sub=sub, vv_need=vv_need, sv_need=sv_need, n=n):
                    d_wd[slot].wait(e)
                    e.wait_ge(sV.h, vv_need)
                    e.wait_ge(sS.h, sv_need)
                    for h in range(HPE):
                        i_ = e.matmul(p_f[:, 0:n], wd_st[slot][:, sub + h * 128:sub + (h + 1) * 128],
                                      gated[:, h * n:(h + 1) * n], start=(h == 0), stop=(h == HPE - 1))
                        if h == HPE - 1:
                            i_.then_inc(sT.h, 1)
                block.tensor(dn); sT.v += 1
                if dmt % 4 == 3:
                    dn_grp_cnt[e8 * 4 + dmt // 4] = sT.v
                tv = sT.v
                def ac(e, dmt=dmt, tv=tv, n=n):
                    e.wait_ge(sT.h, tv)
                    e.tensor_tensor(acc_f[:, dmt * n:(dmt + 1) * n], p_f[:, 0:n],
                                    acc_f[:, dmt * n:(dmt + 1) * n], ALU.add).then_inc(sV.h, 1)
                block.vector(ac); sV.v += 1
            ac_end_prev = sV.v
        for dmt in range(KT):
            slot = dmt % 2
            vcur = sV.v
            def ldr(e, dmt=dmt, slot=slot, res_cols=res_cols, n=n, vcur=vcur):
                e.wait_ge(sV.h, vcur)
                d_stres.wait(e)
                d_xmid_st.wait(e)
                d_res[slot].inc(e.dma_start(
                    out=res_sl[:, slot * LC:slot * LC + n],
                    in_=kt_rows(xres_h, dmt)[:, res_cols:res_cols + n]))
            block.gpsimd(ldr)
            def upd(e, dmt=dmt, slot=slot, n=n):
                d_res[slot].wait(e)
                e.scalar_tensor_tensor(res_sl[:, slot * LC:slot * LC + n],
                                       acc_f[:, dmt * n:(dmt + 1) * n], 0.5,
                                       res_sl[:, slot * LC:slot * LC + n], ALU.mult, ALU.add).then_inc(sV.h, 1)
            block.vector(upd); sV.v += 1
            vv = sV.v
            if is_ffn2:
                def st(e, dmt=dmt, slot=slot, vv=vv, res_cols=res_cols, n=n):
                    e.wait_ge(sV.h, vv)
                    sO.inc(e.dma_start(out=kt_rows(out, dmt)[:, res_cols:res_cols + n],
                                       in_=res_sl[:, slot * LC:slot * LC + n]))
                block.gpsimd(st)
            else:
                def st(e, dmt=dmt, slot=slot, vv=vv, res_cols=res_cols, n=n):
                    e.wait_ge(sV.h, vv)
                    d_xmid_st.inc(e.dma_start(out=kt_rows(xres_h, dmt)[:, res_cols:res_cols + n],
                                              in_=res_sl[:, slot * LC:slot * LC + n]))
                block.gpsimd(st)

    for h in range(2):
        base = h * LC
        def ldx(e, base=base):
            d_xmid_st.wait(e)
            d_ldx.inc(e.dma_start(
                out=acc_f[:, :KT * LC].rearrange("p (kt t) -> p kt t", t=LC),
                in_=xres_h.rearrange("(kt p) t -> p kt t", p=128)[:, :, base:base + LC]))
        block.gpsimd(ldx)
        layernorm(lambda kt: acc_f[:, kt * LC:(kt + 1) * LC],
                  lambda kt: xh_f[:, kt * LC:(kt + 1) * LC], LC,
                  [lambda e: d_ldx.wait(e)])
        ffn(LC, base, False)
        barrier()
    tap("tap_xmid", xres_h)
    if stage == "ffn1":
        return finish()

    for h in range(2):
        base = h * LC
        def ldx2(e, base=base):
            d_xmid_st.wait(e)
            d_ldx.inc(e.dma_start(
                out=acc_f[:, :KT * LC].rearrange("p (kt t) -> p kt t", t=LC),
                in_=xres_h.rearrange("(kt p) t -> p kt t", p=128)[:, :, base:base + LC]))
        block.gpsimd(ldx2)
        layernorm(lambda kt: acc_f[:, kt * LC:(kt + 1) * LC],
                  lambda kt: xh_f[:, kt * LC:(kt + 1) * LC], LC,
                  [lambda e: d_ldx.wait(e)])
        vv = sV.v
        def stag(e, h=h, vv=vv):
            e.wait_ge(sV.h, vv)
            d_ag_st.inc(e.dma_start(out=ag_in[h][0].rearrange("(kt p) t -> p kt t", p=128),
                                    in_=xh_f[:].rearrange("p (kt t) -> p kt t", t=LC)))
        block.gpsimd(stag)
        def agc(e, h=h):
            d_ag_st.wait(e)
            e.collective_compute("AllGather", ALU.bypass, replica_groups=PAIRS,
                                 ins=[ag_in[h]], outs=[ag_out[h]]).then_inc(c_ag[h].h, 1)
        block.gpsimd(agc); c_ag[h].v += 1
        barrier()

    for c in range(NCH):
        hf = c // 2
        coff = (c % 2) * BLK
        tcur0 = sT.v
        def ldxh2(e, hf=hf, coff=coff, tcur0=tcur0):
            e.wait_ge(c_ag[hf].h, c_ag[hf].v)
            e.wait_ge(sT.h, tcur0)
            for slot in range(2):
                d_xh2.inc(e.dma_start(
                    out=xh2.rearrange("p (kt t) -> p kt t", t=LC)[:, :, slot * BLK:(slot + 1) * BLK],
                    in_=ag_out[hf][slot].rearrange("(kt p) t -> p kt t", p=128)[:, :, coff:coff + BLK]))
        block.gpsimd(ldxh2)

        S_A0 = sS.v
        T_A0 = sT.v
        for d in range(DTl):
            slot = d % 2
            wsl = d % 4
            tw = T_A0 + d - 3 if d >= 4 else sT.v
            def lds(e, d=d, wsl=wsl, tw=tw):
                e.wait_ge(sT.h, tw)
                d_strip[wsl].inc(e.dma_start(out=wstrip[wsl][:], in_=inx_bf[:, d * DM:(d + 1) * DM]))
            block.sync(lds)
            scur = sS.v
            def mmx(e, slot=slot, wsl=wsl, scur=scur):
                d_strip[wsl].wait(e)
                d_xh2.wait(e)
                e.wait_ge(sS.h, scur)
                for k in range(KT):
                    i_ = e.matmul(p_mm[:, slot * 512:slot * 512 + LC], wstrip[wsl][:, k * 128:(k + 1) * 128],
                                  xh2[:, k * LC:(k + 1) * LC], start=(k == 0), stop=(k == KT - 1))
                    if k == KT - 1:
                        i_.then_inc(sT.h, 1)
            block.tensor(mmx); sT.v += 1
            assert sT.v == T_A0 + d + 1
            tv = sT.v
            vcur = sV.v
            def evx(e, slot=slot, tv=tv, vcur=vcur):
                e.wait_ge(sT.h, tv)
                e.wait_ge(sV.h, vcur)
                e.copy(xcin[:, 3:3 + LC], p_mm[:, slot * 512:slot * 512 + LC]).then_inc(sS.h, 1)
            block.scalar(evx); sS.v += 1
            def mkre(e):
                e.activation(mkt[:, 0:8], mkt[:, 0:8], AF.Square, bias=0.0, scale=1.0).then_inc(sS.h, 1)
            block.scalar(mkre); sS.v += 1
            sv = sS.v
            def conv(e, d=d, sv=sv):
                e.wait_ge(sS.h, sv)
                e.tensor_copy(xcin[:, 0:3], carry[:, d * 4:d * 4 + 3])
                e.tensor_copy(carry[:, d * 4:d * 4 + 3], xcin[:, LC:LC + 3])
                e.tensor_scalar_mul(cacc, xcin[:, 0:LC], cw_s[:, d * 4:d * 4 + 1])
                for k in range(1, 4):
                    i_ = e.scalar_tensor_tensor(cacc, xcin[:, k:k + LC], cw_s[:, d * 4 + k:d * 4 + k + 1],
                                                cacc, ALU.mult, ALU.add)
                    if k == 3:
                        i_.then_inc(sV.h, 1)
            block.vector(conv); sV.v += 1
            vv = sV.v
            def silu_x(e, d=d, vv=vv):
                e.wait_ge(sV.h, vv)
                d_xt_st.wait(e)
                e.activation(xt_t[:], cacc, AF.Silu, bias=cb_s[:, d:d + 1], scale=1.0)
                e.copy(xt_bf[:, d * LC:(d + 1) * LC], xt_t[:]).then_inc(sS.h, 1)
            block.scalar(silu_x); sS.v += 1
            sv2 = S_A0 + (3 * d + 4 if d < DTl - 1 else 3 * DTl + 1)
            def stxt(e, d=d, c=c, sv2=sv2):
                e.wait_ge(sS.h, sv2)
                d_xt_st.inc(e.dma_start(out=xt_hbm[d][:, c * LC:(c + 1) * LC], in_=xt_t[:]))
            block.gpsimd(stxt)
        def mkra(e):
            e.activation(mkt[:, 0:8], mkt[:, 0:8], AF.Square, bias=0.0, scale=1.0).then_inc(sS.h, 1)
        block.scalar(mkra); sS.v += 1
        assert sS.v == S_A0 + 3 * DTl + 1
        s_xtbf = sS.v

        vcur = sV.v
        def xproj(e, s_xtbf=s_xtbf, vcur=vcur):
            e.wait_ge(sS.h, s_xtbf)
            e.wait_ge(sV.h, vcur)
            for k in range(DTl):
                e.matmul(p_mm[:, 0:LC], xpT[:, k * 160:k * 160 + 128],
                         xt_bf[:, k * LC:(k + 1) * LC], start=(k == 0), stop=(k == DTl - 1))
            for k in range(DTl):
                i_ = e.matmul(p_bcB[0:32, 0:LC], xpT[:, k * 160 + 128:(k + 1) * 160],
                              xt_bf[:, k * LC:(k + 1) * LC], start=(k == 0), stop=(k == DTl - 1))
                if k == DTl - 1:
                    i_.then_inc(sT.h, 1)
        block.tensor(xproj); sT.v += 1
        tv = sT.v
        def evts(e, tv=tv):
            e.wait_ge(sT.h, tv)
            d_xt_st.wait(e)
            e.copy(evac_t[:], p_mm[:, 0:LC])
            e.copy(bc_sb[:], p_bcB[0:32, 0:LC]).then_inc(sS.h, 1)
        block.scalar(evts); sS.v += 1
        def mkrt(e):
            e.activation(mkt[:, 0:8], mkt[:, 0:8], AF.Square, bias=0.0, scale=1.0).then_inc(sS.h, 1)
        block.scalar(mkrt); sS.v += 1
        sv = sS.v
        def sts(e, c=c, sv=sv):
            e.wait_ge(sS.h, sv)
            d_ts_st.inc(e.dma_start(out=ts_in[c][0:128, :], in_=evac_t[:]))
            d_ts_st.inc(e.dma_start(out=ts_in[c][128:160, :], in_=bc_sb[:]))
        block.gpsimd(sts)
        def arc(e, c=c):
            d_ts_st.wait(e)
            e.collective_compute("AllReduce", ALU.add, replica_groups=PAIRS,
                                 ins=[ts_in[c]], outs=[ts_out[c]]).then_inc(c_ts[c].h, 1)
        block.gpsimd(arc); c_ts[c].v += 1
        def ldts(e, c=c):
            e.wait_ge(c_ts[c].h, c_ts[c].v)
            d_tsbf.inc(e.dma_start(out=ts_bf[:], in_=ts_out[c][0:128, :]))
        block.gpsimd(ldts)

        if stage == "dbg3" and c == 1:
            def snap1(e):
                sO.inc(e.dma_start(out=state_snap[1], in_=state[:]))
            block.gpsimd(snap1)
        QD = 4
        v_prep_ok = sV.v
        gmm_cnt = {}
        for q in range(DTl // QD):
            S_prep0 = sS.v
            for dq in range(QD):
                d = q * QD + dq
                dsl = d % 2
                scur = sS.v
                def dtmm(e, d=d, scur=scur):
                    d_tsbf.wait(e)
                    e.wait_ge(sS.h, scur)
                    e.matmul(p_f[:, 512:512 + LC], dtpT[:, d * 128:(d + 1) * 128], ts_bf[:],
                             start=True, stop=True).then_inc(sT.h, 1)
                block.tensor(dtmm); sT.v += 1
                tv = sT.v
                def softp(e, dq=dq, d=d, tv=tv, vfix=v_prep_ok):
                    e.wait_ge(sT.h, tv)
                    e.wait_ge(sV.h, vfix)
                    e.activation(et_t[:, 0:LC], p_f[:, 512:512 + LC], AF.Exp, bias=dtb_s[:, d:d + 1], scale=1.0)
                    e.activation(dt_q[:, dq * LC:(dq + 1) * LC], et_t[:, 0:LC], AF.Ln, bias=1.0, scale=1.0).then_inc(sS.h, 1)
                block.scalar(softp); sS.v += 1
                assert sS.v == S_prep0 + dq + 1
                vcur = sV.v
                def ldxt(e, d=d, c=c, dsl=dsl, vcur=vcur):
                    e.wait_ge(sV.h, vcur)
                    d_xt_st.wait(e)
                    d_xt_ld[dsl].inc(e.dma_start(
                        out=xt_sl[:, dsl * LC:(dsl + 1) * LC],
                        in_=xt_hbm[d][:, c * LC:(c + 1) * LC]))
                block.gpsimd(ldxt)
                s_dt = S_prep0 + dq + 2
                def uprep(e, dq=dq, d=d, dsl=dsl, s_dt=s_dt):
                    e.wait_ge(sS.h, s_dt)
                    d_xt_ld[dsl].wait(e)
                    e.tensor_tensor(u_q[:, dq * LC:(dq + 1) * LC], dt_q[:, dq * LC:(dq + 1) * LC],
                                    xt_sl[:, dsl * LC:(dsl + 1) * LC], ALU.mult)
                    e.tensor_scalar_mul(yac_q[:, dq * LC:(dq + 1) * LC],
                                        xt_sl[:, dsl * LC:(dsl + 1) * LC], D_s[:, d:d + 1]).then_inc(sV.h, 1)
                block.vector(uprep); sV.v += 1
            def mkr2(e):
                e.activation(mkt[:, 0:8], mkt[:, 0:8], AF.Square, bias=0.0, scale=1.0).then_inc(sS.h, 1)
            block.scalar(mkr2); sS.v += 1
            assert sS.v == S_prep0 + QD + 1
            V0 = sV.v
            S0 = sS.v
            T0 = sT.v
            NREF = 8
            for r in range(NREF):
                def refill(e, c=c, r=r, T0=T0, tcur_r=sT.v):
                    e.wait_ge(c_ts[c].h, c_ts[c].v)
                    e.wait_ge(sT.h, T0 + 4 * r if r >= 1 else tcur_r)
                    d_bcf.inc(e.dma_start(
                        out=bcflat[:, 0:2 * LC],
                        in_=ts_out[c][128 + 2 * r:130 + 2 * r, :].rearrange("a b -> (a b)").unsqueeze(0)))
                    d_bcf.inc(e.dma_start(
                        out=bcflat[:, 2 * LC:4 * LC],
                        in_=ts_out[c][144 + 2 * r:146 + 2 * r, :].rearrange("a b -> (a b)").unsqueeze(0)))
                block.gpsimd(refill)
                for i2 in range(2):
                    s_ = 2 * r + i2
                    def bca(e, s_=s_, i2=i2, V0=V0):
                        d_bcf.wait(e)
                        if s_ >= 2:
                            e.wait_ge(sV.h, V0 + QD * (s_ - 1))
                        else:
                            e.wait_ge(sV.h, V0)
                        e.matmul(p_bcB[:, (s_ % 2) * 512:(s_ % 2) * 512 + LC], onesr[:],
                                 bcflat[0:1, i2 * LC:(i2 + 1) * LC], start=True, stop=True)
                        e.matmul(p_bcC[:, (s_ % 2) * 512:(s_ % 2) * 512 + LC], onesr[:],
                                 bcflat[0:1, 2 * LC + i2 * LC:2 * LC + (i2 + 1) * LC], start=True, stop=True).then_inc(sT.h, 2)
                    block.tensor(bca); sT.v += 2
            def dA_all(e, q=q, V0=V0):
                for s_ in range(S):
                    for dq in range(QD):
                        d = q * QD + dq
                        j = s_ * QD + dq
                        e.wait_ge(sV.h, V0 + j - 5 if j >= 6 else V0)
                        e.activation(dA_t[:, (j % 6) * SCW + 1:(j % 6) * SCW + 1 + LC],
                                     dt_q[:, dq * LC:(dq + 1) * LC], AF.Exp,
                                     bias=0.0, scale=A_s[:, d * S + s_:d * S + s_ + 1]).then_inc(sS.h, 1)
                e.activation(mkt[:, 0:8], mkt[:, 0:8], AF.Square, bias=0.0, scale=1.0).then_inc(sS.h, 1)
            block.scalar(dA_all); sS.v += S * QD + 1
            def scan_all(e, q=q, c=c, V0=V0, S0=S0, T0=T0):
                W1 = LC + 1
                rd_c = c % 2
                wr_c = (c + 1) % 2
                state_r = state.rearrange("p (c2 d s) -> p c2 d s", c2=2, s=S)
                dBx_r = dBx4.rearrange("p (q2 w) -> p q2 w", w=W1)
                so_r = so4.rearrange("p (q2 w) -> p q2 w", w=W1)
                u_r = u_q[:, 0:QD * LC].rearrange("p (q2 w) -> p q2 w", w=LC)
                yac_r = yac_q[:, 0:QD * LC].rearrange("p (q2 w) -> p q2 w", w=LC)
                for s_ in range(S):
                    bsl = (s_ % 2) * 512
                    Bb = p_bcB[:, bsl:bsl + LC].unsqueeze(1).broadcast_to((128, QD, LC))
                    Cb = p_bcC[:, bsl:bsl + LC].unsqueeze(1).broadcast_to((128, QD, LC))
                    e.wait_ge(sT.h, T0 + 2 * min(s_ + 2, S))
                    e.wait_ge(sS.h, S0 + min(QD * s_ + QD + 1, S * QD + 1))
                    e.tensor_copy(dBx_r[:, :, 0:1], state_r[:, rd_c, q * QD:(q + 1) * QD, s_:s_ + 1])
                    e.tensor_tensor(dBx_r[:, :, 1:W1], u_r, Bb, ALU.mult)
                    for dq in range(QD):
                        j = s_ * QD + dq
                        i_ = e.tensor_tensor_scan(so4[:, dq * W1:(dq + 1) * W1],
                                                  dA_t[:, (j % 6) * SCW:(j % 6) * SCW + 1 + LC],
                                                  dBx4[:, dq * W1:(dq + 1) * W1], 0.0, ALU.mult, ALU.add)
                        if dq < QD - 1:
                            i_.then_inc(sV.h, 1)
                    e.tensor_tensor(dBx_r[:, :, 1:W1], so_r[:, :, 1:W1], Cb, ALU.mult)
                    i_ = e.tensor_tensor(yac_r, dBx_r[:, :, 1:W1], yac_r, ALU.add)
                    i_.then_inc(sV.h, 1)
                    e.tensor_copy(state_r[:, wr_c, q * QD:(q + 1) * QD, s_:s_ + 1],
                                  so_r[:, :, LC:W1])
            block.vector(scan_all); sV.v += S * QD
            v_scan = sV.v
            G_q0 = sS.v
            yfm_cnt_q = []
            for dq in range(QD):
                d = q * QD + dq
                dsl = d % 2
                wsl = dq
                sgsl = [ln_tmp, evac_t][dq % 2]
                tw = gmm_cnt.get((q - 1, dq), sT.v)
                def ldg(e, d=d, wsl=wsl, tw=tw):
                    e.wait_ge(sT.h, tw)
                    d_strip[wsl].inc(e.dma_start(out=wstrip[wsl][:], in_=ing_bf[:, d * DM:(d + 1) * DM]))
                block.sync(ldg)
                scur2 = sS.v
                def gmm(e, dsl=dsl, wsl=wsl, scur2=scur2):
                    d_strip[wsl].wait(e)
                    d_xh2.wait(e)
                    e.wait_ge(sS.h, scur2)
                    for k in range(KT):
                        i_ = e.matmul(p_mm[:, (dsl % 2) * 512:(dsl % 2) * 512 + LC],
                                      wstrip[wsl][:, k * 128:(k + 1) * 128],
                                      xh2[:, k * LC:(k + 1) * LC], start=(k == 0), stop=(k == KT - 1))
                        if k == KT - 1:
                            i_.then_inc(sT.h, 1)
                block.tensor(gmm); sT.v += 1
                gmm_cnt[(q, dq)] = sT.v
                tv = sT.v
                vfix = yfm_cnt_q[dq - 2] if dq >= 2 else v_prep_ok
                def sgx(e, dsl=dsl, tv=tv, vfix=vfix, sgsl=sgsl):
                    e.wait_ge(sT.h, tv)
                    e.wait_ge(sV.h, vfix)
                    e.activation(sgsl[:, :LC], p_mm[:, (dsl % 2) * 512:(dsl % 2) * 512 + LC],
                                 AF.Silu, bias=0.0, scale=1.0).then_inc(sS.h, 1)
                block.scalar(sgx); sS.v += 1
                assert sS.v == G_q0 + dq + 1
                sv = G_q0 + dq + 2
                def yfm(e, dq=dq, d=d, sv=sv, v_scan=v_scan, sgsl=sgsl):
                    e.wait_ge(sS.h, sv)
                    e.wait_ge(sV.h, v_scan)
                    e.tensor_tensor(yf_t[:, d * LC:(d + 1) * LC], yac_q[:, dq * LC:(dq + 1) * LC],
                                    sgsl[:, :LC], ALU.mult).then_inc(sV.h, 1)
                block.vector(yfm); sV.v += 1
                yfm_cnt_q.append(sV.v)
            def mkrg(e):
                e.activation(mkt[:, 0:8], mkt[:, 0:8], AF.Square, bias=0.0, scale=1.0).then_inc(sS.h, 1)
            block.scalar(mkrg); sS.v += 1
            assert sS.v == G_q0 + QD + 1
            v_prep_ok = sV.v
        v_yf = sV.v
        if stage == "dbg1" and c == 1 and "tap_yf" in tap_out:
            def tapyf(e, v_yf=v_yf):
                e.wait_ge(sV.h, v_yf)
                sO.inc(e.dma_start(out=tap_out["tap_yf"], in_=yf_t[:]))
                sO.inc(e.dma_start(out=tap_out["tap_ts"], in_=ts_out[1]))
            block.gpsimd(tapyf)

        S_C0 = sS.v
        op_base = d_op_st.v
        T_C0 = sT.v
        for i in range(KT):
            slot = i % 2
            wsl = i % 4
            tw = T_C0 + i - 3 if i >= 4 else sT.v
            def ldo(e, i=i, wsl=wsl, tw=tw):
                e.wait_ge(sT.h, tw)
                d_strip[wsl].inc(e.dma_start(out=wstrip[wsl][:], in_=outp_bf[:, i * DH:(i + 1) * DH]))
            block.sync(ldo)
            scur = sS.v
            def omm(e, slot=slot, wsl=wsl, v_yf=v_yf, scur=scur):
                d_strip[wsl].wait(e)
                e.wait_ge(sV.h, v_yf)
                e.wait_ge(sS.h, scur)
                for k in range(DTl):
                    i_ = e.matmul(p_mm[:, slot * 512:slot * 512 + LC], wstrip[wsl][:, k * 128:(k + 1) * 128],
                                  yf_t[:, k * LC:(k + 1) * LC], start=(k == 0), stop=(k == DTl - 1))
                    if k == DTl - 1:
                        i_.then_inc(sT.h, 1)
            block.tensor(omm); sT.v += 1
            assert sT.v == T_C0 + i + 1
            tv = sT.v
            opw = op_base + 16 * (i - 1) if i >= 2 else op_base
            def evo(e, slot=slot, tv=tv, opw=opw):
                e.wait_ge(sT.h, tv)
                e.wait_ge(d_op_st.h, opw)
                e.copy(opart_sl[slot][:, 0:LC], p_mm[:, slot * 512:slot * 512 + LC]).then_inc(sS.h, 1)
            block.scalar(evo); sS.v += 1
            assert sS.v == S_C0 + i + 1
            sv = S_C0 + i + 2
            def sto(e, i=i, c=c, slot=slot, sv=sv):
                e.wait_ge(sS.h, sv)
                d_op_st.inc(e.dma_start(out=op_in[c].rearrange("(kt p) t -> kt p t", p=128)[i],
                                        in_=opart_sl[slot][:, 0:LC]))
            block.gpsimd(sto)
        def mkro(e):
            e.activation(mkt[:, 0:8], mkt[:, 0:8], AF.Square, bias=0.0, scale=1.0).then_inc(sS.h, 1)
        block.scalar(mkro); sS.v += 1
        assert sS.v == S_C0 + KT + 1
        def aro(e, c=c):
            d_op_st.wait(e)
            e.collective_compute("AllReduce", ALU.add, replica_groups=PAIRS,
                                 ins=[op_in[c]], outs=[op_out[c]]).then_inc(c_op[c].h, 1)
        block.gpsimd(aro); c_op[c].v += 1

        owncol = c * BLK
        for i in range(KT):
            slot = i % 2
            vcur = sV.v
            def ldop(e, i=i, c=c, slot=slot, vcur=vcur, owncol=owncol):
                e.wait_ge(c_op[c].h, c_op[c].v)
                e.wait_ge(sV.h, vcur)
                d_stres.wait(e)
                d_op_ld[slot].inc(e.dma_start(
                    out=opart_sl[slot][:, 0:LC],
                    in_=op_out[c].rearrange("(kt p) t -> kt p t", p=128)[i]))
                d_op_ld[slot].inc(e.dma_start(
                    out=res_sl[:, slot * LC:slot * LC + BLK],
                    in_=kt_rows(xres_h, i)[:, owncol:owncol + BLK]))
            block.gpsimd(ldop)
            def resx(e, i=i, slot=slot):
                d_op_ld[slot].wait(e)
                e.scalar_tensor_tensor(res_sl[:, slot * LC:slot * LC + BLK],
                                       opart_sl[slot][:, 0:BLK], al_s[:, 0:1],
                                       res_sl[:, slot * LC:slot * LC + BLK], ALU.mult, ALU.add)
                e.scalar_tensor_tensor(res_c[:, i * BLK:(i + 1) * BLK],
                                       opart_sl[slot][:, BLK:2 * BLK], al_s[:, 1:2],
                                       res_sl[:, slot * LC:slot * LC + BLK], ALU.mult, ALU.add).then_inc(sV.h, 1)
            block.vector(resx); sV.v += 1
            vv = sV.v
            def stres(e, i=i, vv=vv, owncol=owncol):
                e.wait_ge(sV.h, vv)
                d_stres.inc(e.dma_start(out=kt_rows(xres_h, i)[:, owncol:owncol + BLK],
                                        in_=res_c[:, i * BLK:(i + 1) * BLK]))
            block.gpsimd(stres)
        v_res = sV.v
        layernorm(lambda kt: res_c[:, kt * BLK:(kt + 1) * BLK],
                  lambda kt: xh_f[:, kt * LC + (c % 2) * BLK:kt * LC + (c % 2) * BLK + BLK],
                  BLK, [lambda e: e.wait_ge(sV.h, v_res)])

        if c % 2 == 1 and stage == "full":
            g = c // 2
            ffn(LC, g * LC, True)
        if stage == "dbg1" and c == 1 and "tap_op" in tap_out:
            def tapop(e):
                e.wait_ge(c_op[1].h, c_op[1].v)
                sO.inc(e.dma_start(out=tap_out["tap_op"], in_=op_out[1]))
            block.gpsimd(tapop)
        barrier()
        if stage == "dbg3" and c == 0:
            def snap0(e):
                sO.inc(e.dma_start(out=state_snap[0], in_=state[:]))
            block.gpsimd(snap0)
            barrier()
        if stage == "dbg3" and c == 1:
            tap("tap_snap", state_snap.rearrange("a p n -> (a p) n"))
            tap("tap_xafter", xres_h)
            return finish()
        if stage == "dbg1" and c == 1:
            tap("tap_xafter", xres_h)
            return finish()
        if stage == "mamba1" and c == 0:
            tap("tap_xafter", xres_h)
            return finish()
    tap("tap_xafter", xres_h)
    tap("tap_xt", xt_hbm.rearrange("d p l -> (d p) l"))
    tap("tap_ts", ts_out[1])
    tap("tap_op", op_out[1])
    tap("tap_ag", ag_out[0].rearrange("s d t -> (s d) t"))
    if stage == "mamba":
        return finish()

    return finish()


def prep_inputs(inputs):
    x = np.ascontiguousarray(np.asarray(inputs["L0_attn_hidden_states"], np.float32))
    w_in = np.asarray(inputs["L0_attn_in_proj_w"], np.float32)
    conv_w = np.asarray(inputs["L0_attn_conv_w"], np.float32)
    conv_b = np.asarray(inputs["L0_attn_conv_b"], np.float32)
    x_proj = np.asarray(inputs["L0_attn_x_proj_w"], np.float32)
    dt_w = np.asarray(inputs["L0_attn_dt_proj_w"], np.float32)
    dt_b = np.asarray(inputs["L0_attn_dt_proj_b"], np.float32)
    A = np.asarray(inputs["L0_attn_A"], np.float32)
    Dv = np.asarray(inputs["L0_attn_D"], np.float32)
    w_out = np.asarray(inputs["L0_attn_out_proj_w"], np.float32)
    wg = np.asarray(inputs["L0_ffn_w_gate"], np.float32)
    wu = np.asarray(inputs["L0_ffn_w_up"], np.float32)
    wd = np.asarray(inputs["L0_ffn_w_down"], np.float32)

    def pmaj(W, nt):
        K = W.shape[1]
        return np.ascontiguousarray(
            W.reshape(nt, 128, K // 128, 128).transpose(3, 0, 2, 1).reshape(128, nt * K))
    wg_s = pmaj(wg, HT)
    wu_s = pmaj(wu, HT)
    wd_s = np.ascontiguousarray(
        wd.reshape(KT, 128, NE, HPE, 128).transpose(4, 2, 0, 3, 1).reshape(128, NE * KT * HPE * 128))

    in_maps = []
    for core in range(8):
        p = core // 2
        q = core & 1
        blocks = [q, q + 2, q + 4, q + 6]
        xb = x[p].reshape(8, BLK, DM)
        x_ownT = np.ascontiguousarray(
            np.concatenate([xb[b] for b in blocks], axis=0).T)
        xrows = slice(q * DH, (q + 1) * DH)
        grows = slice(4096 + q * DH, 4096 + (q + 1) * DH)
        w_inx = pmaj(w_in[xrows], DTl)
        w_ing = pmaj(w_in[grows], DTl)
        w_outs = pmaj(w_out[:, q * DH:(q + 1) * DH], KT)
        w_xp = np.ascontiguousarray(
            x_proj[:, q * DH:(q + 1) * DH].T.reshape(KT, 128, 160).transpose(1, 0, 2).reshape(128, KT * 160))
        w_dtp = np.ascontiguousarray(dt_w[q * DH:(q + 1) * DH, :].T)
        Ah = A[q * DH:(q + 1) * DH]
        A_sb = np.ascontiguousarray(Ah.reshape(DTl, 128, S).transpose(1, 0, 2).reshape(128, DTl * S))
        cwh = conv_w[q * DH:(q + 1) * DH]
        cw_sb = np.ascontiguousarray(cwh.reshape(DTl, 128, 4).transpose(1, 0, 2).reshape(128, DTl * 4))
        cb_sb = np.ascontiguousarray(conv_b[q * DH:(q + 1) * DH].reshape(DTl, 128).T)
        D_sb = np.ascontiguousarray(Dv[q * DH:(q + 1) * DH].reshape(DTl, 128).T)
        dtb_sb = np.ascontiguousarray(dt_b[q * DH:(q + 1) * DH].reshape(DTl, 128).T)
        alpha = np.zeros((128, 2), np.float32)
        alpha[:, 0] = 1.0 if q == 0 else 0.0
        alpha[:, 1] = 0.0 if q == 0 else 1.0
        in_maps.append({
            "x_own": x_ownT, "w_inx": w_inx, "w_ing": w_ing, "w_out": w_outs,
            "w_xp": w_xp, "w_dtp": w_dtp, "w_g": wg_s, "w_u": wu_s, "w_d": wd_s,
            "A_sb": A_sb, "cw_sb": cw_sb, "cb_sb": cb_sb, "D_sb": D_sb,
            "dtb_sb": dtb_sb, "alpha": alpha,
        })
    return in_maps


def own_tokens_to_full(results, key="out"):
    full = np.zeros((4, 2048, DM), np.float32)
    for core in range(8):
        p = core // 2
        q = core & 1
        blocks = [q, q + 2, q + 4, q + 6]
        o = np.asarray(results[core][key]).T.reshape(4, BLK, DM)
        for j, b in enumerate(blocks):
            full[p, b * BLK:(b + 1) * BLK, :] = o[j]
    return full


_CACHE = {}


def kernel(**inputs):
    import concourse.bass_utils as _bu
    if "nc" not in _CACHE:
        nc, ctx, _ = build(stage="full", taps=())
        _CACHE["nc"] = nc
        _CACHE["ctx"] = ctx
    nc = _CACHE["nc"]
    in_maps = prep_inputs(inputs)
    res = _bu.run_bass_kernel_spmd(nc, in_maps, core_ids=list(range(8)))
    full = own_tokens_to_full(res.results, "out")
    return np.ascontiguousarray(full.astype(np.float32))
